# version 41
# baseline (speedup 1.0000x reference)
import numpy as np
import ml_dtypes
import contextlib
import concourse.bass as bass
import concourse.mybir as mybir
from concourse.bass_utils import run_bass_kernel_spmd

F32 = mybir.dt.float32
BF16 = mybir.dt.bfloat16
AF = mybir.ActivationFunctionType
ALU = mybir.AluOpType
AX = mybir.AxisListType

D = 1024
TOK = 2048
TS = 512
NB = TS // 128
NSL = TOK // TS
NR = 8
CB = D // 128

EPS_LN = 1e-5
EPS_F = 1.1920929e-07
EPS_Y = 1e-6
DEFER_W2 = True

BF = ml_dtypes.bfloat16


class Sem:
    def __init__(self, stack, nc, name):
        self.h = stack.enter_context(nc.semaphore(name))
        self.n = 0

    def inc(self, inst, k=1):
        inst.then_inc(self.h, k)
        self.n += k
        return inst

    def wait_all(self, eng):
        if self.n:
            eng.wait_ge(self.h, self.n)

    def wait_at(self, eng, val):
        if val:
            eng.wait_ge(self.h, val)


class Builder:
    def __init__(self, nc, block, stack):
        self.nc = nc
        self.block = block
        s = stack.enter_context

        self.d_x = nc.dram_tensor("x_own", [TOK, D], F32, kind="ExternalInput").ap()
        self.d_mask = nc.dram_tensor("mask", [128, NR], F32, kind="ExternalInput").ap()
        self.d_tin = nc.dram_tensor("tot_in", [128, 16], F32, kind="Internal").ap()
        self.d_tout = nc.dram_tensor("tot_out", [NR * 128, 16], F32, kind="Internal").ap()
        self.d_wqkvT = nc.dram_tensor("wqkvT", [D, 3 * D], F32, kind="ExternalInput").ap()
        self.d_wswT = nc.dram_tensor("wswT", [D, 2 * D], F32, kind="ExternalInput").ap()
        self.d_woutT = nc.dram_tensor("woutT", [D, D], F32, kind="ExternalInput").ap()
        self.d_gateT = nc.dram_tensor("gateT", [D, 8], F32, kind="ExternalInput").ap()
        self.d_cb = nc.dram_tensor("cbb", [128, 8], F32, kind="ExternalInput").ap()
        self.d_fc1T = nc.dram_tensor("fc1T", [D, 512], F32, kind="ExternalInput").ap()
        self.d_fc2T = nc.dram_tensor("fc2T", [512, D], F32, kind="ExternalInput").ap()
        self.d_w1T = nc.dram_tensor("w1T", [8, 512, 512], F32, kind="ExternalInput").ap()
        self.d_w2T = nc.dram_tensor("w2T", [8, 512, 512], F32, kind="ExternalInput").ap()
        self.d_upT = nc.dram_tensor("upT", [D, 4 * D], F32, kind="ExternalInput").ap()
        self.d_downT = nc.dram_tensor("downT", [4 * D, D], F32, kind="ExternalInput").ap()
        self.d_ident = nc.dram_tensor("ident", [128, 128], F32, kind="ExternalInput").ap()
        self.d_ones1 = nc.dram_tensor("ones1", [1, 128], BF16, kind="ExternalInput").ap()
        self.d_ones128 = nc.dram_tensor("ones128", [128, 1], BF16, kind="ExternalInput").ap()
        self.d_consts = nc.dram_tensor("consts", [128, 8], F32, kind="ExternalInput").ap()
        self.d_y = nc.dram_tensor("y", [TOK, D], F32, kind="ExternalOutput").ap()
        self.d_wr = nc.dram_tensor("wr_scratch", [8, TOK], BF16, kind="Internal").ap()
        self.d_wbf = nc.dram_tensor("wbf", [76, 128, 2048], BF16, kind="Internal").ap()

        self.HI = s(nc.sbuf_tensor("HI", [128, NSL * 4096], BF16))
        self.AR_A = s(nc.sbuf_tensor("AR_A", [128, 4096], F32))
        self.AR_B = s(nc.sbuf_tensor("AR_B", [128, 4096], F32))
        self.AR_C = s(nc.sbuf_tensor("AR_C", [128, 4096], F32))
        self.AR_XN = s(nc.sbuf_tensor("AR_XN", [128, 4096], F32))
        self.BSC = s(nc.sbuf_tensor("BSC", [128, 8192], BF16))
        self.B2 = s(nc.sbuf_tensor("B2", [128, 8192], BF16))
        self.WBuf = s(nc.sbuf_tensor("WBuf", [128, 20480], BF16))
        self.SCR2 = s(nc.sbuf_tensor("SCR2", [128, 512], F32))
        self.SCRH = s(nc.sbuf_tensor("SCRH", [128, 512], F32))
        self.SCR3 = s(nc.sbuf_tensor("SCR3", [128, 512], F32))
        self.SCR4 = s(nc.sbuf_tensor("SCR4", [128, 512], F32))
        self.W8H = s(nc.sbuf_tensor("W8H", [8, 512], BF16))
        self.G8 = self.SCR3[0:8, :]
        self.ST1 = self.SCR4[:, 64:96]
        self.MASK = self.SCR4[:, 96:104]
        self.WBC2 = s(nc.sbuf_tensor("WBC2", [128, 1024], F32))
        self.ROWS = s(nc.sbuf_tensor("ROWS", [1, 2048], F32))
        self.ROWSB = s(nc.sbuf_tensor("ROWSB", [1, 2048], BF16))
        self.R_ALL = s(nc.sbuf_tensor("R_ALL", [1, NSL * 512], F32))
        self.SOUT = s(nc.sbuf_tensor("SOUT", [128, NSL * 8], F32))
        self.WR16 = s(nc.sbuf_tensor("WR16", [1, 1024], BF16))
        self.GT = s(nc.sbuf_tensor("GT", [128, 64], F32))
        self.GH = s(nc.sbuf_tensor("GH", [128, 64], BF16))
        self.GL = s(nc.sbuf_tensor("GL", [128, 64], BF16))
        self.CBB = s(nc.sbuf_tensor("CBB", [128, 8], F32))
        self.TOTS = s(nc.sbuf_tensor("TOTS", [128, NR * 16], F32))
        self.IDENT = s(nc.sbuf_tensor("IDENT", [128, 128], F32))
        self.ONES1 = s(nc.sbuf_tensor("ONES1", [1, 128], BF16))
        self.ONES128 = s(nc.sbuf_tensor("ONES128", [128, 1], BF16))
        self.CONSTS = s(nc.sbuf_tensor("CONSTS", [128, 8], F32))
        self.CARRY = s(nc.sbuf_tensor("CARRY", [128, 16], F32))
        self.SLOT = s(nc.sbuf_tensor("SLOT", [128, 16], F32))
        self.OFFS = s(nc.sbuf_tensor("OFFS", [128, 16], F32))
        self.COLT = s(nc.sbuf_tensor("COLT", [128, 16], F32))
        self.R_s = s(nc.sbuf_tensor("R_s", [128, 32], F32))
        self.R_wte = s(nc.sbuf_tensor("R_wte", [128, 32], F32))
        self.R_t = [s(nc.sbuf_tensor(f"R_t{i}", [128, 32], F32)) for i in range(5)]
        self.R_m = s(nc.sbuf_tensor("R_m", [128, 32], F32))

        self.pb = [s(nc.psum_tensor(f"pb{i}", [128, 512], F32)) for i in range(8)]
        self.brel = [[] for _ in range(8)]

        self.sLD = Sem(stack, nc, "sLD")
        self.sX = Sem(stack, nc, "sX")
        self.sDW = Sem(stack, nc, "sDW")
        self.sR = [Sem(stack, nc, "sR0"), Sem(stack, nc, "sR1")]
        self.sCC = Sem(stack, nc, "sCC")
        self.sCL = Sem(stack, nc, "sCL")
        self.sPL = Sem(stack, nc, "sPL")
        self.sCV = Sem(stack, nc, "sCV")
        self.sWLs = [Sem(stack, nc, f"sWL{i}") for i in range(5)]
        self.sPE = Sem(stack, nc, "sPE")
        self.sDV = Sem(stack, nc, "sDV")
        self.sAC = Sem(stack, nc, "sAC")
        self.sST = Sem(stack, nc, "sST")

        self.wready = {}
        self.wslotrel = {}
        self.wparity = 0
        self.wr_written = [0] * NSL
        self.preloaded = {}
        self.ex_pend = []
        self.ex_g2_prev = 0

    def T(self, f): self.block.tensor(f)
    def V(self, f): self.block.vector(f)
    def A(self, f): self.block.scalar(f)
    def G(self, f): self.block.gpsimd(f)
    def Y(self, f): self.block.sync(f)

    def hi(self, sl, cb): return self.HI[:, sl * 4096 + cb * 512: sl * 4096 + (cb + 1) * 512]
    def lo(self, cb):   return self.BSC[:, cb * 512:(cb + 1) * 512]
    def ys(self, cb):   return self.BSC[:, cb * 512:(cb + 1) * 512]
    def ug(self, cb):   return self.BSC[:, 4096 + cb * 512: 4096 + (cb + 1) * 512]
    def k2(self, j):    return self.B2[:, j * 512:(j + 1) * 512]
    def combv(self, lb): return self.B2[:, lb * 512:(lb + 1) * 512]
    def lat(self, lb):  return self.B2[:, 4096 + lb * 512: 4096 + (lb + 1) * 512]
    def q2v(self, j):   return self.B2[:, 4096 + j * 512: 4096 + (j + 1) * 512]
    def g2(self, ib):   return self.B2[:, 6144 + ib * 512: 6144 + (ib + 1) * 512]
    def A_(self, j):    return self.AR_A[:, j * 512:(j + 1) * 512]
    def B_(self, j):    return self.AR_B[:, j * 512:(j + 1) * 512]
    def C_(self, j):    return self.AR_C[:, j * 512:(j + 1) * 512]
    def XN_(self, j):   return self.AR_XN[:, j * 512:(j + 1) * 512]
    def sg(self, j):    return self.AR_A[:, j * 512:(j + 1) * 512]
    def xb(self, b):    return self.AR_C[:, b * 1024:(b + 1) * 1024]
    def xnb(self, b):   return self.AR_XN[:, b * 1024:(b + 1) * 1024]
    def outb(self, b):  return self.AR_XN[:, b * 1024:(b + 1) * 1024]
    def xrb(self, b):   return self.AR_C[:, b * 1024:(b + 1) * 1024]
    def hsq(self, sb):
        ar = self.AR_A if sb < 16 else self.AR_B
        j = sb % 16
        return ar[:].bitcast(BF16)[:, j * 512:(j + 1) * 512]
    def kv_blk(self, cb, j):
        return self.WBuf[:, cb * 2048 + j * 128: cb * 2048 + (j + 1) * 128]
    def slot_blk(self, par, cb, jj):
        return self.slot(par)[:, cb * 512 + jj * 128: cb * 512 + (jj + 1) * 128]
    def wbc2(self, e):  return self.WBC2[:, (e % 2) * 512:((e % 2) + 1) * 512]
    def cst(self, i):   return self.CONSTS[:, i:i + 1]
    def slot(self, par): return self.WBuf[:, par * 4096:(par + 1) * 4096]

    def bank_wait(self, eng, i):
        for tr, val in self.brel[i]:
            tr.wait_at(eng, val)
        self.brel[i] = []

    def bank_free(self, i, *trackers):
        self.brel[i] = [(tr, tr.n) for tr in trackers]

    def wload(self, name, dst, src_ap, slot_key=None):
        keys = slot_key if isinstance(slot_key, (list, tuple)) else (
            [] if slot_key is None else [slot_key])
        def g(g):
            for k in keys:
                rel = self.wslotrel.pop(k, None)
                if rel is not None:
                    tr, val = rel
                    tr.wait_at(g, val)
            self.sWLs[0].inc(g.dma_start(out=dst, in_=src_ap), 16)
        self.G(g)
        self.wready[name] = self.sWLs[0].n

    def wload_slot(self, name, src_cols):
        par = self.wparity; self.wparity = (self.wparity + 1) % 5
        self.wload(name, self.slot(par).rearrange("p (cb e) -> p cb e", e=512),
                   src_cols.rearrange("(cb p) e -> p cb e", p=128), slot_key=par)
        return par, self.wready[name]

    def wrelease(self, slot_key, tracker):
        self.wslotrel[slot_key] = (tracker, tracker.n)

    def build_convert_plan(self):
        def halves(src2d):
            r = src2d.shape[0]
            h = r // 2
            return [src2d[0:h].rearrange("(o p) i -> p o i", p=128),
                    src2d[h:r].rearrange("(o p) i -> p o i", p=128)]
        loads = []
        loads.append(halves(self.d_wqkvT[:, 1024:1536]))
        loads.append(halves(self.d_wqkvT[:, 1536:2048]))
        loads.append(halves(self.d_wqkvT[:, 2048:2560]))
        loads.append(halves(self.d_wqkvT[:, 2560:3072]))
        loads.append(halves(self.d_fc1T[:, 0:512]))
        for e in range(8):
            loads.append([self.d_w1T[e].rearrange("(o p) i -> p o i", p=128),
                          self.d_w2T[e].rearrange("(o p) i -> p o i", p=128)])
        loads.append(halves(self.d_wqkvT[:, 0:512]))
        loads.append(halves(self.d_wqkvT[:, 512:1024]))
        for p in range(8):
            loads.append(halves(self.d_upT[:, p * 512:(p + 1) * 512]))
        loads.append(halves(self.d_fc2T))
        for p in range(8):
            loads.append(halves(self.d_downT[p * 512:(p + 1) * 512, :]))
        for f0 in (8, 12, 0, 4):
            loads.append(halves(self.d_wswT[:, f0 * 128:(f0 + 4) * 128]))
        loads.append(halves(self.d_woutT[:, 0:512]))
        loads.append(halves(self.d_woutT[:, 512:1024]))
        self.cv_half = [h for pair in loads for h in pair]
        self.cv_load_val = [0] * 76
        self.cv_store_val = [0] * 76
        self.cv_next = 0

    def convert_all(self):
        def g(g):
            for i in range(76):
                src = self.cv_half[i]
                isz = src.shape[2]
                dst = self.d_wbf[i].rearrange("p (o i) -> p o i", i=isz)
                self.sCV.inc(g.dma_start(out=dst, in_=src), 16)
                self.cv_store_val[i] = self.sCV.n
        self.G(g)

    def wload_cached(self, name, idx):
        if name in self.preloaded:
            return self.preloaded.pop(name)
        par = self.wparity; self.wparity = (self.wparity + 1) % 5
        def g(g):
            rel = self.wslotrel.pop(par, None)
            if rel is not None:
                tr, val = rel
                tr.wait_at(g, val)
            g.wait_ge(self.sCV.h, self.cv_total)
            self.sWLs[par].inc(g.dma_start(
                out=self.slot(par).rearrange("p (t c) -> p t c", c=2048),
                in_=self.d_wbf[2 * idx:2 * idx + 2].rearrange("t p c -> p t c")), 16)
        self.G(g)
        self.wready[name] = self.sWLs[par].n
        return par, self.wready[name]

    def slice_barrier(self, with_store=True):
        trs = (self.sPE, self.sDV, self.sAC, self.sST, self.sLD) if with_store else (
            self.sPE, self.sDV, self.sAC, self.sLD)
        snaps = [(tr, tr.n) for tr in trs]
        def mk(engf):
            def f(e):
                for tr, val in snaps:
                    tr.wait_at(e, val)
            engf(f)
        mk(self.T); mk(self.V); mk(self.A); mk(self.Y)

    def startup(self):
        sLD, sDV = self.sLD, self.sDV

        def y(sync):
            sLD.inc(sync.dma_start(out=self.IDENT[:], in_=self.d_ident), 16)
            sLD.inc(sync.dma_start(out=self.ONES1[:], in_=self.d_ones1), 16)
            sLD.inc(sync.dma_start(out=self.ONES128[:], in_=self.d_ones128), 16)
            sLD.inc(sync.dma_start(out=self.CONSTS[:], in_=self.d_consts), 16)
            sLD.inc(sync.dma_start(out=self.CBB[:], in_=self.d_cb), 16)
            sLD.inc(sync.dma_start(out=self.MASK[:], in_=self.d_mask), 16)
            sLD.inc(sync.dma_start(
                out=self.GT[:].rearrange("p (cb e) -> p cb e", e=8),
                in_=self.d_gateT.rearrange("(cb p) e -> p cb e", p=128)), 16)
        self.Y(y)

        def g(g):
            self.sWLs[0].inc(g.dma_start(
                out=self.WBuf[:, 0:16384].rearrange("p (cb e) -> p cb e", e=2048),
                in_=self.d_wqkvT[:, 1024:3072].rearrange("(cb p) e -> p cb e", p=128)), 16)
        self.G(g)
        self.wready["KV"] = self.sWLs[0].n

        def v(v):
            sLD.wait_all(v)
            v.memset(self.SLOT[:], 0.0)
            v.memset(self.OFFS[:], 0.0)
            i = v.tensor_copy(self.GH[:], self.GT[:])
            sDV.inc(i); v.wait_ge(sDV.h, sDV.n)
            i = v.tensor_tensor(self.GL[:], self.GT[:], self.GH[:], op=ALU.subtract)
            sDV.inc(i)
        self.V(v)

    def pre_x(self, sl):
        sDV = self.sDV

        def y(sync):
            sDV.wait_at(sync, self.pp_xn_done)
            self.sX.inc(sync.dma_start(
                out=self.AR_C[:].rearrange("p (b d) -> p b d", d=1024),
                in_=self.d_x[sl * TS:(sl + 1) * TS, :].rearrange("(b p) d -> p b d", p=128)), 16)
        self.Y(y)
        self.pp_x_ready = self.sX.n

    def pre_ln(self, sl):
        sLD, sPE, sDV, sAC = self.sLD, self.sPE, self.sDV, self.sAC
        x_ready = self.pp_x_ready

        def v(v):
            self.sX.wait_at(v, x_ready)
            for b in range(NB):
                i = v.tensor_reduce(self.ST1[:, b:b + 1], self.xb(b), axis=AX.X, op=ALU.add)
            sDV.inc(i)
        self.V(v)

        def a(sc):
            self.sX.wait_at(sc, x_ready)
            sDV.wait_at(sc, self.pp_var_done)
            dmp = self.B2[:].bitcast(F32)
            for b in range(NB):
                i = sc.activation(dmp[:, b * 1024:(b + 1) * 1024], self.xb(b), AF.Square,
                                  accum_out=self.ST1[:, 4 + b:5 + b])
            sAC.inc(i)
        self.A(a)

        def v2(v):
            sDV.wait_all(v)
            sAC.wait_all(v)
            i = v.tensor_scalar(self.ST1[:, 8:12], self.ST1[:, 0:4], 1.0 / 1024.0, None, op0=ALU.mult)
            sDV.inc(i); v.wait_ge(sDV.h, sDV.n)
            i = v.tensor_scalar(self.ST1[:, 20:24], self.ST1[:, 4:8], 1.0 / 1024.0, None, op0=ALU.mult)
            sDV.inc(i); v.wait_ge(sDV.h, sDV.n)
            i = v.tensor_tensor(self.ST1[:, 24:28], self.ST1[:, 8:12], self.ST1[:, 8:12], op=ALU.mult)
            sDV.inc(i); v.wait_ge(sDV.h, sDV.n)
            i = v.tensor_tensor(self.ST1[:, 4:8], self.ST1[:, 20:24], self.ST1[:, 24:28], op=ALU.subtract)
            sDV.inc(i)
        self.V(v2)
        self.pp_var_done = sDV.n

        def a2(sc):
            sDV.wait_all(sc)
            i = sc.activation(self.ST1[:, 20:24], self.ST1[:, 4:8], AF.Ln, bias=self.cst(0))
            sAC.inc(i); sc.wait_ge(sAC.h, sAC.n)
            i = sc.activation(self.ST1[:, 12:16], self.ST1[:, 20:24], AF.Exp, scale=self.cst(3))
            sAC.inc(i)
        self.A(a2)

        def v3(v):
            sAC.wait_all(v)
            i = v.tensor_tensor(self.ST1[:, 24:28], self.ST1[:, 4:8], self.ST1[:, 12:16], op=ALU.mult)
            sDV.inc(i); v.wait_ge(sDV.h, sDV.n)
            i = v.tensor_tensor(self.ST1[:, 24:28], self.ST1[:, 24:28], self.ST1[:, 12:16], op=ALU.mult)
            sDV.inc(i)
        self.V(v3)

        def a3(sc):
            sDV.wait_all(sc)
            sPE.wait_at(sc, self.pp_xpose_done)
            i = sc.activation(self.ST1[:, 20:24], self.ST1[:, 24:28], AF.Ln, bias=self.cst(1))
            sAC.inc(i); sc.wait_ge(sAC.h, sAC.n)
            i = sc.activation(self.ST1[:, 16:20], self.ST1[:, 20:24], AF.Exp, scale=self.cst(3))
            sAC.inc(i)
        self.A(a3)

        def v4(v):
            sAC.wait_all(v)
            i = v.tensor_scalar(self.SOUT[:, sl * 8: sl * 8 + 4], self.ST1[:, 12:16],
                                1.0, None, op0=ALU.add)
            sDV.inc(i); v.wait_ge(sDV.h, sDV.n)
            i = v.tensor_tensor(self.ST1[:, 28:32], self.ST1[:, 8:12], self.ST1[:, 12:16], op=ALU.mult)
            sDV.inc(i); v.wait_ge(sDV.h, sDV.n)
            i = v.tensor_scalar(self.SOUT[:, sl * 8 + 4: sl * 8 + 8], self.ST1[:, 28:32],
                                -1.0, None, op0=ALU.mult)
            sDV.inc(i)
        self.V(v4)

        def v4b(v):
            sPE.wait_at(v, self.pp_xpose_done)
            for b in range(NB):
                i = v.tensor_scalar(self.xnb(b), self.xb(b),
                                    self.ST1[:, 8 + b:9 + b], self.ST1[:, 12 + b:13 + b],
                                    op0=ALU.subtract, op1=ALU.mult)
            sDV.inc(i)
        self.V(v4b)
        xn_done = sDV.n
        self.pp_xn_done = xn_done

        def t(tt):
            self.bank_wait(tt, 6)
            sDV.wait_at(tt, xn_done)
            for b in range(NB):
                i = tt.transpose(self.pb[6][0:1, b * 128:(b + 1) * 128],
                                 self.ST1[:, 16 + b:17 + b], self.IDENT[:])
            sPE.inc(i)
        self.T(t)
        tr_done = sPE.n

        def a4(sc):
            sPE.wait_at(sc, tr_done)
            sDV.wait_at(sc, self.pp_srow_v2)
            i = sc.activation(self.ROWS[0:1, 0:512], self.pb[6][0:1, 0:512], AF.Copy)
            sAC.inc(i)
        self.A(a4)
        self.bank_free(6, sAC)

        def v5(v):
            sPE.wait_at(v, self.pp_rbc_pe)
            sAC.wait_all(v)
            i = v.tensor_copy(self.R_ALL[0:1, sl * 512:(sl + 1) * 512], self.ROWS[0:1, 0:512])
            sDV.inc(i); v.wait_ge(sDV.h, sDV.n)
            i = v.tensor_copy(self.ROWSB[0:1, 0:512], self.ROWS[0:1, 0:512])
            sDV.inc(i); v.wait_ge(sDV.h, sDV.n)
            i = v.tensor_tensor(self.ROWSB[0:1, 512:1024], self.ROWS[0:1, 0:512],
                                self.ROWSB[0:1, 0:512], op=ALU.subtract)
            sDV.inc(i)
        self.V(v5)

        self.pp_hilo = []
        for cb in range(CB):
            bk = 6 + (cb % 2)
            def t2(tt, cb=cb, bk=bk):
                self.bank_wait(tt, bk)
                sDV.wait_at(tt, xn_done)
                for b in range(NB):
                    i = tt.transpose(self.pb[bk][:, b * 128:(b + 1) * 128],
                                     self.AR_XN[:, b * 1024 + cb * 128: b * 1024 + (cb + 1) * 128],
                                     self.IDENT[:])
                sPE.inc(i)
            self.T(t2)
            pe_done = sPE.n

            def e2(v, sl=sl, cb=cb, bk=bk, pe_done=pe_done):
                sPE.wait_at(v, pe_done)
                sPE.wait_at(v, self.pp_gate_pe)
                i = v.tensor_copy(self.hi(sl, cb), self.pb[bk][:, 0:512])
                sDV.inc(i); v.wait_ge(sDV.h, sDV.n)
                i = v.tensor_tensor(self.lo(cb), self.pb[bk][:, 0:512], self.hi(sl, cb),
                                    op=ALU.subtract)
                sDV.inc(i)
            self.V(e2)
            self.bank_free(bk, sDV)
            self.pp_hilo.append(sDV.n)
        self.pp_xpose_done = self.sPE.n

    def pre_gate(self, sl):
        sPE, sDV, sAC, sLD = self.sPE, self.sDV, self.sAC, self.sLD
        hi_done = sDV.n

        def t(tt):
            self.bank_wait(tt, 4)
            n = 0
            prods = [(self.GH, lambda cb: self.hi(sl, cb)), (self.GH, self.lo),
                     (self.GL, lambda cb: self.hi(sl, cb)), (self.GL, self.lo)]
            total = len(prods) * CB
            for cb in range(CB):
                sDV.wait_at(tt, self.pp_hilo[cb])
                for gw, xf in prods:
                    i = tt.matmul(self.pb[4][0:8, 0:512],
                                  gw[:, cb * 8:(cb + 1) * 8], xf(cb),
                                  start=(n == 0), stop=(n == total - 1))
                    n += 1
            sPE.inc(i)
        self.T(t)
        g_done = sPE.n
        self.pp_gate_pe = g_done

        def a(sc):
            sPE.wait_at(sc, g_done)
            sPE.wait_at(sc, self.pp_gt_pe)
            i = sc.activation(self.G8[:, :], self.pb[4][0:8, 0:512], AF.Copy)
            sAC.inc(i)
        self.A(a)
        self.bank_free(4, sAC)
        g8_done = sAC.n

        def t2(tt):
            self.bank_wait(tt, 5)
            sAC.wait_at(tt, g8_done)
            for b in range(NB):
                i = tt.transpose(self.pb[5][:, b * 8:(b + 1) * 8],
                                 self.G8[:, b * 128:(b + 1) * 128],
                                 self.IDENT[0:8, 0:8])
            sPE.inc(i)
        self.T(t2)
        gt_done = sPE.n
        self.pp_gt_pe = gt_done

        def a2(sc):
            sPE.wait_at(sc, gt_done)
            sDV.wait_at(sc, self.pp_wte_done)
            sPE.wait_at(sc, self.pp_wt_pe)
            i = sc.activation(self.R_s[:, 0:32], self.pb[5][:, 0:32], AF.Exp, scale=self.cst(5))
            sAC.inc(i); sc.wait_ge(sAC.h, sAC.n)
            i = sc.activation(self.R_wte[:, 0:32], self.R_s[:, 0:32], AF.Ln, bias=self.cst(4))
            sAC.inc(i); sc.wait_ge(sAC.h, sAC.n)
            i = sc.activation(self.R_s[:, 0:32], self.R_wte[:, 0:32], AF.Exp, scale=self.cst(5))
            sAC.inc(i)
        self.A(a2)
        self.bank_free(5, sAC)

        def v(v):
            sAC.wait_all(v)
            B4 = range(NB)
            def step(fn):
                for b in B4:
                    i = fn(b)
                self.sDV.inc(i)
                v.wait_ge(self.sDV.h, self.sDV.n)
            s_ = lambda b: self.R_s[:, b * 8:(b + 1) * 8]
            t0 = lambda b: self.R_t[0][:, b * 8:(b + 1) * 8]
            t1 = lambda b: self.R_t[1][:, b * 8:(b + 1) * 8]
            t2_ = lambda b: self.R_t[2][:, b * 8:(b + 1) * 8]
            t3 = lambda b: self.R_t[3][:, b * 8:(b + 1) * 8]
            t4 = lambda b: self.R_t[4][:, b * 8:(b + 1) * 8]
            wte = lambda b: self.R_wte[:, b * 8:(b + 1) * 8]
            m = lambda b, i_: self.R_m[:, b * 8 + i_: b * 8 + i_ + 1]
            step(lambda b: v.tensor_tensor(t0(b), s_(b), self.CBB[:, 0:8], op=ALU.add))
            step(lambda b: v.tensor_reduce(m(b, 0), t0(b)[:, 0:4], axis=AX.X, op=ALU.max))
            step(lambda b: v.tensor_scalar(t1(b)[:, 0:4], t0(b)[:, 0:4], m(b, 0), None, op0=ALU.is_equal))
            step(lambda b: v.tensor_scalar(t1(b)[:, 0:4], t1(b)[:, 0:4], -1e30, None, op0=ALU.mult))
            step(lambda b: v.tensor_tensor(t2_(b)[:, 0:4], t0(b)[:, 0:4], t1(b)[:, 0:4], op=ALU.add))
            step(lambda b: v.tensor_reduce(m(b, 1), t2_(b)[:, 0:4], axis=AX.X, op=ALU.max))
            step(lambda b: v.tensor_tensor(m(b, 2), m(b, 0), m(b, 1), op=ALU.add))
            step(lambda b: v.tensor_reduce(m(b, 3), t0(b)[:, 4:8], axis=AX.X, op=ALU.max))
            step(lambda b: v.tensor_scalar(t1(b)[:, 4:8], t0(b)[:, 4:8], m(b, 3), None, op0=ALU.is_equal))
            step(lambda b: v.tensor_scalar(t1(b)[:, 4:8], t1(b)[:, 4:8], -1e30, None, op0=ALU.mult))
            step(lambda b: v.tensor_tensor(t2_(b)[:, 4:8], t0(b)[:, 4:8], t1(b)[:, 4:8], op=ALU.add))
            step(lambda b: v.tensor_reduce(m(b, 4), t2_(b)[:, 4:8], axis=AX.X, op=ALU.max))
            step(lambda b: v.tensor_tensor(m(b, 5), m(b, 3), m(b, 4), op=ALU.add))
            step(lambda b: v.tensor_tensor(m(b, 6), m(b, 2), m(b, 5), op=ALU.is_ge))
            step(lambda b: v.tensor_scalar(m(b, 7), m(b, 6), -1.0, 1.0, op0=ALU.mult, op1=ALU.add))
            step(lambda b: v.tensor_scalar(t3(b)[:, 0:4], t0(b)[:, 0:4], m(b, 6), None, op0=ALU.mult))
            step(lambda b: v.tensor_scalar(t3(b)[:, 4:8], t0(b)[:, 4:8], m(b, 7), None, op0=ALU.mult))
            step(lambda b: v.tensor_reduce(m(b, 0), t3(b), axis=AX.X, op=ALU.max))
            step(lambda b: v.tensor_scalar(t1(b), t3(b), m(b, 0), None, op0=ALU.is_equal))
            step(lambda b: v.tensor_scalar(t4(b), t1(b), -1e30, None, op0=ALU.mult))
            step(lambda b: v.tensor_tensor(t4(b), t3(b), t4(b), op=ALU.add))
            step(lambda b: v.tensor_reduce(m(b, 1), t4(b), axis=AX.X, op=ALU.max))
            step(lambda b: v.tensor_scalar(t2_(b), t4(b), m(b, 1), None, op0=ALU.is_equal))
            step(lambda b: v.tensor_tensor(t1(b), t1(b), t2_(b), op=ALU.add))
            step(lambda b: v.tensor_tensor(wte(b), s_(b), t1(b), op=ALU.mult))
            step(lambda b: v.tensor_reduce(m(b, 2), wte(b), axis=AX.X, op=ALU.add))
            step(lambda b: v.tensor_scalar(m(b, 2), m(b, 2), 1e-20, None, op0=ALU.add))
            step(lambda b: v.reciprocal(m(b, 3), m(b, 2)))
            step(lambda b: v.tensor_scalar(wte(b), wte(b), m(b, 3), None, op0=ALU.mult))
        self.V(v)
        wte_done = self.sDV.n
        self.pp_wte_done = wte_done

        def t3_(tt):
            self.bank_wait(tt, 6)
            sDV.wait_at(tt, wte_done)
            for b in range(NB):
                i = tt.transpose(self.pb[6][0:8, b * 128:(b + 1) * 128],
                                 self.R_wte[:, b * 8:(b + 1) * 8],
                                 self.IDENT[:])
            sPE.inc(i)
        self.T(t3_)
        wt_done = sPE.n
        self.pp_wt_pe = wt_done

        def a3(sc):
            sPE.wait_at(sc, wt_done)
            if sl > 0:
                self.sDW.wait_at(sc, self.wr_written[sl - 1])
            i = sc.activation(self.W8H[:, :], self.pb[6][0:8, 0:512], AF.Copy)
            sAC.inc(i)
        self.A(a3)
        self.bank_free(6, sAC)
        w8_done = sAC.n

        def y(sync):
            sAC.wait_at(sync, w8_done)
            self.sDW.inc(sync.dma_start(out=self.d_wr[0:8, sl * 512:(sl + 1) * 512], in_=self.W8H[:, :]), 16)
        self.Y(y)
        self.wr_written[sl] = self.sDW.n

    def pre_mid(self, sl):
        sPE, sDV, sAC = self.sPE, self.sDV, self.sAC

        for j in range(CB):
            bk = j % 4
            def t(tt, sl=sl, j=j, bk=bk):
                self.bank_wait(tt, bk)
                self.sWLs[0].wait_at(tt, self.wready["KV"])
                for cb in range(CB):
                    i = tt.matmul(self.pb[bk][:, 0:512],
                                  self.kv_blk(cb, j), self.hi(sl, cb),
                                  start=(cb == 0), stop=(cb == CB - 1))
                sPE.inc(i)
            self.T(t)
            peval = sPE.n
            def ev(sc, j=j, bk=bk, peval=peval):
                sPE.wait_at(sc, peval)
                i = sc.activation(self.k2(j), self.pb[bk][:, 0:512], AF.Square)
                sAC.inc(i)
            self.A(ev)
            def ev2(v, j=j, bk=bk, peval=peval):
                sPE.wait_at(v, peval)
                i = v.tensor_copy(self.A_(j), self.pb[bk][:, 0:512])
                sDV.inc(i)
            self.V(ev2)
            self.bank_free(bk, sAC, sDV)
        k2_done = sAC.n

        def t(tt):
            self.bank_wait(tt, 4)
            sAC.wait_at(tt, k2_done)
            for j in range(CB):
                i = tt.matmul(self.pb[4][0:1, 0:512], self.ONES128[:, :], self.k2(j),
                              start=(j == 0), stop=(j == CB - 1))
            sPE.inc(i)
        self.T(t)
        ss_done = sPE.n

        def a(sc):
            sPE.wait_at(sc, ss_done)
            sDV.wait_at(sc, self.pp_srow_v2)
            i = sc.activation(self.ROWS[0:1, 512:1024], self.pb[4][0:1, 0:512], AF.Copy)
            sAC.inc(i)
        self.A(a)
        self.bank_free(4, sAC)

        self._s_row_math(pe_guard=self.pp_skb)
        self.pp_srow_v2 = self.sDV.n

        def t2(tt):
            self.bank_wait(tt, 5)
            sDV.wait_at(tt, self.sDV.n)
            i = tt.matmul(self.pb[5][:, 0:512], self.ONES1[:, :],
                          self.ROWSB[0:1, 1024:1536], start=True, stop=False)
            i = tt.matmul(self.pb[5][:, 0:512], self.ONES1[:, :],
                          self.ROWSB[0:1, 1536:2048], start=False, stop=True)
            sPE.inc(i)
        self.T(t2)
        skb_done = sPE.n
        self.pp_skb = skb_done

        def v(v):
            sPE.wait_at(v, skb_done)
            for j in range(CB):
                i = v.tensor_tensor(self.A_(j), self.A_(j), self.pb[5][:, 0:512], op=ALU.mult)
            sDV.inc(i)
        self.V(v)
        ks_done = sDV.n
        self.bank_free(5, sDV)

        def a2(sc):
            sDV.wait_at(sc, ks_done)
            for j in range(CB):
                i = sc.activation(self.B_(j), self.A_(j), AF.Exp,
                                  accum_out=self.COLT[:, j:j + 1])
            sAC.inc(i)
        self.A(a2)
        w_done = sAC.n

        def t3(tt):
            self.bank_wait(tt, 5)
            i = tt.matmul(self.pb[5][:, 0:512], self.ONES1[:, :],
                          self.ROWSB[0:1, 0:512], start=True, stop=False)
            i = tt.matmul(self.pb[5][:, 0:512], self.ONES1[:, :],
                          self.ROWSB[0:1, 512:1024], start=False, stop=True)
            sPE.inc(i)
        self.T(t3)
        rb_done = sPE.n
        self.pp_rbc_pe = rb_done

        def a3(sc):
            sPE.wait_at(sc, rb_done)
            sDV.wait_at(sc, self.pp_vevac)
            i = sc.activation(self.SCR2[:, :], self.pb[5][:, 0:512], AF.Copy)
            sAC.inc(i)
        self.A(a3)
        self.bank_free(5, sAC)
        rbc_done = sAC.n

        for j in range(CB):
            bk = j % 4
            def t4(tt, sl=sl, j=j, bk=bk):
                self.bank_wait(tt, bk)
                for cb in range(CB):
                    i = tt.matmul(self.pb[bk][:, 0:512],
                                  self.kv_blk(cb, 8 + j), self.hi(sl, cb),
                                  start=(cb == 0), stop=(cb == CB - 1))
                sPE.inc(i)
            self.T(t4)
            peval = sPE.n
            def ev(v, j=j, bk=bk, peval=peval):
                sPE.wait_at(v, peval)
                sAC.wait_at(v, max(rbc_done, w_done))
                i = v.tensor_tensor(self.A_(j), self.pb[bk][:, 0:512], self.SCR2[:, :], op=ALU.mult)
                sDV.inc(i)
            self.V(ev)
            self.bank_free(bk, sDV)
        self.pp_vevac = sDV.n

        def v2(v):
            sAC.wait_at(v, w_done)
            v.wait_ge(sDV.h, sDV.n)
            dmp = self.B2[:].bitcast(F32)
            for j in range(CB):
                i = v.scalar_tensor_tensor(dmp[:, j * 512:(j + 1) * 512],
                                           self.A_(j), 1.0, self.B_(j),
                                           op0=ALU.mult, op1=ALU.mult,
                                           accum_out=self.COLT[:, 8 + j:9 + j])
            sDV.inc(i); v.wait_ge(sDV.h, sDV.n)
            i = v.tensor_tensor(self.SLOT[:, 0:16], self.SLOT[:, 0:16],
                                self.COLT[:, 0:16], op=ALU.add)
            sDV.inc(i)
        self.V(v2)

    def _s_row_math(self, s2_off=512, pe_guard=0):
        sDV, sAC, sPE = self.sDV, self.sAC, self.sPE
        R = self.ROWS
        RB = self.ROWSB
        scr1 = R[0:1, 1536:2048]
        scr2 = R[0:1, 1024:1536]

        def v(v):
            sAC.wait_all(v)
            sPE.wait_at(v, pe_guard)
            i = v.tensor_scalar(scr1, R[0:1, s2_off:s2_off + 512], 1.0 / 1024.0, None, op0=ALU.mult)
            sDV.inc(i); v.wait_ge(sDV.h, sDV.n)
            i = v.tensor_tensor(scr2, R[0:1, 0:512], R[0:1, 0:512], op=ALU.mult)
            sDV.inc(i); v.wait_ge(sDV.h, sDV.n)
            i = v.tensor_tensor(scr1, scr1, scr2, op=ALU.mult)
            sDV.inc(i)
        self.V(v)

        def a(sc):
            sDV.wait_all(sc)
            i = sc.activation(scr2, scr1, AF.Ln, bias=self.CONSTS[0:1, 1:2])
            sAC.inc(i); sc.wait_ge(sAC.h, sAC.n)
            i = sc.activation(scr1, scr2, AF.Exp, scale=self.CONSTS[0:1, 3:4])
            sAC.inc(i)
        self.A(a)

        def v2(v):
            sAC.wait_all(v)
            i = v.tensor_tensor(R[0:1, 1024:1536], R[0:1, 0:512], scr1, op=ALU.mult)
            sDV.inc(i); v.wait_ge(sDV.h, sDV.n)
            i = v.tensor_copy(RB[0:1, 1024:1536], R[0:1, 1024:1536])
            sDV.inc(i); v.wait_ge(sDV.h, sDV.n)
            i = v.tensor_tensor(RB[0:1, 1536:2048], R[0:1, 1024:1536],
                                RB[0:1, 1024:1536], op=ALU.subtract)
            sDV.inc(i)
        self.V(v2)

    def make_offsets(self):
        sDV, sLD, sCC = self.sDV, self.sLD, self.sCC
        tot_ready = sDV.n

        sCL, sPL = self.sCL, self.sPL

        def g(g):
            sDV.wait_at(g, tot_ready)
            sCC.inc(g.dma_start(out=self.d_tin, in_=self.SLOT[:, 0:16]), 16)
            g.wait_ge(sCC.h, sCC.n)
            sCL.inc(g.collective_compute(
                "AllGather",
                mybir.AluOpType.bypass,
                replica_groups=[list(range(NR))],
                ins=[self.d_tin.opt()],
                outs=[self.d_tout.opt()],
            ), 1)
            g.wait_ge(sCL.h, sCL.n)
            sCC.inc(g.dma_start(
                out=self.TOTS[:].rearrange("p (r c) -> p r c", c=16),
                in_=self.d_tout.rearrange("(r p) c -> p r c", p=128)), 16)
        self.G(g)
        gath_done = sCC.n

        def g2(g):
            g.wait_ge(sCC.h, gath_done)
            for r in range(NR):
                i = g.tensor_scalar(self.COLT[:, 0:16], self.TOTS[:, r * 16:(r + 1) * 16],
                                    self.MASK[:, r:r + 1], None, op0=ALU.mult)
                sPL.inc(i, 1)
                g.wait_ge(sPL.h, sPL.n)
                i = g.tensor_tensor(self.OFFS[:, :], self.OFFS[:, :], self.COLT[:, 0:16], op=ALU.add)
                sPL.inc(i, 1)
                g.wait_ge(sPL.h, sPL.n)
            i = g.tensor_copy(self.CARRY[:, :], self.OFFS[:, :])
            sPL.inc(i, 1)
        self.G(g2)
        self.offs_done = sPL.n

    def m_begin(self, sl):
        sLD, sPE, sDV, sAC = self.sLD, self.sPE, self.sDV, self.sAC
        self.wr16_guard = [0, 0]

        def v(v):
            i = v.tensor_copy(self.ROWS[0:1, 0:512], self.R_ALL[0:1, sl * 512:(sl + 1) * 512])
            sDV.inc(i); v.wait_ge(sDV.h, sDV.n)
            i = v.tensor_copy(self.ROWSB[0:1, 0:512], self.ROWS[0:1, 0:512])
            sDV.inc(i); v.wait_ge(sDV.h, sDV.n)
            i = v.tensor_tensor(self.ROWSB[0:1, 512:1024], self.ROWS[0:1, 0:512],
                                self.ROWSB[0:1, 0:512], op=ALU.subtract)
            sDV.inc(i)
        self.V(v)
        rb_ready = sDV.n

        def t(tt):
            self.bank_wait(tt, 5)
            sDV.wait_at(tt, rb_ready)
            i = tt.matmul(self.pb[5][:, 0:512], self.ONES1[:, :],
                          self.ROWSB[0:1, 0:512], start=True, stop=False)
            i = tt.matmul(self.pb[5][:, 0:512], self.ONES1[:, :],
                          self.ROWSB[0:1, 512:1024], start=False, stop=True)
            sPE.inc(i)
        self.T(t)
        rb_done = sPE.n

        def a(sc):
            sPE.wait_at(sc, rb_done)
            i = sc.activation(self.SCR2[:, :], self.pb[5][:, 0:512], AF.Copy)
            sAC.inc(i)
        self.A(a)
        self.bank_free(5, sAC)
        self.rbc_done = sAC.n

    def m_k(self, sl):
        sPE, sDV, sAC = self.sPE, self.sDV, self.sAC
        p0, w0 = self.wload_cached(f"k0_{sl}", 0)
        p1, w1 = self.wload_cached(f"k1_{sl}", 1)
        pars = [(p0, w0)] * 4 + [(p1, w1)] * 4
        for j in range(CB):
            par, wr = pars[j]
            bk = j % 4
            def t(tt, sl=sl, j=j, bk=bk, par=par, wr=wr):
                self.bank_wait(tt, bk)
                self.sWLs[par].wait_at(tt, wr)
                for cb in range(CB):
                    i = tt.matmul(self.pb[bk][:, 0:512],
                                  self.slot_blk(par, cb, j % 4), self.hi(sl, cb),
                                  start=(cb == 0), stop=(cb == CB - 1))
                sPE.inc(i)
            self.T(t)
            peval = sPE.n
            def ev(sc, j=j, bk=bk, peval=peval):
                sPE.wait_at(sc, peval)
                i = sc.activation(self.k2(j), self.pb[bk][:, 0:512], AF.Square)
                sAC.inc(i)
            self.A(ev)
            def ev2(v, j=j, bk=bk, peval=peval):
                sPE.wait_at(v, peval)
                i = v.tensor_copy(self.A_(j), self.pb[bk][:, 0:512])
                sDV.inc(i)
            self.V(ev2)
            self.bank_free(bk, sAC, sDV)
        self.wrelease(p0, sPE)
        self.wrelease(p1, sPE)
        self.k2_done = sAC.n

    def m_ksum(self):
        sPE, sAC = self.sPE, self.sAC
        k2_done = self.k2_done

        def t(tt):
            self.bank_wait(tt, 4)
            sAC.wait_at(tt, k2_done)
            for j in range(CB):
                i = tt.matmul(self.pb[4][0:1, 0:512], self.ONES128[:, :], self.k2(j),
                              start=(j == 0), stop=(j == CB - 1))
            sPE.inc(i)
        self.T(t)
        ss_done = sPE.n

        def a(sc):
            sPE.wait_at(sc, ss_done)
            i = sc.activation(self.ROWS[0:1, 512:1024], self.pb[4][0:1, 0:512], AF.Copy)
            sAC.inc(i)
        self.A(a)
        self.bank_free(4, sAC)

    def m_v(self, sl):
        sPE, sDV, sAC = self.sPE, self.sDV, self.sAC
        p0, w0 = self.wload_cached(f"v0_{sl}", 2)
        p1, w1 = self.wload_cached(f"v1_{sl}", 3)
        pars = [(p0, w0)] * 4 + [(p1, w1)] * 4
        rbc_done = self.rbc_done
        for j in range(CB):
            if j == 2:
                self.m_ksum()
            par, wr = pars[j]
            bk = j % 4
            def t(tt, sl=sl, j=j, bk=bk, par=par, wr=wr):
                self.bank_wait(tt, bk)
                self.sWLs[par].wait_at(tt, wr)
                for cb in range(CB):
                    i = tt.matmul(self.pb[bk][:, 0:512],
                                  self.slot_blk(par, cb, j % 4), self.hi(sl, cb),
                                  start=(cb == 0), stop=(cb == CB - 1))
                sPE.inc(i)
            self.T(t)
            peval = sPE.n
            def ev(v, j=j, bk=bk, peval=peval):
                sPE.wait_at(v, peval)
                sAC.wait_at(v, rbc_done)
                i = v.tensor_tensor(self.C_(j), self.pb[bk][:, 0:512], self.SCR2[:, :], op=ALU.mult)
                sDV.inc(i)
            self.V(ev)
            self.bank_free(bk, sDV)
        self.wrelease(p0, sPE)
        self.wrelease(p1, sPE)

    def m_fc1(self, sl):
        sPE, sDV = self.sPE, self.sDV
        par, wr = self.wload_cached(f"fc1_{sl}", 4)
        for lb in range(4):
            bk = 6 + (lb % 2)
            def t(tt, sl=sl, lb=lb, bk=bk, par=par, wr=wr):
                self.bank_wait(tt, bk)
                self.sWLs[par].wait_at(tt, wr)
                for cb in range(CB):
                    i = tt.matmul(self.pb[bk][:, 0:512],
                                  self.slot_blk(par, cb, lb), self.hi(sl, cb),
                                  start=(cb == 0), stop=(cb == CB - 1))
                sPE.inc(i)
            self.T(t)
            peval = sPE.n
            def ev(v, lb=lb, bk=bk, peval=peval):
                sPE.wait_at(v, peval)
                i = v.tensor_copy(self.lat(lb), self.pb[bk][:, 0:512])
                sDV.inc(i)
            self.V(ev)
            self.bank_free(bk, sDV)
        self.wrelease(par, sPE)
        self.lat_done = sDV.n

    def m_smath_k(self):
        sPE, sDV, sAC = self.sPE, self.sDV, self.sAC

        def t(tt):
            self.bank_wait(tt, 5)
            sDV.wait_at(tt, self.sDV.n)
            i = tt.matmul(self.pb[5][:, 0:512], self.ONES1[:, :],
                          self.ROWSB[0:1, 1024:1536], start=True, stop=False)
            i = tt.matmul(self.pb[5][:, 0:512], self.ONES1[:, :],
                          self.ROWSB[0:1, 1536:2048], start=False, stop=True)
            sPE.inc(i)
        self.T(t)
        skb_done = sPE.n
        self.skb_done = skb_done

        def v(v):
            sPE.wait_at(v, skb_done)
            for j in range(CB):
                i = v.tensor_tensor(self.A_(j), self.A_(j), self.pb[5][:, 0:512], op=ALU.mult)
            sDV.inc(i)
        self.V(v)
        ks_done = sDV.n
        self.bank_free(5, sDV)

        def a(sc):
            sDV.wait_at(sc, ks_done)
            for j in range(CB):
                i = sc.activation(self.B_(j), self.A_(j), AF.Exp)
            sAC.inc(i)
        self.A(a)
        w_done = sAC.n

        def v2(v):
            sAC.wait_at(v, w_done)
            for j in range(CB):
                i = v.tensor_tensor(self.C_(j), self.C_(j), self.B_(j), op=ALU.mult)
            sDV.inc(i)
        self.V(v2)
        kv_done = sDV.n

        self.kv_done = kv_done
        self.w_done_k = w_done

    def m_scans(self, part):
        sDV, sAC, sCC = self.sDV, self.sAC, self.sCC
        kv_done, w_done = self.kv_done, self.w_done_k

        def v2(v):
            if part == 0:
                sAC.wait_at(v, w_done)
                sDV.wait_at(v, kv_done)
                self.sPL.wait_at(v, self.offs_done)
            js = range((part % 2) * 4, (part % 2) * 4 + 4)
            if part < 2:
                for j in js:
                    i = v.tensor_tensor_scan(self.A_(j), self.B_(j), self.B_(j),
                                             self.CARRY[:, j:j + 1], op0=ALU.add, op1=ALU.bypass)
                    sDV.inc(i)
            else:
                for j in js:
                    i = v.tensor_tensor_scan(self.B_(j), self.C_(j), self.C_(j),
                                             self.CARRY[:, 8 + j:9 + j], op0=ALU.add, op1=ALU.bypass)
                    sDV.inc(i)
            if part == 3:
                v.wait_ge(sDV.h, sDV.n)
                for j in range(CB):
                    i = v.tensor_copy(self.CARRY[:, j:j + 1], self.A_(j)[:, 511:512])
                    i = v.tensor_copy(self.CARRY[:, 8 + j:9 + j], self.B_(j)[:, 511:512])
                sDV.inc(i)
        self.V(v2)
        if part == 3:
            self.scans_done = sDV.n

    def m_experts(self, sl, es, first):
        sPE, sDV, sAC, sLD = self.sPE, self.sDV, self.sAC, self.sLD
        lat_done = self.lat_done
        if first:
            self.ex_rot = 0
        for e in es:
            while self.ex_pend and self.ex_pend[0][1] == 3 and self.ex_pend[0][0] <= e - 2:
                self._emit_w2(*self.ex_pend.pop(0))
            par, wr = self.wload_cached(f"ex_{e}_{sl}", 5 + e)

            half = e % 2
            guard = self.wr16_guard[half]
            def y(sync, sl=sl, e=e, half=half, guard=guard):
                self.sDW.wait_at(sync, self.wr_written_all)
                sPE.wait_at(sync, guard)
                self.sR[half].inc(sync.dma_start(out=self.WR16[0:1, half * 512:(half + 1) * 512],
                                                 in_=self.d_wr[e:e + 1, sl * 512:(sl + 1) * 512]), 16)
            self.Y(y)
            row_ready = self.sR[half].n

            wbk = (4, 5, 6, 7)[self.ex_rot % 4]; self.ex_rot += 1
            def tb(tt, e=e, wbk=wbk, half=half, row_ready=row_ready):
                self.bank_wait(tt, wbk)
                self.sR[half].wait_at(tt, row_ready)
                i = tt.matmul(self.pb[wbk][:, 0:512], self.ONES1[:, :],
                              self.WR16[0:1, half * 512:(half + 1) * 512], start=True, stop=True)
                sPE.inc(i)
            self.T(tb)
            wb_peval = sPE.n
            self.wr16_guard[half] = wb_peval
            def ab(v, e=e, wbk=wbk, wb_peval=wb_peval):
                sPE.wait_at(v, wb_peval)
                i = v.tensor_copy(self.wbc2(e), self.pb[wbk][:, 0:512])
                sDV.inc(i)
            self.V(ab)
            self.bank_free(wbk, sDV)
            wbc_ready = sDV.n

            for ib in range(4):
                bk = (4, 5, 6, 7)[self.ex_rot % 4]; self.ex_rot += 1
                def t(tt, e=e, ib=ib, bk=bk, par=par, wr=wr):
                    self.bank_wait(tt, bk)
                    self.sWLs[par].wait_at(tt, wr)
                    sDV.wait_at(tt, lat_done)
                    for lb in range(4):
                        i = tt.matmul(self.pb[bk][:, 0:512],
                                      self.slot(par)[:, lb * 512 + ib * 128: lb * 512 + (ib + 1) * 128],
                                      self.lat(lb),
                                      start=(lb == 0), stop=(lb == 3))
                    sPE.inc(i)
                self.T(t)
                peval = sPE.n
                guard = max(wbc_ready, self.ex_g2_prev)
                def ev(v, e=e, ib=ib, bk=bk, peval=peval, guard=guard):
                    sPE.wait_at(v, peval)
                    sDV.wait_at(v, guard)
                    i = v.tensor_tensor(self.SCRH[:, :], self.pb[bk][:, 0:512],
                                        self.wbc2(e), op=ALU.mult)
                    sDV.inc(i); v.wait_ge(sDV.h, sDV.n)
                    i = v.scalar_tensor_tensor(self.g2(ib), self.pb[bk][:, 0:512], 0.0,
                                               self.SCRH[:, :], op0=ALU.max, op1=ALU.mult)
                    sDV.inc(i)
                self.V(ev)
                self.ex_g2_prev = sDV.n
                self.bank_free(bk, sDV)
                if DEFER_W2:
                    self.ex_pend.append((e, ib, par, sDV.n, first and e == es[0] and ib == 0))
                    if len(self.ex_pend) > 2:
                        self._emit_w2(*self.ex_pend.pop(0))
                else:
                    self._emit_w2(e, ib, par, sDV.n, first and e == es[0] and ib == 0)

    def _emit_w2(self, e, ib, par, g2_done, first_w2):
        sPE, sDV = self.sPE, self.sDV
        def t2(tt, e=e, ib=ib, par=par, g2_done=g2_done, first_w2=first_w2):
            if first_w2:
                for lb in range(4):
                    self.bank_wait(tt, lb)
            sDV.wait_at(tt, g2_done)
            for lb in range(4):
                i = tt.matmul(self.pb[lb][:, 0:512],
                              self.slot(par)[:, 2048 + ib * 512 + lb * 128: 2048 + ib * 512 + (lb + 1) * 128],
                              self.g2(ib),
                              start=(e == 0 and ib == 0), stop=(e == 7 and ib == 3))
            sPE.inc(i)
        self.T(t2)
        if ib == 3:
            self.wrelease(par, sPE)

    def m_comb_evac(self):
        sPE, sDV = self.sPE, self.sDV
        while self.ex_pend:
            self._emit_w2(*self.ex_pend.pop(0))
        comb_done = sPE.n

        def v(v):
            sPE.wait_at(v, comb_done)
            for lb in range(4):
                i = v.tensor_copy(self.combv(lb), self.pb[lb][:, 0:512])
                sDV.inc(i)
        self.V(v)
        for lb in range(4):
            self.bank_free(lb, sDV)

    def m_q_jobs(self, sl):
        sPE, sDV, sAC = self.sPE, self.sDV, self.sAC
        p0, w0 = self.wload_cached(f"q0_{sl}", 13)
        p1, w1 = self.wload_cached(f"q1_{sl}", 14)
        pars = [(p0, w0)] * 4 + [(p1, w1)] * 4
        for j in range(CB):
            par, wr = pars[j]
            bk = j % 4
            def t(tt, sl=sl, j=j, bk=bk, par=par, wr=wr):
                self.bank_wait(tt, bk)
                self.sWLs[par].wait_at(tt, wr)
                for cb in range(CB):
                    i = tt.matmul(self.pb[bk][:, 0:512],
                                  self.slot_blk(par, cb, j % 4), self.hi(sl, cb),
                                  start=(cb == 0), stop=(cb == CB - 1))
                sPE.inc(i)
            self.T(t)
            peval = sPE.n
            def ev(sc, j=j, bk=bk, peval=peval):
                sPE.wait_at(sc, peval)
                if j == 0:
                    self.sST.wait_at(sc, self.prev_store)
                i = sc.activation(self.k2(j), self.pb[bk][:, 0:512], AF.Square)
                sAC.inc(i)
                i = sc.activation(self.XN_(j), self.pb[bk][:, 0:512], AF.Copy)
                sAC.inc(i)
            self.A(ev)
            self.bank_free(bk, sAC)
            if j == 3:
                q2_half = sAC.n
        self.wrelease(p0, sPE)
        self.wrelease(p1, sPE)
        q2_done = sAC.n
        self.q_evac_ac = q2_done

        def t(tt):
            self.bank_wait(tt, 4)
            sAC.wait_at(tt, q2_done)
            for j in range(CB):
                i = tt.matmul(self.pb[4][0:1, 0:512], self.ONES128[:, :], self.k2(j),
                              start=(j == 0), stop=(j == CB - 1))
            sPE.inc(i)
        self.T(t)
        ss_done = sPE.n

        def a(sc):
            sPE.wait_at(sc, ss_done)
            i = sc.activation(self.ROWS[0:1, 512:1024], self.pb[4][0:1, 0:512], AF.Copy)
            sAC.inc(i)
        self.A(a)
        self.bank_free(4, sAC)

    def m_q_tail(self, sl):
        sPE, sDV, sAC, sLD = self.sPE, self.sDV, self.sAC, self.sLD
        self._s_row_math(s2_off=512, pe_guard=self.skb_done)

        def t2(tt):
            self.bank_wait(tt, 5)
            sDV.wait_at(tt, self.sDV.n)
            i = tt.matmul(self.pb[5][:, 0:512], self.ONES1[:, :],
                          self.ROWSB[0:1, 1024:1536], start=True, stop=False)
            i = tt.matmul(self.pb[5][:, 0:512], self.ONES1[:, :],
                          self.ROWSB[0:1, 1536:2048], start=False, stop=True)
            sPE.inc(i)
        self.T(t2)
        sqb_done = sPE.n

        def v(v):
            sPE.wait_at(v, sqb_done)
            sAC.wait_at(v, self.q_evac_ac)
            for j in range(CB):
                i = v.tensor_tensor(self.XN_(j), self.XN_(j), self.pb[5][:, 0:512], op=ALU.mult)
            sDV.inc(i)
        self.V(v)
        self.bank_free(5, sDV)
        self.qs_done = sDV.n

    def m_q_sig(self, sl):
        sDV, sAC = self.sDV, self.sAC
        qs_done = self.qs_done

        def a2(sc):
            sDV.wait_at(sc, qs_done)
            sDV.wait_at(sc, self.scans_done)
            for j in range(CB):
                i = sc.activation(self.C_(j), self.XN_(j), AF.Exp, scale=self.cst(5))
            sAC.inc(i); sc.wait_ge(sAC.h, sAC.n)
            for j in range(CB):
                i = sc.activation(self.XN_(j), self.C_(j), AF.Ln, bias=self.cst(4))
            sAC.inc(i); sc.wait_ge(sAC.h, sAC.n)
            for j in range(CB):
                i = sc.activation(self.C_(j), self.XN_(j), AF.Exp, scale=self.cst(5))
            sAC.inc(i)
            for j in range(CB):
                i = sc.activation(self.XN_(j), self.A_(j), AF.Ln, bias=self.cst(2))
            sAC.inc(i); sc.wait_ge(sAC.h, sAC.n)
            for j in range(CB):
                i = sc.activation(self.A_(j), self.XN_(j), AF.Exp, scale=self.cst(5))
            sAC.inc(i)
        self.A(a2)
        self.rw_done = sAC.n

    def m_y_tail(self, sl):
        sDV, sAC, sLD = self.sDV, self.sAC, self.sLD

        def v2(v):
            sAC.wait_at(v, self.rw_done)
            for j in range(CB):
                i = v.tensor_tensor(self.B_(j), self.B_(j), self.A_(j), op=ALU.mult)
            sDV.inc(i); v.wait_ge(sDV.h, sDV.n)
            for j in range(CB):
                i = v.tensor_tensor(self.ys(j), self.B_(j), self.C_(j), op=ALU.mult)
            sDV.inc(i)
        self.V(v2)
        self.ysig_done = sDV.n

        def y(sync):
            sDV.wait_at(sync, self.ysig_done)
            self.sX.inc(sync.dma_start(
                out=self.AR_C[:].rearrange("p (b d) -> p b d", d=1024),
                in_=self.d_x[sl * TS:(sl + 1) * TS, :].rearrange("(b p) d -> p b d", p=128)), 16)
        self.Y(y)
        self.x_ready = self.sX.n

    def m_up(self, sl):
        sPE, sDV, sAC = self.sPE, self.sDV, self.sAC
        ysig_done = self.ysig_done
        for p in range(8):
            par, wr = self.wload_cached(f"up_{p}_{sl}", 15 + p)
            for sbw in range(4):
                sb = p * 4 + sbw
                bk = (6, 7, 0, 1)[sbw]
                def t(tt, sl=sl, sbw=sbw, bk=bk, par=par, wr=wr):
                    self.bank_wait(tt, bk)
                    self.sWLs[par].wait_at(tt, wr)
                    for cb in range(CB):
                        i = tt.matmul(self.pb[bk][:, 0:512],
                                      self.slot_blk(par, cb, sbw), self.hi(sl, cb),
                                      start=(cb == 0), stop=(cb == CB - 1))
                    sPE.inc(i)
                self.T(t)
                peval = sPE.n
                scr = (self.SCRH, self.SCR2, self.SCR3, self.SCR4)[sb % 4]
                def ev_a(sc, bk=bk, peval=peval, scr=scr):
                    sPE.wait_at(sc, peval)
                    i = sc.activation(scr[:, :], self.pb[bk][:, 0:512], AF.Relu)
                    sAC.inc(i)
                self.A(ev_a)
                acval = sAC.n
                def ev(v, sb=sb, bk=bk, acval=acval, scr=scr):
                    sAC.wait_at(v, acval)
                    sDV.wait_at(v, ysig_done)
                    i = v.tensor_tensor(self.hsq(sb), self.pb[bk][:, 0:512], scr[:, :], op=ALU.mult)
                    sDV.inc(i)
                self.V(ev)
                self.bank_free(bk, sDV, sAC)
            self.wrelease(par, sPE)

    def m_fc2_down(self, sl):
        sPE, sDV, sLD = self.sPE, self.sDV, self.sLD
        parC, wr2 = self.wload_cached(f"fc2_{sl}", 23)
        comb_evac = sDV.n

        def bw(tt):
            for i_ in range(8):
                self.bank_wait(tt, i_)
        self.T(bw)

        def t2(tt):
            self.sWLs[parC].wait_at(tt, wr2)
            sDV.wait_at(tt, comb_evac)
            for tb in range(NB):
                for lb in range(4):
                    lhsT = self.B2[:, lb * 512 + tb * 128: lb * 512 + (tb + 1) * 128]
                    i = tt.matmul(self.pb[2 * tb][:, 0:512],
                                  lhsT, self.slot(parC)[:, lb * 1024: lb * 1024 + 512],
                                  start=(lb == 0), stop=False)
                    i = tt.matmul(self.pb[2 * tb + 1][:, 0:512],
                                  lhsT, self.slot(parC)[:, lb * 1024 + 512: (lb + 1) * 1024],
                                  start=(lb == 0), stop=False)
            sPE.inc(i)
        self.T(t2)
        self.wrelease(parC, sPE)

        hsq_done = sDV.n
        for p in range(8):
            par, wr3 = self.wload_cached(f"dn_{p}_{sl}", 24 + p)

            def t3(tt, p=p, par=par, wr3=wr3):
                self.sWLs[par].wait_at(tt, wr3)
                sDV.wait_at(tt, hsq_done)
                last_piece = (p == 7)
                for tb in range(NB):
                    for sbw in range(4):
                        sb = p * 4 + sbw
                        ar = self.AR_A if sb < 16 else self.AR_B
                        jj = sb % 16
                        lhsT = ar[:].bitcast(BF16)[:, jj * 512 + tb * 128: jj * 512 + (tb + 1) * 128]
                        stop = last_piece and (sbw == 3)
                        i = tt.matmul(self.pb[2 * tb][:, 0:512],
                                      lhsT, self.slot(par)[:, sbw * 1024: sbw * 1024 + 512],
                                      start=False, stop=stop)
                        i = tt.matmul(self.pb[2 * tb + 1][:, 0:512],
                                      lhsT, self.slot(par)[:, sbw * 1024 + 512: (sbw + 1) * 1024],
                                      start=False, stop=stop)
                sPE.inc(i)
            self.T(t3)
            self.wrelease(par, sPE)
        fin_done = sPE.n

        def v(v):
            sPE.wait_at(v, fin_done)
            self.sX.wait_at(v, self.x_ready)
            self.sST.wait_at(v, self.prev_store)
            for tb in range(NB):
                i = v.tensor_scalar(self.xrb(tb), self.xrb(tb),
                                    self.SOUT[:, sl * 8 + tb: sl * 8 + tb + 1],
                                    self.SOUT[:, sl * 8 + 4 + tb: sl * 8 + 5 + tb],
                                    op0=ALU.mult, op1=ALU.add)
                sDV.inc(i); v.wait_ge(sDV.h, sDV.n)
                i = v.tensor_tensor(self.outb(tb)[:, 0:512], self.pb[2 * tb][:, 0:512],
                                    self.xrb(tb)[:, 0:512], op=ALU.add)
                sDV.inc(i)
                i = v.tensor_tensor(self.outb(tb)[:, 512:1024], self.pb[2 * tb + 1][:, 0:512],
                                    self.xrb(tb)[:, 512:1024], op=ALU.add)
                sDV.inc(i)
        self.V(v)
        for i_ in range(8):
            self.bank_free(i_, sDV)

    def m_swiglu(self, sl):
        sPE, sDV, sAC = self.sPE, self.sDV, self.sAC
        ysig_done = self.ysig_done
        self.sw_scr_guard = [0, 0, 0, 0]
        self.sw_sg_ac = [0] * 8
        quarters = [(8, "swg1"), (12, "swg2"), (0, "swu1"), (4, "swu2")]
        for qi, (f0, name) in enumerate(quarters):
            par, wr = self.wload_cached(name + str(sl), 32 + qi)
            for fj in range(4):
                fb = f0 + fj
                bk = (0, 1, 4, 5)[fb % 4]
                scrX = (self.SCRH, self.SCR2, self.SCR3, self.SCR4)[fb % 4]
                is_g = fb >= 8
                def t(tt, fj=fj, bk=bk, par=par, wr=wr):
                    self.bank_wait(tt, bk)
                    self.sWLs[par].wait_at(tt, wr)
                    sDV.wait_at(tt, ysig_done)
                    for cb in range(CB):
                        i = tt.matmul(self.pb[bk][:, 0:512],
                                      self.slot(par)[:, cb * 512 + fj * 128: cb * 512 + (fj + 1) * 128],
                                      self.ys(cb),
                                      start=(cb == 0), stop=(cb == CB - 1))
                    sPE.inc(i)
                self.T(t)
                peval = self.sPE.n
                if is_g:
                    scr_guard = self.sw_scr_guard[fb % 4]
                    def ev(sc, fb=fb, bk=bk, peval=peval, scrX=scrX, scr_guard=scr_guard):
                        sPE.wait_at(sc, peval)
                        sDV.wait_at(sc, scr_guard)
                        i = sc.activation(scrX[:, :], self.pb[bk][:, 0:512], AF.Exp, scale=self.cst(5))
                        sAC.inc(i); sc.wait_ge(sAC.h, sAC.n)
                        i = sc.activation(self.sg(fb - 8), scrX[:, :], AF.Ln, bias=self.cst(4))
                        sAC.inc(i); sc.wait_ge(sAC.h, sAC.n)
                        i = sc.activation(scrX[:, :], self.sg(fb - 8), AF.Exp, scale=self.cst(5))
                        sAC.inc(i)
                    self.A(ev)
                    acv = sAC.n
                    self.sw_sg_ac[fb - 8] = acv
                    def ev_m(v, fb=fb, bk=bk, acv=acv, scrX=scrX):
                        sAC.wait_at(v, acv)
                        i = v.tensor_tensor(self.sg(fb - 8), self.pb[bk][:, 0:512], scrX[:, :], op=ALU.mult)
                        sDV.inc(i)
                    self.V(ev_m)
                    self.bank_free(bk, sAC, sDV)
                    self.sw_scr_guard[fb % 4] = sDV.n
                else:
                    sg_ac = self.sw_sg_ac[fb]
                    def ev(v, fb=fb, bk=bk, peval=peval, sg_ac=sg_ac):
                        sPE.wait_at(v, peval)
                        sAC.wait_at(v, sg_ac)
                        i = v.tensor_tensor(self.ug(fb), self.pb[bk][:, 0:512],
                                            self.sg(fb), op=ALU.mult)
                        sDV.inc(i)
                    self.V(ev)
                    self.bank_free(bk, sDV)
            self.wrelease(par, sPE)

    def m_proj(self, sl):
        sPE, sDV = self.sPE, self.sDV
        ug_done = sDV.n
        parA, wrA = self.wload_cached(f"wo0_{sl}", 36)
        parB, wr = self.wload_cached(f"wo1_{sl}", 37)

        for tb in range(NB):
            b0 = 2 + 4 * (tb % 2)
            def t(tt, tb=tb, b0=b0):
                self.bank_wait(tt, b0)
                self.bank_wait(tt, b0 + 1)
                self.sWLs[parA].wait_at(tt, wrA)
                self.sWLs[parB].wait_at(tt, wr)
                sDV.wait_at(tt, ug_done)
                for dcb in range(CB):
                    lhsT = self.BSC[:, 4096 + dcb * 512 + tb * 128: 4096 + dcb * 512 + (tb + 1) * 128]
                    i = tt.matmul(self.pb[b0][:, 0:512],
                                  lhsT, self.slot(parA)[:, dcb * 512:(dcb + 1) * 512],
                                  start=(dcb == 0), stop=(dcb == CB - 1))
                    i = tt.matmul(self.pb[b0 + 1][:, 0:512],
                                  lhsT, self.slot(parB)[:, dcb * 512:(dcb + 1) * 512],
                                  start=(dcb == 0), stop=(dcb == CB - 1))
                sPE.inc(i)
            self.T(t)
            peval = sPE.n
            def ev(v, tb=tb, b0=b0, peval=peval):
                sPE.wait_at(v, peval)
                i = v.tensor_tensor(self.outb(tb)[:, 0:512], self.outb(tb)[:, 0:512],
                                    self.pb[b0][:, 0:512], op=ALU.add)
                sDV.inc(i)
                i = v.tensor_tensor(self.outb(tb)[:, 512:1024], self.outb(tb)[:, 512:1024],
                                    self.pb[b0 + 1][:, 0:512], op=ALU.add)
                sDV.inc(i)
            self.V(ev)
            self.bank_free(b0, sDV)
            self.bank_free(b0 + 1, sDV)
        self.wrelease(parA, sPE)
        self.wrelease(parB, sPE)
        out_done = sDV.n

        def y(sync):
            sDV.wait_at(sync, out_done)
            self.sST.inc(sync.dma_start(
                out=self.d_y[sl * TS:(sl + 1) * TS, :].rearrange("(b p) d -> p b d", p=128),
                in_=self.AR_XN[:].rearrange("p (b d) -> p b d", d=1024)), 16)
        self.Y(y)

    def program(self):
        self.startup()
        self.build_convert_plan()
        self.convert_all()
        self.cv_total = self.sCV.n
        self.pp_xn_done = 0
        self.pp_var_done = 0
        self.pp_xpose_done = 0
        self.pp_vevac = 0
        self.pp_wte_done = 0
        self.pp_skb = 0
        self.pp_srow_v2 = 0
        self.pp_rbc_pe = 0
        self.pp_gate_pe = 0
        self.pp_gt_pe = 0
        self.pp_wt_pe = 0
        for sl in range(NSL):
            self.pre_x(sl)
            self.slice_barrier()
            self.pre_ln(sl)
            self.pre_gate(sl)
            self.pre_mid(sl)
        self.wr_written_all = self.sDW.n
        for s_ in range(4):
            self.wrelease(s_, self.sPE)
        for name, idx in (("k0_0", 0), ("k1_0", 1), ("v0_0", 2), ("v1_0", 3), ("fc1_0", 4)):
            self.preloaded[name] = self.wload_cached(name, idx)
        self.make_offsets()
        for sl in range(NSL):
            self.prev_store = self.sST.n
            self.slice_barrier(with_store=False)
            self.m_begin(sl)
            self.m_k(sl)
            self.m_v(sl)
            self.m_fc1(sl)
            self._s_row_math()
            self.m_q_jobs(sl)
            self.m_smath_k()
            self.m_experts(sl, [0], first=True)
            self.m_q_tail(sl)
            self.m_experts(sl, [1], first=False)
            self.m_scans(0)
            self.m_experts(sl, [2], first=False)
            self.m_scans(1)
            self.m_experts(sl, [3], first=False)
            self.m_scans(2)
            self.m_experts(sl, [4], first=False)
            self.m_scans(3)
            self.m_q_sig(sl)
            self.m_experts(sl, [5, 6, 7], first=False)
            self.m_comb_evac()
            self.m_y_tail(sl)
            self.m_up(sl)
            self.m_fc2_down(sl)
            self.m_swiglu(sl)
            self.m_proj(sl)

        def y(sync):
            sync.wait_ge(self.sST.h, self.sST.n)
        self.Y(y)


def build_nc():
    nc = bass.Bass("TRN2", target_bir_lowering=False)
    with contextlib.ExitStack() as stack:
        b = Builder(nc, None, stack)
        b.block = stack.enter_context(nc.Block())
        b.program()
    return nc


def kernel(**inputs):
    x = np.asarray(inputs["L0_attn_x"], np.float32)
    wqkvT = np.ascontiguousarray(np.asarray(inputs["L0_attn_w_qkv"], np.float32).T)
    wswT = np.ascontiguousarray(np.asarray(inputs["L0_attn_w_swiglu"], np.float32).T)
    woutT = np.ascontiguousarray(np.asarray(inputs["L0_attn_w_out"], np.float32).T)
    gateT = np.ascontiguousarray(np.asarray(inputs["L0_ffn_gate"], np.float32).T)
    cb = np.asarray(inputs["L0_ffn_corr_bias"], np.float32)
    fc1T = np.ascontiguousarray(np.asarray(inputs["L0_ffn_fc1_w"], np.float32).T)
    fc2T = np.ascontiguousarray(np.asarray(inputs["L0_ffn_fc2_w"], np.float32).T)
    w1T = np.ascontiguousarray(np.transpose(np.asarray(inputs["L0_ffn_w1"], np.float32), (0, 2, 1)))
    w2T = np.ascontiguousarray(np.transpose(np.asarray(inputs["L0_ffn_w2"], np.float32), (0, 2, 1)))
    upT = np.ascontiguousarray(np.asarray(inputs["L0_ffn_up_w"], np.float32).T)
    downT = np.ascontiguousarray(np.asarray(inputs["L0_ffn_down_w"], np.float32).T)

    ident = np.eye(128, dtype=np.float32)
    ones1 = np.ones((1, 128), BF)
    ones128 = np.ones((128, 1), BF)
    consts = np.zeros((128, 8), np.float32)
    consts[:, 0] = EPS_LN
    consts[:, 1] = EPS_F
    consts[:, 2] = EPS_Y
    consts[:, 3] = -0.5
    consts[:, 4] = 1.0
    consts[:, 5] = -1.0
    consts[:, 7] = 1.0 / 1024.0
    cbb = np.tile(cb[None, :], (128, 1)).astype(np.float32)

    in_maps = []
    for cid in range(8):
        bidx, q = cid // 4, cid % 4
        x_own = np.ascontiguousarray(x[bidx, q * TOK:(q + 1) * TOK, :])
        mask = np.zeros((128, NR), np.float32)
        mask[:, bidx * 4: bidx * 4 + q] = 1.0
        in_maps.append({
            "x_own": x_own, "mask": mask,
            "wqkvT": wqkvT, "wswT": wswT, "woutT": woutT,
            "gateT": gateT, "cbb": cbb, "fc1T": fc1T, "fc2T": fc2T,
            "w1T": w1T, "w2T": w2T, "upT": upT, "downT": downT,
            "ident": ident, "ones1": ones1, "ones128": ones128,
            "consts": consts,
        })

    nc = build_nc()
    res = run_bass_kernel_spmd(nc, in_maps, core_ids=list(range(8)))
    outs = [res.results[c]["y"] for c in range(8)]
    full = np.stack([np.concatenate(outs[0:4], axis=0),
                     np.concatenate(outs[4:8], axis=0)], axis=0)
    return full.astype(np.float32)


# version 42
# speedup vs baseline: 1.1302x; 1.1302x over previous
import numpy as np
import ml_dtypes
import contextlib
import concourse.bass as bass
import concourse.mybir as mybir
from concourse.bass_utils import run_bass_kernel_spmd

F32 = mybir.dt.float32
BF16 = mybir.dt.bfloat16
AF = mybir.ActivationFunctionType
ALU = mybir.AluOpType
AX = mybir.AxisListType

D = 1024
TOK = 2048
TS = 512
NB = TS // 128
NSL = TOK // TS
NR = 8
CB = D // 128

EPS_LN = 1e-5
EPS_F = 1.1920929e-07
EPS_Y = 1e-6
DEFER_W2 = True

BF = ml_dtypes.bfloat16


class Sem:
    def __init__(self, stack, nc, name):
        self.h = stack.enter_context(nc.semaphore(name))
        self.n = 0

    def inc(self, inst, k=1):
        inst.then_inc(self.h, k)
        self.n += k
        return inst

    def wait_all(self, eng):
        if self.n:
            eng.wait_ge(self.h, self.n)

    def wait_at(self, eng, val):
        if val:
            eng.wait_ge(self.h, val)


class Builder:
    def __init__(self, nc, block, stack):
        self.nc = nc
        self.block = block
        s = stack.enter_context

        self.d_x = nc.dram_tensor("x_own", [TOK, D], F32, kind="ExternalInput").ap()
        self.d_mask = nc.dram_tensor("mask", [128, NR], F32, kind="ExternalInput").ap()
        self.d_tin = nc.dram_tensor("tot_in", [128, 16], F32, kind="Internal").ap()
        self.d_tout = nc.dram_tensor("tot_out", [NR * 128, 16], F32, kind="Internal").ap()
        self.d_wqkvT = nc.dram_tensor("wqkvT", [D, 3 * D], F32, kind="ExternalInput").ap()
        self.d_wswT = nc.dram_tensor("wswT", [D, 2 * D], F32, kind="ExternalInput").ap()
        self.d_woutT = nc.dram_tensor("woutT", [D, D], F32, kind="ExternalInput").ap()
        self.d_gateT = nc.dram_tensor("gateT", [D, 8], F32, kind="ExternalInput").ap()
        self.d_cb = nc.dram_tensor("cbb", [128, 8], F32, kind="ExternalInput").ap()
        self.d_fc1T = nc.dram_tensor("fc1T", [D, 512], F32, kind="ExternalInput").ap()
        self.d_fc2T = nc.dram_tensor("fc2T", [512, D], F32, kind="ExternalInput").ap()
        self.d_w1T = nc.dram_tensor("w1T", [8, 512, 512], F32, kind="ExternalInput").ap()
        self.d_w2T = nc.dram_tensor("w2T", [8, 512, 512], F32, kind="ExternalInput").ap()
        self.d_upT = nc.dram_tensor("upT", [D, 4 * D], F32, kind="ExternalInput").ap()
        self.d_downT = nc.dram_tensor("downT", [4 * D, D], F32, kind="ExternalInput").ap()
        self.d_ident = nc.dram_tensor("ident", [128, 128], F32, kind="ExternalInput").ap()
        self.d_ones1 = nc.dram_tensor("ones1", [1, 128], BF16, kind="ExternalInput").ap()
        self.d_ones128 = nc.dram_tensor("ones128", [128, 1], BF16, kind="ExternalInput").ap()
        self.d_consts = nc.dram_tensor("consts", [128, 8], F32, kind="ExternalInput").ap()
        self.d_y = nc.dram_tensor("y", [TOK, D], F32, kind="ExternalOutput").ap()
        self.d_wr = nc.dram_tensor("wr_scratch", [8, TOK], BF16, kind="Internal").ap()
        self.d_wbf = nc.dram_tensor("wbf", [76, 128, 2048], BF16, kind="Internal").ap()

        self.HI = s(nc.sbuf_tensor("HI", [128, NSL * 4096], BF16))
        self.AR_A = s(nc.sbuf_tensor("AR_A", [128, 4096], F32))
        self.AR_B = s(nc.sbuf_tensor("AR_B", [128, 4096], F32))
        self.AR_C = s(nc.sbuf_tensor("AR_C", [128, 4096], F32))
        self.AR_XN = s(nc.sbuf_tensor("AR_XN", [128, 4096], F32))
        self.BSC = s(nc.sbuf_tensor("BSC", [128, 8192], BF16))
        self.B2 = s(nc.sbuf_tensor("B2", [128, 8192], BF16))
        self.WBuf = s(nc.sbuf_tensor("WBuf", [128, 20480], BF16))
        self.SCR2 = s(nc.sbuf_tensor("SCR2", [128, 512], F32))
        self.SCRH = s(nc.sbuf_tensor("SCRH", [128, 512], F32))
        self.SCR3 = s(nc.sbuf_tensor("SCR3", [128, 512], F32))
        self.SCR4 = s(nc.sbuf_tensor("SCR4", [128, 512], F32))
        self.W8H = s(nc.sbuf_tensor("W8H", [8, 512], BF16))
        self.G8 = self.SCR3[0:8, :]
        self.ST1 = self.SCR4[:, 64:96]
        self.MASK = self.SCR4[:, 96:104]
        self.WBC2 = s(nc.sbuf_tensor("WBC2", [128, 1024], F32))
        self.ROWS = s(nc.sbuf_tensor("ROWS", [1, 2048], F32))
        self.ROWSB = s(nc.sbuf_tensor("ROWSB", [1, 2048], BF16))
        self.R_ALL = s(nc.sbuf_tensor("R_ALL", [1, NSL * 512], F32))
        self.SOUT = s(nc.sbuf_tensor("SOUT", [128, NSL * 8], F32))
        self.WR16 = s(nc.sbuf_tensor("WR16", [1, 1024], BF16))
        self.GT = s(nc.sbuf_tensor("GT", [128, 64], F32))
        self.GH = s(nc.sbuf_tensor("GH", [128, 64], BF16))
        self.GL = s(nc.sbuf_tensor("GL", [128, 64], BF16))
        self.CBB = s(nc.sbuf_tensor("CBB", [128, 8], F32))
        self.TOTS = s(nc.sbuf_tensor("TOTS", [128, NR * 16], F32))
        self.IDENT = s(nc.sbuf_tensor("IDENT", [128, 128], F32))
        self.ONES1 = s(nc.sbuf_tensor("ONES1", [1, 128], BF16))
        self.ONES128 = s(nc.sbuf_tensor("ONES128", [128, 1], BF16))
        self.CONSTS = s(nc.sbuf_tensor("CONSTS", [128, 8], F32))
        self.CARRY = s(nc.sbuf_tensor("CARRY", [128, 16], F32))
        self.SLOT = s(nc.sbuf_tensor("SLOT", [128, 16], F32))
        self.OFFS = s(nc.sbuf_tensor("OFFS", [128, 16], F32))
        self.COLT = s(nc.sbuf_tensor("COLT", [128, 16], F32))
        self.R_s = s(nc.sbuf_tensor("R_s", [128, 32], F32))
        self.R_wte = s(nc.sbuf_tensor("R_wte", [128, 32], F32))
        self.R_t = [s(nc.sbuf_tensor(f"R_t{i}", [128, 32], F32)) for i in range(5)]
        self.R_m = s(nc.sbuf_tensor("R_m", [128, 32], F32))

        self.pb = [s(nc.psum_tensor(f"pb{i}", [128, 512], F32)) for i in range(8)]
        self.brel = [[] for _ in range(8)]

        self.sLD = Sem(stack, nc, "sLD")
        self.sX = Sem(stack, nc, "sX")
        self.sDW = Sem(stack, nc, "sDW")
        self.sR = [Sem(stack, nc, "sR0"), Sem(stack, nc, "sR1")]
        self.sCC = Sem(stack, nc, "sCC")
        self.sCL = Sem(stack, nc, "sCL")
        self.sPL = Sem(stack, nc, "sPL")
        self.sCV = Sem(stack, nc, "sCV")
        self.sWLs = [Sem(stack, nc, f"sWL{i}") for i in range(5)]
        self.sPE = Sem(stack, nc, "sPE")
        self.sDV = Sem(stack, nc, "sDV")
        self.sAC = Sem(stack, nc, "sAC")
        self.sST = Sem(stack, nc, "sST")

        self.wready = {}
        self.wslotrel = {}
        self.wparity = 0
        self.wr_written = [0] * NSL
        self.preloaded = {}
        self.ex_pend = []
        self.ex_g2_prev = 0

    def T(self, f): self.block.tensor(f)
    def V(self, f): self.block.vector(f)
    def A(self, f): self.block.scalar(f)
    def G(self, f): self.block.gpsimd(f)
    def Y(self, f): self.block.sync(f)

    def hi(self, sl, cb): return self.HI[:, sl * 4096 + cb * 512: sl * 4096 + (cb + 1) * 512]
    def lo(self, cb):   return self.BSC[:, cb * 512:(cb + 1) * 512]
    def ys(self, cb):   return self.BSC[:, cb * 512:(cb + 1) * 512]
    def ug(self, cb):   return self.BSC[:, 4096 + cb * 512: 4096 + (cb + 1) * 512]
    def k2(self, j):    return self.B2[:, j * 512:(j + 1) * 512]
    def combv(self, lb): return self.B2[:, lb * 512:(lb + 1) * 512]
    def lat(self, lb):  return self.B2[:, 4096 + lb * 512: 4096 + (lb + 1) * 512]
    def q2v(self, j):   return self.B2[:, 4096 + j * 512: 4096 + (j + 1) * 512]
    def g2(self, ib):   return self.B2[:, 6144 + ib * 512: 6144 + (ib + 1) * 512]
    def A_(self, j):    return self.AR_A[:, j * 512:(j + 1) * 512]
    def B_(self, j):    return self.AR_B[:, j * 512:(j + 1) * 512]
    def C_(self, j):    return self.AR_C[:, j * 512:(j + 1) * 512]
    def XN_(self, j):   return self.AR_XN[:, j * 512:(j + 1) * 512]
    def sg(self, j):    return self.AR_A[:, j * 512:(j + 1) * 512]
    def xb(self, b):    return self.AR_C[:, b * 1024:(b + 1) * 1024]
    def xnb(self, b):   return self.AR_XN[:, b * 1024:(b + 1) * 1024]
    def outb(self, b):  return self.AR_XN[:, b * 1024:(b + 1) * 1024]
    def xrb(self, b):   return self.AR_C[:, b * 1024:(b + 1) * 1024]
    def hsq(self, sb):
        ar = self.AR_A if sb < 16 else self.AR_B
        j = sb % 16
        return ar[:].bitcast(BF16)[:, j * 512:(j + 1) * 512]
    def kv_blk(self, cb, j):
        return self.WBuf[:, cb * 2048 + j * 128: cb * 2048 + (j + 1) * 128]
    def slot_blk(self, par, cb, jj):
        return self.slot(par)[:, cb * 512 + jj * 128: cb * 512 + (jj + 1) * 128]
    def wbc2(self, e):  return self.WBC2[:, (e % 2) * 512:((e % 2) + 1) * 512]
    def cst(self, i):   return self.CONSTS[:, i:i + 1]
    def slot(self, par): return self.WBuf[:, par * 4096:(par + 1) * 4096]

    def bank_wait(self, eng, i):
        for tr, val in self.brel[i]:
            tr.wait_at(eng, val)
        self.brel[i] = []

    def bank_free(self, i, *trackers):
        self.brel[i] = [(tr, tr.n) for tr in trackers]

    def wload(self, name, dst, src_ap, slot_key=None):
        keys = slot_key if isinstance(slot_key, (list, tuple)) else (
            [] if slot_key is None else [slot_key])
        def g(g):
            for k in keys:
                rel = self.wslotrel.pop(k, None)
                if rel is not None:
                    tr, val = rel
                    tr.wait_at(g, val)
            self.sWLs[0].inc(g.dma_start(out=dst, in_=src_ap), 16)
        self.G(g)
        self.wready[name] = self.sWLs[0].n

    def wload_slot(self, name, src_cols):
        par = self.wparity; self.wparity = (self.wparity + 1) % 5
        self.wload(name, self.slot(par).rearrange("p (cb e) -> p cb e", e=512),
                   src_cols.rearrange("(cb p) e -> p cb e", p=128), slot_key=par)
        return par, self.wready[name]

    def wrelease(self, slot_key, tracker):
        self.wslotrel[slot_key] = (tracker, tracker.n)

    def build_convert_plan(self):
        def halves(src2d):
            r = src2d.shape[0]
            h = r // 2
            return [src2d[0:h].rearrange("(o p) i -> p o i", p=128),
                    src2d[h:r].rearrange("(o p) i -> p o i", p=128)]
        loads = []
        loads.append(halves(self.d_wqkvT[:, 1024:1536]))
        loads.append(halves(self.d_wqkvT[:, 1536:2048]))
        loads.append(halves(self.d_wqkvT[:, 2048:2560]))
        loads.append(halves(self.d_wqkvT[:, 2560:3072]))
        loads.append(halves(self.d_fc1T[:, 0:512]))
        for e in range(8):
            loads.append([self.d_w1T[e].rearrange("(o p) i -> p o i", p=128),
                          self.d_w2T[e].rearrange("(o p) i -> p o i", p=128)])
        loads.append(halves(self.d_wqkvT[:, 0:512]))
        loads.append(halves(self.d_wqkvT[:, 512:1024]))
        for p in range(8):
            loads.append(halves(self.d_upT[:, p * 512:(p + 1) * 512]))
        loads.append(halves(self.d_fc2T))
        for p in range(8):
            loads.append(halves(self.d_downT[p * 512:(p + 1) * 512, :]))
        for f0 in (8, 12, 0, 4):
            loads.append(halves(self.d_wswT[:, f0 * 128:(f0 + 4) * 128]))
        loads.append(halves(self.d_woutT[:, 0:512]))
        loads.append(halves(self.d_woutT[:, 512:1024]))
        self.cv_half = [h for pair in loads for h in pair]
        self.cv_load_val = [0] * 76
        self.cv_store_val = [0] * 76
        self.cv_next = 0

    def convert_all(self):
        def g(g):
            for i in range(76):
                src = self.cv_half[i]
                isz = src.shape[2]
                dst = self.d_wbf[i].rearrange("p (o i) -> p o i", i=isz)
                self.sCV.inc(g.dma_start(out=dst, in_=src), 16)
                self.cv_store_val[i] = self.sCV.n
        self.G(g)

    def wload_cached(self, name, idx):
        if name in self.preloaded:
            return self.preloaded.pop(name)
        par = self.wparity; self.wparity = (self.wparity + 1) % 5
        def g(g):
            rel = self.wslotrel.pop(par, None)
            if rel is not None:
                tr, val = rel
                tr.wait_at(g, val)
            g.wait_ge(self.sCV.h, self.cv_total)
            self.sWLs[par].inc(g.dma_start(
                out=self.slot(par).rearrange("p (t c) -> p t c", c=2048),
                in_=self.d_wbf[2 * idx:2 * idx + 2].rearrange("t p c -> p t c")), 16)
        self.G(g)
        self.wready[name] = self.sWLs[par].n
        return par, self.wready[name]

    def slice_barrier(self, with_store=True):
        trs = (self.sPE, self.sDV, self.sAC, self.sST, self.sLD) if with_store else (
            self.sPE, self.sDV, self.sAC, self.sLD)
        snaps = [(tr, tr.n) for tr in trs]
        def mk(engf):
            def f(e):
                for tr, val in snaps:
                    tr.wait_at(e, val)
            engf(f)
        mk(self.T); mk(self.V); mk(self.A); mk(self.Y)

    def startup(self):
        sLD, sDV = self.sLD, self.sDV

        def y(sync):
            sLD.inc(sync.dma_start(out=self.IDENT[:], in_=self.d_ident), 16)
            sLD.inc(sync.dma_start(out=self.ONES1[:], in_=self.d_ones1), 16)
            sLD.inc(sync.dma_start(out=self.ONES128[:], in_=self.d_ones128), 16)
            sLD.inc(sync.dma_start(out=self.CONSTS[:], in_=self.d_consts), 16)
            sLD.inc(sync.dma_start(out=self.CBB[:], in_=self.d_cb), 16)
            sLD.inc(sync.dma_start(out=self.MASK[:], in_=self.d_mask), 16)
            sLD.inc(sync.dma_start(
                out=self.GT[:].rearrange("p (cb e) -> p cb e", e=8),
                in_=self.d_gateT.rearrange("(cb p) e -> p cb e", p=128)), 16)
        self.Y(y)

        def g(g):
            self.sWLs[0].inc(g.dma_start(
                out=self.WBuf[:, 0:16384].rearrange("p (cb e) -> p cb e", e=2048),
                in_=self.d_wqkvT[:, 1024:3072].rearrange("(cb p) e -> p cb e", p=128)), 16)
        self.G(g)
        self.wready["KV"] = self.sWLs[0].n

        def v(v):
            sLD.wait_all(v)
            v.memset(self.SLOT[:], 0.0)
            v.memset(self.OFFS[:], 0.0)
            i = v.tensor_copy(self.GH[:], self.GT[:])
            sDV.inc(i); v.wait_ge(sDV.h, sDV.n)
            i = v.tensor_tensor(self.GL[:], self.GT[:], self.GH[:], op=ALU.subtract)
            sDV.inc(i)
        self.V(v)

    def pre_x(self, sl):
        sDV = self.sDV

        def y(sync):
            sDV.wait_at(sync, self.pp_xn_done)
            self.sX.inc(sync.dma_start(
                out=self.AR_C[:].rearrange("p (b d) -> p b d", d=1024),
                in_=self.d_x[sl * TS:(sl + 1) * TS, :].rearrange("(b p) d -> p b d", p=128)), 16)
        self.Y(y)
        self.pp_x_ready = self.sX.n

    def pre_ln(self, sl):
        sLD, sPE, sDV, sAC = self.sLD, self.sPE, self.sDV, self.sAC
        x_ready = self.pp_x_ready

        def v(v):
            self.sX.wait_at(v, x_ready)
            for b in range(NB):
                i = v.tensor_reduce(self.ST1[:, b:b + 1], self.xb(b), axis=AX.X, op=ALU.add)
            sDV.inc(i)
        self.V(v)

        def a(sc):
            self.sX.wait_at(sc, x_ready)
            sDV.wait_at(sc, self.pp_var_done)
            dmp = self.B2[:].bitcast(F32)
            for b in range(NB):
                i = sc.activation(dmp[:, b * 1024:(b + 1) * 1024], self.xb(b), AF.Square,
                                  accum_out=self.ST1[:, 4 + b:5 + b])
            sAC.inc(i)
        self.A(a)

        def v2(v):
            sDV.wait_all(v)
            sAC.wait_all(v)
            i = v.tensor_scalar(self.ST1[:, 8:12], self.ST1[:, 0:4], 1.0 / 1024.0, None, op0=ALU.mult)
            sDV.inc(i); v.wait_ge(sDV.h, sDV.n)
            i = v.tensor_scalar(self.ST1[:, 20:24], self.ST1[:, 4:8], 1.0 / 1024.0, None, op0=ALU.mult)
            sDV.inc(i); v.wait_ge(sDV.h, sDV.n)
            i = v.tensor_tensor(self.ST1[:, 24:28], self.ST1[:, 8:12], self.ST1[:, 8:12], op=ALU.mult)
            sDV.inc(i); v.wait_ge(sDV.h, sDV.n)
            i = v.tensor_tensor(self.ST1[:, 4:8], self.ST1[:, 20:24], self.ST1[:, 24:28], op=ALU.subtract)
            sDV.inc(i)
        self.V(v2)
        self.pp_var_done = sDV.n

        def a2(sc):
            sDV.wait_all(sc)
            i = sc.activation(self.ST1[:, 20:24], self.ST1[:, 4:8], AF.Ln, bias=self.cst(0))
            sAC.inc(i); sc.wait_ge(sAC.h, sAC.n)
            i = sc.activation(self.ST1[:, 12:16], self.ST1[:, 20:24], AF.Exp, scale=self.cst(3))
            sAC.inc(i)
        self.A(a2)

        def v3(v):
            sAC.wait_all(v)
            i = v.tensor_tensor(self.ST1[:, 24:28], self.ST1[:, 4:8], self.ST1[:, 12:16], op=ALU.mult)
            sDV.inc(i); v.wait_ge(sDV.h, sDV.n)
            i = v.tensor_tensor(self.ST1[:, 24:28], self.ST1[:, 24:28], self.ST1[:, 12:16], op=ALU.mult)
            sDV.inc(i)
        self.V(v3)

        def a3(sc):
            sDV.wait_all(sc)
            sPE.wait_at(sc, self.pp_xpose_done)
            i = sc.activation(self.ST1[:, 20:24], self.ST1[:, 24:28], AF.Ln, bias=self.cst(1))
            sAC.inc(i); sc.wait_ge(sAC.h, sAC.n)
            i = sc.activation(self.ST1[:, 16:20], self.ST1[:, 20:24], AF.Exp, scale=self.cst(3))
            sAC.inc(i)
        self.A(a3)

        def v4(v):
            sAC.wait_all(v)
            i = v.tensor_scalar(self.SOUT[:, sl * 8: sl * 8 + 4], self.ST1[:, 12:16],
                                1.0, None, op0=ALU.add)
            sDV.inc(i); v.wait_ge(sDV.h, sDV.n)
            i = v.tensor_tensor(self.ST1[:, 28:32], self.ST1[:, 8:12], self.ST1[:, 12:16], op=ALU.mult)
            sDV.inc(i); v.wait_ge(sDV.h, sDV.n)
            i = v.tensor_scalar(self.SOUT[:, sl * 8 + 4: sl * 8 + 8], self.ST1[:, 28:32],
                                -1.0, None, op0=ALU.mult)
            sDV.inc(i)
        self.V(v4)

        def v4b(v):
            sPE.wait_at(v, self.pp_xpose_done)
            for b in range(NB):
                i = v.tensor_scalar(self.xnb(b), self.xb(b),
                                    self.ST1[:, 8 + b:9 + b], self.ST1[:, 12 + b:13 + b],
                                    op0=ALU.subtract, op1=ALU.mult)
            sDV.inc(i)
        self.V(v4b)
        xn_done = sDV.n
        self.pp_xn_done = xn_done

        def t(tt):
            self.bank_wait(tt, 6)
            sDV.wait_at(tt, xn_done)
            for b in range(NB):
                i = tt.transpose(self.pb[6][0:1, b * 128:(b + 1) * 128],
                                 self.ST1[:, 16 + b:17 + b], self.IDENT[:])
            sPE.inc(i)
        self.T(t)
        tr_done = sPE.n

        def a4(sc):
            sPE.wait_at(sc, tr_done)
            sDV.wait_at(sc, self.pp_srow_v2)
            i = sc.activation(self.ROWS[0:1, 0:512], self.pb[6][0:1, 0:512], AF.Copy)
            sAC.inc(i)
        self.A(a4)
        self.bank_free(6, sAC)

        def v5(v):
            sPE.wait_at(v, self.pp_rbc_pe)
            sAC.wait_all(v)
            i = v.tensor_copy(self.R_ALL[0:1, sl * 512:(sl + 1) * 512], self.ROWS[0:1, 0:512])
            sDV.inc(i); v.wait_ge(sDV.h, sDV.n)
            i = v.tensor_copy(self.ROWSB[0:1, 0:512], self.ROWS[0:1, 0:512])
            sDV.inc(i); v.wait_ge(sDV.h, sDV.n)
            i = v.tensor_tensor(self.ROWSB[0:1, 512:1024], self.ROWS[0:1, 0:512],
                                self.ROWSB[0:1, 0:512], op=ALU.subtract)
            sDV.inc(i)
        self.V(v5)

        self.pp_hilo = []
        for cb in range(CB):
            bk = 6 + (cb % 2)
            def t2(tt, cb=cb, bk=bk):
                self.bank_wait(tt, bk)
                sDV.wait_at(tt, xn_done)
                for b in range(NB):
                    i = tt.transpose(self.pb[bk][:, b * 128:(b + 1) * 128],
                                     self.AR_XN[:, b * 1024 + cb * 128: b * 1024 + (cb + 1) * 128],
                                     self.IDENT[:])
                sPE.inc(i)
            self.T(t2)
            pe_done = sPE.n

            def e2(v, sl=sl, cb=cb, bk=bk, pe_done=pe_done):
                sPE.wait_at(v, pe_done)
                sPE.wait_at(v, self.pp_gate_pe)
                i = v.tensor_copy(self.hi(sl, cb), self.pb[bk][:, 0:512])
                sDV.inc(i); v.wait_ge(sDV.h, sDV.n)
                i = v.tensor_tensor(self.lo(cb), self.pb[bk][:, 0:512], self.hi(sl, cb),
                                    op=ALU.subtract)
                sDV.inc(i)
            self.V(e2)
            self.bank_free(bk, sDV)
            self.pp_hilo.append(sDV.n)
        self.pp_xpose_done = self.sPE.n

    def pre_gate(self, sl):
        sPE, sDV, sAC, sLD = self.sPE, self.sDV, self.sAC, self.sLD
        hi_done = sDV.n

        def t(tt):
            self.bank_wait(tt, 4)
            n = 0
            prods = [(self.GH, lambda cb: self.hi(sl, cb)), (self.GH, self.lo),
                     (self.GL, lambda cb: self.hi(sl, cb)), (self.GL, self.lo)]
            total = len(prods) * CB
            for cb in range(CB):
                sDV.wait_at(tt, self.pp_hilo[cb])
                for gw, xf in prods:
                    i = tt.matmul(self.pb[4][0:8, 0:512],
                                  gw[:, cb * 8:(cb + 1) * 8], xf(cb),
                                  start=(n == 0), stop=(n == total - 1))
                    n += 1
            sPE.inc(i)
        self.T(t)
        g_done = sPE.n
        self.pp_gate_pe = g_done

        def a(sc):
            sPE.wait_at(sc, g_done)
            sPE.wait_at(sc, self.pp_gt_pe)
            i = sc.activation(self.G8[:, :], self.pb[4][0:8, 0:512], AF.Copy)
            sAC.inc(i)
        self.A(a)
        self.bank_free(4, sAC)
        g8_done = sAC.n

        def t2(tt):
            self.bank_wait(tt, 5)
            sAC.wait_at(tt, g8_done)
            for b in range(NB):
                i = tt.transpose(self.pb[5][:, b * 8:(b + 1) * 8],
                                 self.G8[:, b * 128:(b + 1) * 128],
                                 self.IDENT[0:8, 0:8])
            sPE.inc(i)
        self.T(t2)
        gt_done = sPE.n
        self.pp_gt_pe = gt_done

        def a2(sc):
            sPE.wait_at(sc, gt_done)
            sDV.wait_at(sc, self.pp_wte_done)
            sPE.wait_at(sc, self.pp_wt_pe)
            i = sc.activation(self.R_s[:, 0:32], self.pb[5][:, 0:32], AF.Exp, scale=self.cst(5))
            sAC.inc(i); sc.wait_ge(sAC.h, sAC.n)
            i = sc.activation(self.R_wte[:, 0:32], self.R_s[:, 0:32], AF.Ln, bias=self.cst(4))
            sAC.inc(i); sc.wait_ge(sAC.h, sAC.n)
            i = sc.activation(self.R_s[:, 0:32], self.R_wte[:, 0:32], AF.Exp, scale=self.cst(5))
            sAC.inc(i)
        self.A(a2)
        self.bank_free(5, sAC)

        def v(v):
            sAC.wait_all(v)
            B4 = range(NB)
            def step(fn):
                for b in B4:
                    i = fn(b)
                self.sDV.inc(i)
                v.wait_ge(self.sDV.h, self.sDV.n)
            s_ = lambda b: self.R_s[:, b * 8:(b + 1) * 8]
            t0 = lambda b: self.R_t[0][:, b * 8:(b + 1) * 8]
            t1 = lambda b: self.R_t[1][:, b * 8:(b + 1) * 8]
            t2_ = lambda b: self.R_t[2][:, b * 8:(b + 1) * 8]
            t3 = lambda b: self.R_t[3][:, b * 8:(b + 1) * 8]
            t4 = lambda b: self.R_t[4][:, b * 8:(b + 1) * 8]
            wte = lambda b: self.R_wte[:, b * 8:(b + 1) * 8]
            m = lambda b, i_: self.R_m[:, b * 8 + i_: b * 8 + i_ + 1]
            step(lambda b: v.tensor_tensor(t0(b), s_(b), self.CBB[:, 0:8], op=ALU.add))
            step(lambda b: v.tensor_reduce(m(b, 0), t0(b)[:, 0:4], axis=AX.X, op=ALU.max))
            step(lambda b: v.tensor_scalar(t1(b)[:, 0:4], t0(b)[:, 0:4], m(b, 0), None, op0=ALU.is_equal))
            step(lambda b: v.tensor_scalar(t1(b)[:, 0:4], t1(b)[:, 0:4], -1e30, None, op0=ALU.mult))
            step(lambda b: v.tensor_tensor(t2_(b)[:, 0:4], t0(b)[:, 0:4], t1(b)[:, 0:4], op=ALU.add))
            step(lambda b: v.tensor_reduce(m(b, 1), t2_(b)[:, 0:4], axis=AX.X, op=ALU.max))
            step(lambda b: v.tensor_tensor(m(b, 2), m(b, 0), m(b, 1), op=ALU.add))
            step(lambda b: v.tensor_reduce(m(b, 3), t0(b)[:, 4:8], axis=AX.X, op=ALU.max))
            step(lambda b: v.tensor_scalar(t1(b)[:, 4:8], t0(b)[:, 4:8], m(b, 3), None, op0=ALU.is_equal))
            step(lambda b: v.tensor_scalar(t1(b)[:, 4:8], t1(b)[:, 4:8], -1e30, None, op0=ALU.mult))
            step(lambda b: v.tensor_tensor(t2_(b)[:, 4:8], t0(b)[:, 4:8], t1(b)[:, 4:8], op=ALU.add))
            step(lambda b: v.tensor_reduce(m(b, 4), t2_(b)[:, 4:8], axis=AX.X, op=ALU.max))
            step(lambda b: v.tensor_tensor(m(b, 5), m(b, 3), m(b, 4), op=ALU.add))
            step(lambda b: v.tensor_tensor(m(b, 6), m(b, 2), m(b, 5), op=ALU.is_ge))
            step(lambda b: v.tensor_scalar(m(b, 7), m(b, 6), -1.0, 1.0, op0=ALU.mult, op1=ALU.add))
            step(lambda b: v.tensor_scalar(t3(b)[:, 0:4], t0(b)[:, 0:4], m(b, 6), None, op0=ALU.mult))
            step(lambda b: v.tensor_scalar(t3(b)[:, 4:8], t0(b)[:, 4:8], m(b, 7), None, op0=ALU.mult))
            step(lambda b: v.tensor_reduce(m(b, 0), t3(b), axis=AX.X, op=ALU.max))
            step(lambda b: v.tensor_scalar(t1(b), t3(b), m(b, 0), None, op0=ALU.is_equal))
            step(lambda b: v.tensor_scalar(t4(b), t1(b), -1e30, None, op0=ALU.mult))
            step(lambda b: v.tensor_tensor(t4(b), t3(b), t4(b), op=ALU.add))
            step(lambda b: v.tensor_reduce(m(b, 1), t4(b), axis=AX.X, op=ALU.max))
            step(lambda b: v.tensor_scalar(t2_(b), t4(b), m(b, 1), None, op0=ALU.is_equal))
            step(lambda b: v.tensor_tensor(t1(b), t1(b), t2_(b), op=ALU.add))
            step(lambda b: v.tensor_tensor(wte(b), s_(b), t1(b), op=ALU.mult))
            step(lambda b: v.tensor_reduce(m(b, 2), wte(b), axis=AX.X, op=ALU.add))
            step(lambda b: v.tensor_scalar(m(b, 2), m(b, 2), 1e-20, None, op0=ALU.add))
            step(lambda b: v.reciprocal(m(b, 3), m(b, 2)))
            step(lambda b: v.tensor_scalar(wte(b), wte(b), m(b, 3), None, op0=ALU.mult))
        self.V(v)
        wte_done = self.sDV.n
        self.pp_wte_done = wte_done

        def t3_(tt):
            self.bank_wait(tt, 6)
            sDV.wait_at(tt, wte_done)
            for b in range(NB):
                i = tt.transpose(self.pb[6][0:8, b * 128:(b + 1) * 128],
                                 self.R_wte[:, b * 8:(b + 1) * 8],
                                 self.IDENT[:])
            sPE.inc(i)
        self.T(t3_)
        wt_done = sPE.n
        self.pp_wt_pe = wt_done

        def a3(sc):
            sPE.wait_at(sc, wt_done)
            if sl > 0:
                self.sDW.wait_at(sc, self.wr_written[sl - 1])
            i = sc.activation(self.W8H[:, :], self.pb[6][0:8, 0:512], AF.Copy)
            sAC.inc(i)
        self.A(a3)
        self.bank_free(6, sAC)
        w8_done = sAC.n

        def y(sync):
            sAC.wait_at(sync, w8_done)
            self.sDW.inc(sync.dma_start(out=self.d_wr[0:8, sl * 512:(sl + 1) * 512], in_=self.W8H[:, :]), 16)
        self.Y(y)
        self.wr_written[sl] = self.sDW.n

    def pre_mid(self, sl):
        sPE, sDV, sAC = self.sPE, self.sDV, self.sAC

        for j in range(CB):
            bk = j % 4
            def t(tt, sl=sl, j=j, bk=bk):
                self.bank_wait(tt, bk)
                self.sWLs[0].wait_at(tt, self.wready["KV"])
                for cb in range(CB):
                    i = tt.matmul(self.pb[bk][:, 0:512],
                                  self.kv_blk(cb, j), self.hi(sl, cb),
                                  start=(cb == 0), stop=(cb == CB - 1))
                sPE.inc(i)
            self.T(t)
            peval = sPE.n
            def ev(sc, j=j, bk=bk, peval=peval):
                sPE.wait_at(sc, peval)
                i = sc.activation(self.k2(j), self.pb[bk][:, 0:512], AF.Square)
                sAC.inc(i)
            self.A(ev)
            def ev2(v, j=j, bk=bk, peval=peval):
                sPE.wait_at(v, peval)
                i = v.tensor_copy(self.A_(j), self.pb[bk][:, 0:512])
                sDV.inc(i)
            self.V(ev2)
            self.bank_free(bk, sAC, sDV)
        k2_done = sAC.n

        def t(tt):
            self.bank_wait(tt, 4)
            sAC.wait_at(tt, k2_done)
            for j in range(CB):
                i = tt.matmul(self.pb[4][0:1, 0:512], self.ONES128[:, :], self.k2(j),
                              start=(j == 0), stop=(j == CB - 1))
            sPE.inc(i)
        self.T(t)
        ss_done = sPE.n

        def a(sc):
            sPE.wait_at(sc, ss_done)
            sDV.wait_at(sc, self.pp_srow_v2)
            i = sc.activation(self.ROWS[0:1, 512:1024], self.pb[4][0:1, 0:512], AF.Copy)
            sAC.inc(i)
        self.A(a)
        self.bank_free(4, sAC)

        self._s_row_math(pe_guard=self.pp_skb)
        self.pp_srow_v2 = self.sDV.n

        def t2(tt):
            self.bank_wait(tt, 5)
            sDV.wait_at(tt, self.sDV.n)
            i = tt.matmul(self.pb[5][:, 0:512], self.ONES1[:, :],
                          self.ROWSB[0:1, 1024:1536], start=True, stop=False)
            i = tt.matmul(self.pb[5][:, 0:512], self.ONES1[:, :],
                          self.ROWSB[0:1, 1536:2048], start=False, stop=True)
            sPE.inc(i)
        self.T(t2)
        skb_done = sPE.n
        self.pp_skb = skb_done

        def v(v):
            sPE.wait_at(v, skb_done)
            for j in range(CB):
                i = v.tensor_tensor(self.A_(j), self.A_(j), self.pb[5][:, 0:512], op=ALU.mult)
            sDV.inc(i)
        self.V(v)
        ks_done = sDV.n
        self.bank_free(5, sDV)

        def a2(sc):
            sDV.wait_at(sc, ks_done)
            for j in range(CB):
                i = sc.activation(self.B_(j), self.A_(j), AF.Exp,
                                  accum_out=self.COLT[:, j:j + 1])
            sAC.inc(i)
        self.A(a2)
        w_done = sAC.n

        def t3(tt):
            self.bank_wait(tt, 5)
            i = tt.matmul(self.pb[5][:, 0:512], self.ONES1[:, :],
                          self.ROWSB[0:1, 0:512], start=True, stop=False)
            i = tt.matmul(self.pb[5][:, 0:512], self.ONES1[:, :],
                          self.ROWSB[0:1, 512:1024], start=False, stop=True)
            sPE.inc(i)
        self.T(t3)
        rb_done = sPE.n
        self.pp_rbc_pe = rb_done

        def a3(sc):
            sPE.wait_at(sc, rb_done)
            sDV.wait_at(sc, self.pp_vevac)
            i = sc.activation(self.SCR2[:, :], self.pb[5][:, 0:512], AF.Copy)
            sAC.inc(i)
        self.A(a3)
        self.bank_free(5, sAC)
        rbc_done = sAC.n

        for j in range(CB):
            bk = j % 4
            def t4(tt, sl=sl, j=j, bk=bk):
                self.bank_wait(tt, bk)
                for cb in range(CB):
                    i = tt.matmul(self.pb[bk][:, 0:512],
                                  self.kv_blk(cb, 8 + j), self.hi(sl, cb),
                                  start=(cb == 0), stop=(cb == CB - 1))
                sPE.inc(i)
            self.T(t4)
            peval = sPE.n
            def ev(v, j=j, bk=bk, peval=peval):
                sPE.wait_at(v, peval)
                sAC.wait_at(v, max(rbc_done, w_done))
                i = v.tensor_tensor(self.A_(j), self.pb[bk][:, 0:512], self.SCR2[:, :], op=ALU.mult)
                sDV.inc(i)
            self.V(ev)
            self.bank_free(bk, sDV)
        self.pp_vevac = sDV.n

        def v2(v):
            sAC.wait_at(v, w_done)
            v.wait_ge(sDV.h, sDV.n)
            dmp = self.B2[:].bitcast(F32)
            for j in range(CB):
                i = v.scalar_tensor_tensor(dmp[:, j * 512:(j + 1) * 512],
                                           self.A_(j), 1.0, self.B_(j),
                                           op0=ALU.mult, op1=ALU.mult,
                                           accum_out=self.COLT[:, 8 + j:9 + j])
            sDV.inc(i); v.wait_ge(sDV.h, sDV.n)
            i = v.tensor_tensor(self.SLOT[:, 0:16], self.SLOT[:, 0:16],
                                self.COLT[:, 0:16], op=ALU.add)
            sDV.inc(i)
        self.V(v2)

    def _s_row_math(self, s2_off=512, pe_guard=0):
        sDV, sAC, sPE = self.sDV, self.sAC, self.sPE
        R = self.ROWS
        RB = self.ROWSB
        scr1 = R[0:1, 1536:2048]
        scr2 = R[0:1, 1024:1536]

        def v(v):
            sAC.wait_all(v)
            sPE.wait_at(v, pe_guard)
            i = v.tensor_scalar(scr1, R[0:1, s2_off:s2_off + 512], 1.0 / 1024.0, None, op0=ALU.mult)
            sDV.inc(i); v.wait_ge(sDV.h, sDV.n)
            i = v.tensor_tensor(scr2, R[0:1, 0:512], R[0:1, 0:512], op=ALU.mult)
            sDV.inc(i); v.wait_ge(sDV.h, sDV.n)
            i = v.tensor_tensor(scr1, scr1, scr2, op=ALU.mult)
            sDV.inc(i)
        self.V(v)

        def a(sc):
            sDV.wait_all(sc)
            i = sc.activation(scr2, scr1, AF.Ln, bias=self.CONSTS[0:1, 1:2])
            sAC.inc(i); sc.wait_ge(sAC.h, sAC.n)
            i = sc.activation(scr1, scr2, AF.Exp, scale=self.CONSTS[0:1, 3:4])
            sAC.inc(i)
        self.A(a)

        def v2(v):
            sAC.wait_all(v)
            i = v.tensor_tensor(R[0:1, 1024:1536], R[0:1, 0:512], scr1, op=ALU.mult)
            sDV.inc(i); v.wait_ge(sDV.h, sDV.n)
            i = v.tensor_copy(RB[0:1, 1024:1536], R[0:1, 1024:1536])
            sDV.inc(i); v.wait_ge(sDV.h, sDV.n)
            i = v.tensor_tensor(RB[0:1, 1536:2048], R[0:1, 1024:1536],
                                RB[0:1, 1024:1536], op=ALU.subtract)
            sDV.inc(i)
        self.V(v2)

    def make_offsets(self):
        sDV, sLD, sCC = self.sDV, self.sLD, self.sCC
        tot_ready = sDV.n

        sCL, sPL = self.sCL, self.sPL

        def g(g):
            sDV.wait_at(g, tot_ready)
            sCC.inc(g.dma_start(out=self.d_tin, in_=self.SLOT[:, 0:16]), 16)
            g.wait_ge(sCC.h, sCC.n)
            sCL.inc(g.collective_compute(
                "AllGather",
                mybir.AluOpType.bypass,
                replica_groups=[list(range(NR))],
                ins=[self.d_tin.opt()],
                outs=[self.d_tout.opt()],
            ), 1)
            g.wait_ge(sCL.h, sCL.n)
            sCC.inc(g.dma_start(
                out=self.TOTS[:].rearrange("p (r c) -> p r c", c=16),
                in_=self.d_tout.rearrange("(r p) c -> p r c", p=128)), 16)
        self.G(g)
        gath_done = sCC.n

        def g2(g):
            g.wait_ge(sCC.h, gath_done)
            for r in range(NR):
                i = g.tensor_scalar(self.COLT[:, 0:16], self.TOTS[:, r * 16:(r + 1) * 16],
                                    self.MASK[:, r:r + 1], None, op0=ALU.mult)
                sPL.inc(i, 1)
                g.wait_ge(sPL.h, sPL.n)
                i = g.tensor_tensor(self.OFFS[:, :], self.OFFS[:, :], self.COLT[:, 0:16], op=ALU.add)
                sPL.inc(i, 1)
                g.wait_ge(sPL.h, sPL.n)
            i = g.tensor_copy(self.CARRY[:, :], self.OFFS[:, :])
            sPL.inc(i, 1)
        self.G(g2)
        self.offs_done = sPL.n

    def m_begin(self, sl):
        sLD, sPE, sDV, sAC = self.sLD, self.sPE, self.sDV, self.sAC
        self.wr16_guard = [0, 0]

        def v(v):
            i = v.tensor_copy(self.ROWS[0:1, 0:512], self.R_ALL[0:1, sl * 512:(sl + 1) * 512])
            sDV.inc(i); v.wait_ge(sDV.h, sDV.n)
            i = v.tensor_copy(self.ROWSB[0:1, 0:512], self.ROWS[0:1, 0:512])
            sDV.inc(i); v.wait_ge(sDV.h, sDV.n)
            i = v.tensor_tensor(self.ROWSB[0:1, 512:1024], self.ROWS[0:1, 0:512],
                                self.ROWSB[0:1, 0:512], op=ALU.subtract)
            sDV.inc(i)
        self.V(v)
        rb_ready = sDV.n

        def t(tt):
            self.bank_wait(tt, 5)
            sDV.wait_at(tt, rb_ready)
            i = tt.matmul(self.pb[5][:, 0:512], self.ONES1[:, :],
                          self.ROWSB[0:1, 0:512], start=True, stop=False)
            i = tt.matmul(self.pb[5][:, 0:512], self.ONES1[:, :],
                          self.ROWSB[0:1, 512:1024], start=False, stop=True)
            sPE.inc(i)
        self.T(t)
        rb_done = sPE.n

        def a(sc):
            sPE.wait_at(sc, rb_done)
            i = sc.activation(self.SCR2[:, :], self.pb[5][:, 0:512], AF.Copy)
            sAC.inc(i)
        self.A(a)
        self.bank_free(5, sAC)
        self.rbc_done = sAC.n

    def m_k(self, sl):
        sPE, sDV, sAC = self.sPE, self.sDV, self.sAC
        p0, w0 = self.wload_cached(f"k0_{sl}", 0)
        p1, w1 = self.wload_cached(f"k1_{sl}", 1)
        pars = [(p0, w0)] * 4 + [(p1, w1)] * 4
        for j in range(CB):
            par, wr = pars[j]
            bk = j % 4
            def t(tt, sl=sl, j=j, bk=bk, par=par, wr=wr):
                self.bank_wait(tt, bk)
                self.sWLs[par].wait_at(tt, wr)
                for cb in range(CB):
                    i = tt.matmul(self.pb[bk][:, 0:512],
                                  self.slot_blk(par, cb, j % 4), self.hi(sl, cb),
                                  start=(cb == 0), stop=(cb == CB - 1))
                sPE.inc(i)
            self.T(t)
            peval = sPE.n
            def ev(sc, j=j, bk=bk, peval=peval):
                sPE.wait_at(sc, peval)
                i = sc.activation(self.k2(j), self.pb[bk][:, 0:512], AF.Square)
                sAC.inc(i)
            self.A(ev)
            def ev2(v, j=j, bk=bk, peval=peval):
                sPE.wait_at(v, peval)
                i = v.tensor_copy(self.A_(j), self.pb[bk][:, 0:512])
                sDV.inc(i)
            self.V(ev2)
            self.bank_free(bk, sAC, sDV)
        self.wrelease(p0, sPE)
        self.wrelease(p1, sPE)
        self.k2_done = sAC.n

    def m_ksum(self):
        sPE, sAC = self.sPE, self.sAC
        k2_done = self.k2_done

        def t(tt):
            self.bank_wait(tt, 4)
            sAC.wait_at(tt, k2_done)
            for j in range(CB):
                i = tt.matmul(self.pb[4][0:1, 0:512], self.ONES128[:, :], self.k2(j),
                              start=(j == 0), stop=(j == CB - 1))
            sPE.inc(i)
        self.T(t)
        ss_done = sPE.n

        def a(sc):
            sPE.wait_at(sc, ss_done)
            i = sc.activation(self.ROWS[0:1, 512:1024], self.pb[4][0:1, 0:512], AF.Copy)
            sAC.inc(i)
        self.A(a)
        self.bank_free(4, sAC)

    def m_v(self, sl):
        sPE, sDV, sAC = self.sPE, self.sDV, self.sAC
        p0, w0 = self.wload_cached(f"v0_{sl}", 2)
        p1, w1 = self.wload_cached(f"v1_{sl}", 3)
        pars = [(p0, w0)] * 4 + [(p1, w1)] * 4
        rbc_done = self.rbc_done
        for j in range(CB):
            if j == 2:
                self.m_ksum()
            par, wr = pars[j]
            bk = j % 4
            def t(tt, sl=sl, j=j, bk=bk, par=par, wr=wr):
                self.bank_wait(tt, bk)
                self.sWLs[par].wait_at(tt, wr)
                for cb in range(CB):
                    i = tt.matmul(self.pb[bk][:, 0:512],
                                  self.slot_blk(par, cb, j % 4), self.hi(sl, cb),
                                  start=(cb == 0), stop=(cb == CB - 1))
                sPE.inc(i)
            self.T(t)
            peval = sPE.n
            def ev(v, j=j, bk=bk, peval=peval):
                sPE.wait_at(v, peval)
                sAC.wait_at(v, rbc_done)
                i = v.tensor_tensor(self.C_(j), self.pb[bk][:, 0:512], self.SCR2[:, :], op=ALU.mult)
                sDV.inc(i)
            self.V(ev)
            self.bank_free(bk, sDV)
        self.wrelease(p0, sPE)
        self.wrelease(p1, sPE)

    def m_fc1(self, sl):
        sPE, sDV = self.sPE, self.sDV
        par, wr = self.wload_cached(f"fc1_{sl}", 4)
        for lb in range(4):
            bk = 6 + (lb % 2)
            def t(tt, sl=sl, lb=lb, bk=bk, par=par, wr=wr):
                self.bank_wait(tt, bk)
                self.sWLs[par].wait_at(tt, wr)
                for cb in range(CB):
                    i = tt.matmul(self.pb[bk][:, 0:512],
                                  self.slot_blk(par, cb, lb), self.hi(sl, cb),
                                  start=(cb == 0), stop=(cb == CB - 1))
                sPE.inc(i)
            self.T(t)
            peval = sPE.n
            def ev(v, lb=lb, bk=bk, peval=peval):
                sPE.wait_at(v, peval)
                i = v.tensor_copy(self.lat(lb), self.pb[bk][:, 0:512])
                sDV.inc(i)
            self.V(ev)
            self.bank_free(bk, sDV)
        self.wrelease(par, sPE)
        self.lat_done = sDV.n

    def m_smath_k(self):
        sPE, sDV, sAC = self.sPE, self.sDV, self.sAC

        def t(tt):
            self.bank_wait(tt, 5)
            sDV.wait_at(tt, self.sDV.n)
            i = tt.matmul(self.pb[5][:, 0:512], self.ONES1[:, :],
                          self.ROWSB[0:1, 1024:1536], start=True, stop=False)
            i = tt.matmul(self.pb[5][:, 0:512], self.ONES1[:, :],
                          self.ROWSB[0:1, 1536:2048], start=False, stop=True)
            sPE.inc(i)
        self.T(t)
        skb_done = sPE.n
        self.skb_done = skb_done

        def v(v):
            sPE.wait_at(v, skb_done)
            for j in range(CB):
                i = v.tensor_tensor(self.A_(j), self.A_(j), self.pb[5][:, 0:512], op=ALU.mult)
            sDV.inc(i)
        self.V(v)
        ks_done = sDV.n
        self.bank_free(5, sDV)

        def a(sc):
            sDV.wait_at(sc, ks_done)
            for j in range(CB):
                i = sc.activation(self.B_(j), self.A_(j), AF.Exp)
            sAC.inc(i)
        self.A(a)
        w_done = sAC.n

        def v2(v):
            sAC.wait_at(v, w_done)
            for j in range(CB):
                i = v.tensor_tensor(self.C_(j), self.C_(j), self.B_(j), op=ALU.mult)
            sDV.inc(i)
        self.V(v2)
        kv_done = sDV.n

        self.kv_done = kv_done
        self.w_done_k = w_done

    def m_scans(self, part):
        sDV, sAC, sCC = self.sDV, self.sAC, self.sCC
        kv_done, w_done = self.kv_done, self.w_done_k

        def v2(v):
            if part == 0:
                sAC.wait_at(v, w_done)
                sDV.wait_at(v, kv_done)
                self.sPL.wait_at(v, self.offs_done)
            js = range((part % 2) * 4, (part % 2) * 4 + 4)
            if part < 2:
                for j in js:
                    i = v.tensor_tensor_scan(self.A_(j), self.B_(j), self.B_(j),
                                             self.CARRY[:, j:j + 1], op0=ALU.add, op1=ALU.bypass)
                    sDV.inc(i)
            else:
                for j in js:
                    i = v.tensor_tensor_scan(self.B_(j), self.C_(j), self.C_(j),
                                             self.CARRY[:, 8 + j:9 + j], op0=ALU.add, op1=ALU.bypass)
                    sDV.inc(i)
            if part == 3:
                v.wait_ge(sDV.h, sDV.n)
                for j in range(CB):
                    i = v.tensor_copy(self.CARRY[:, j:j + 1], self.A_(j)[:, 511:512])
                    i = v.tensor_copy(self.CARRY[:, 8 + j:9 + j], self.B_(j)[:, 511:512])
                sDV.inc(i)
        self.V(v2)
        if part == 3:
            self.scans_done = sDV.n

    def m_experts(self, sl, es, first):
        sPE, sDV, sAC, sLD = self.sPE, self.sDV, self.sAC, self.sLD
        lat_done = self.lat_done
        if first:
            self.ex_rot = 0
        for e in es:
            while self.ex_pend and self.ex_pend[0][1] == 3 and self.ex_pend[0][0] <= e - 2:
                self._emit_w2(*self.ex_pend.pop(0))
            par, wr = self.wload_cached(f"ex_{e}_{sl}", 5 + e)

            half = e % 2
            guard = self.wr16_guard[half]
            def y(sync, sl=sl, e=e, half=half, guard=guard):
                self.sDW.wait_at(sync, self.wr_written_all)
                sPE.wait_at(sync, guard)
                self.sR[half].inc(sync.dma_start(out=self.WR16[0:1, half * 512:(half + 1) * 512],
                                                 in_=self.d_wr[e:e + 1, sl * 512:(sl + 1) * 512]), 16)
            self.Y(y)
            row_ready = self.sR[half].n

            wbk = (4, 5, 6, 7)[self.ex_rot % 4]; self.ex_rot += 1
            def tb(tt, e=e, wbk=wbk, half=half, row_ready=row_ready):
                self.bank_wait(tt, wbk)
                self.sR[half].wait_at(tt, row_ready)
                i = tt.matmul(self.pb[wbk][:, 0:512], self.ONES1[:, :],
                              self.WR16[0:1, half * 512:(half + 1) * 512], start=True, stop=True)
                sPE.inc(i)
            self.T(tb)
            wb_peval = sPE.n
            self.wr16_guard[half] = wb_peval
            def ab(v, e=e, wbk=wbk, wb_peval=wb_peval):
                sPE.wait_at(v, wb_peval)
                i = v.tensor_copy(self.wbc2(e), self.pb[wbk][:, 0:512])
                sDV.inc(i)
            self.V(ab)
            self.bank_free(wbk, sDV)
            wbc_ready = sDV.n

            for ib in range(4):
                bk = (4, 5, 6, 7)[self.ex_rot % 4]; self.ex_rot += 1
                def t(tt, e=e, ib=ib, bk=bk, par=par, wr=wr):
                    self.bank_wait(tt, bk)
                    self.sWLs[par].wait_at(tt, wr)
                    sDV.wait_at(tt, lat_done)
                    for lb in range(4):
                        i = tt.matmul(self.pb[bk][:, 0:512],
                                      self.slot(par)[:, lb * 512 + ib * 128: lb * 512 + (ib + 1) * 128],
                                      self.lat(lb),
                                      start=(lb == 0), stop=(lb == 3))
                    sPE.inc(i)
                self.T(t)
                peval = sPE.n
                guard = max(wbc_ready, self.ex_g2_prev)
                def ev(v, e=e, ib=ib, bk=bk, peval=peval, guard=guard):
                    sPE.wait_at(v, peval)
                    sDV.wait_at(v, guard)
                    i = v.tensor_tensor(self.SCRH[:, :], self.pb[bk][:, 0:512],
                                        self.wbc2(e), op=ALU.mult)
                    sDV.inc(i); v.wait_ge(sDV.h, sDV.n)
                    i = v.scalar_tensor_tensor(self.g2(ib), self.pb[bk][:, 0:512], 0.0,
                                               self.SCRH[:, :], op0=ALU.max, op1=ALU.mult)
                    sDV.inc(i)
                self.V(ev)
                self.ex_g2_prev = sDV.n
                self.bank_free(bk, sDV)
                if DEFER_W2:
                    self.ex_pend.append((e, ib, par, sDV.n, first and e == es[0] and ib == 0))
                    if len(self.ex_pend) > 2:
                        self._emit_w2(*self.ex_pend.pop(0))
                else:
                    self._emit_w2(e, ib, par, sDV.n, first and e == es[0] and ib == 0)

    def _emit_w2(self, e, ib, par, g2_done, first_w2):
        sPE, sDV = self.sPE, self.sDV
        def t2(tt, e=e, ib=ib, par=par, g2_done=g2_done, first_w2=first_w2):
            if first_w2:
                for lb in range(4):
                    self.bank_wait(tt, lb)
            sDV.wait_at(tt, g2_done)
            for lb in range(4):
                i = tt.matmul(self.pb[lb][:, 0:512],
                              self.slot(par)[:, 2048 + ib * 512 + lb * 128: 2048 + ib * 512 + (lb + 1) * 128],
                              self.g2(ib),
                              start=(e == 0 and ib == 0), stop=(e == 7 and ib == 3))
            sPE.inc(i)
        self.T(t2)
        if ib == 3:
            self.wrelease(par, sPE)

    def m_comb_evac(self):
        sPE, sDV = self.sPE, self.sDV
        while self.ex_pend:
            self._emit_w2(*self.ex_pend.pop(0))
        comb_done = sPE.n

        def v(v):
            sPE.wait_at(v, comb_done)
            for lb in range(4):
                i = v.tensor_copy(self.combv(lb), self.pb[lb][:, 0:512])
                sDV.inc(i)
        self.V(v)
        for lb in range(4):
            self.bank_free(lb, sDV)

    def m_q_jobs(self, sl):
        sPE, sDV, sAC = self.sPE, self.sDV, self.sAC
        p0, w0 = self.wload_cached(f"q0_{sl}", 13)
        p1, w1 = self.wload_cached(f"q1_{sl}", 14)
        pars = [(p0, w0)] * 4 + [(p1, w1)] * 4
        for j in range(CB):
            par, wr = pars[j]
            bk = j % 4
            def t(tt, sl=sl, j=j, bk=bk, par=par, wr=wr):
                self.bank_wait(tt, bk)
                self.sWLs[par].wait_at(tt, wr)
                for cb in range(CB):
                    i = tt.matmul(self.pb[bk][:, 0:512],
                                  self.slot_blk(par, cb, j % 4), self.hi(sl, cb),
                                  start=(cb == 0), stop=(cb == CB - 1))
                sPE.inc(i)
            self.T(t)
            peval = sPE.n
            def ev(sc, j=j, bk=bk, peval=peval):
                sPE.wait_at(sc, peval)
                if j == 0:
                    self.sST.wait_at(sc, self.prev_store)
                i = sc.activation(self.k2(j), self.pb[bk][:, 0:512], AF.Square)
                sAC.inc(i)
                i = sc.activation(self.XN_(j), self.pb[bk][:, 0:512], AF.Copy)
                sAC.inc(i)
            self.A(ev)
            self.bank_free(bk, sAC)
            if j == 3:
                q2_half = sAC.n
        self.wrelease(p0, sPE)
        self.wrelease(p1, sPE)
        q2_done = sAC.n
        self.q_evac_ac = q2_done

        def t(tt):
            self.bank_wait(tt, 4)
            sAC.wait_at(tt, q2_done)
            for j in range(CB):
                i = tt.matmul(self.pb[4][0:1, 0:512], self.ONES128[:, :], self.k2(j),
                              start=(j == 0), stop=(j == CB - 1))
            sPE.inc(i)
        self.T(t)
        ss_done = sPE.n

        def a(sc):
            sPE.wait_at(sc, ss_done)
            i = sc.activation(self.ROWS[0:1, 512:1024], self.pb[4][0:1, 0:512], AF.Copy)
            sAC.inc(i)
        self.A(a)
        self.bank_free(4, sAC)

    def m_q_tail(self, sl):
        sPE, sDV, sAC, sLD = self.sPE, self.sDV, self.sAC, self.sLD
        self._s_row_math(s2_off=512, pe_guard=self.skb_done)

        def t2(tt):
            self.bank_wait(tt, 5)
            sDV.wait_at(tt, self.sDV.n)
            i = tt.matmul(self.pb[5][:, 0:512], self.ONES1[:, :],
                          self.ROWSB[0:1, 1024:1536], start=True, stop=False)
            i = tt.matmul(self.pb[5][:, 0:512], self.ONES1[:, :],
                          self.ROWSB[0:1, 1536:2048], start=False, stop=True)
            sPE.inc(i)
        self.T(t2)
        sqb_done = sPE.n

        def v(v):
            sPE.wait_at(v, sqb_done)
            sAC.wait_at(v, self.q_evac_ac)
            for j in range(CB):
                i = v.tensor_tensor(self.XN_(j), self.XN_(j), self.pb[5][:, 0:512], op=ALU.mult)
            sDV.inc(i)
        self.V(v)
        self.bank_free(5, sDV)
        self.qs_done = sDV.n

    def m_q_sig(self, sl):
        sDV, sAC = self.sDV, self.sAC
        qs_done = self.qs_done

        def a2(sc):
            sDV.wait_at(sc, qs_done)
            sDV.wait_at(sc, self.scans_done)
            for j in range(CB):
                i = sc.activation(self.C_(j), self.XN_(j), AF.Exp, scale=self.cst(5))
            sAC.inc(i); sc.wait_ge(sAC.h, sAC.n)
            for j in range(CB):
                i = sc.activation(self.XN_(j), self.C_(j), AF.Ln, bias=self.cst(4))
            sAC.inc(i); sc.wait_ge(sAC.h, sAC.n)
            for j in range(CB):
                i = sc.activation(self.C_(j), self.XN_(j), AF.Exp, scale=self.cst(5))
            sAC.inc(i); sc.wait_ge(sAC.h, sAC.n)
            for j in range(CB):
                i = sc.activation(self.XN_(j), self.A_(j), AF.Ln, bias=self.cst(2))
            sAC.inc(i); sc.wait_ge(sAC.h, sAC.n)
            for j in range(CB):
                i = sc.activation(self.A_(j), self.XN_(j), AF.Exp, scale=self.cst(5))
            sAC.inc(i)
        self.A(a2)
        self.rw_done = sAC.n

    def m_y_tail(self, sl):
        sDV, sAC, sLD = self.sDV, self.sAC, self.sLD

        def v2(v):
            sAC.wait_at(v, self.rw_done)
            for j in range(CB):
                i = v.tensor_tensor(self.B_(j), self.B_(j), self.A_(j), op=ALU.mult)
            sDV.inc(i); v.wait_ge(sDV.h, sDV.n)
            for j in range(CB):
                i = v.tensor_tensor(self.ys(j), self.B_(j), self.C_(j), op=ALU.mult)
            sDV.inc(i)
        self.V(v2)
        self.ysig_done = sDV.n

        def y(sync):
            sDV.wait_at(sync, self.ysig_done)
            self.sX.inc(sync.dma_start(
                out=self.AR_C[:].rearrange("p (b d) -> p b d", d=1024),
                in_=self.d_x[sl * TS:(sl + 1) * TS, :].rearrange("(b p) d -> p b d", p=128)), 16)
        self.Y(y)
        self.x_ready = self.sX.n

    def m_up(self, sl):
        sPE, sDV, sAC = self.sPE, self.sDV, self.sAC
        ysig_done = self.ysig_done
        for p in range(8):
            par, wr = self.wload_cached(f"up_{p}_{sl}", 15 + p)
            for sbw in range(4):
                sb = p * 4 + sbw
                bk = (6, 7, 0, 1)[sbw]
                def t(tt, sl=sl, sbw=sbw, bk=bk, par=par, wr=wr):
                    self.bank_wait(tt, bk)
                    self.sWLs[par].wait_at(tt, wr)
                    for cb in range(CB):
                        i = tt.matmul(self.pb[bk][:, 0:512],
                                      self.slot_blk(par, cb, sbw), self.hi(sl, cb),
                                      start=(cb == 0), stop=(cb == CB - 1))
                    sPE.inc(i)
                self.T(t)
                peval = sPE.n
                scr = (self.SCRH, self.SCR2, self.SCR3, self.SCR4)[sb % 4]
                def ev_a(sc, bk=bk, peval=peval, scr=scr):
                    sPE.wait_at(sc, peval)
                    i = sc.activation(scr[:, :], self.pb[bk][:, 0:512], AF.Relu)
                    sAC.inc(i)
                self.A(ev_a)
                acval = sAC.n
                def ev(v, sb=sb, bk=bk, acval=acval, scr=scr):
                    sAC.wait_at(v, acval)
                    sDV.wait_at(v, ysig_done)
                    i = v.tensor_tensor(self.hsq(sb), self.pb[bk][:, 0:512], scr[:, :], op=ALU.mult)
                    sDV.inc(i)
                self.V(ev)
                self.bank_free(bk, sDV, sAC)
            self.wrelease(par, sPE)

    def m_fc2_down(self, sl):
        sPE, sDV, sLD = self.sPE, self.sDV, self.sLD
        parC, wr2 = self.wload_cached(f"fc2_{sl}", 23)
        comb_evac = sDV.n

        def bw(tt):
            for i_ in range(8):
                self.bank_wait(tt, i_)
        self.T(bw)

        def t2(tt):
            self.sWLs[parC].wait_at(tt, wr2)
            sDV.wait_at(tt, comb_evac)
            for tb in range(NB):
                for lb in range(4):
                    lhsT = self.B2[:, lb * 512 + tb * 128: lb * 512 + (tb + 1) * 128]
                    i = tt.matmul(self.pb[2 * tb][:, 0:512],
                                  lhsT, self.slot(parC)[:, lb * 1024: lb * 1024 + 512],
                                  start=(lb == 0), stop=False)
                    i = tt.matmul(self.pb[2 * tb + 1][:, 0:512],
                                  lhsT, self.slot(parC)[:, lb * 1024 + 512: (lb + 1) * 1024],
                                  start=(lb == 0), stop=False)
            sPE.inc(i)
        self.T(t2)
        self.wrelease(parC, sPE)

        hsq_done = sDV.n
        for p in range(8):
            par, wr3 = self.wload_cached(f"dn_{p}_{sl}", 24 + p)

            def t3(tt, p=p, par=par, wr3=wr3):
                self.sWLs[par].wait_at(tt, wr3)
                sDV.wait_at(tt, hsq_done)
                last_piece = (p == 7)
                for tb in range(NB):
                    for sbw in range(4):
                        sb = p * 4 + sbw
                        ar = self.AR_A if sb < 16 else self.AR_B
                        jj = sb % 16
                        lhsT = ar[:].bitcast(BF16)[:, jj * 512 + tb * 128: jj * 512 + (tb + 1) * 128]
                        stop = last_piece and (sbw == 3)
                        i = tt.matmul(self.pb[2 * tb][:, 0:512],
                                      lhsT, self.slot(par)[:, sbw * 1024: sbw * 1024 + 512],
                                      start=False, stop=stop)
                        i = tt.matmul(self.pb[2 * tb + 1][:, 0:512],
                                      lhsT, self.slot(par)[:, sbw * 1024 + 512: (sbw + 1) * 1024],
                                      start=False, stop=stop)
                sPE.inc(i)
            self.T(t3)
            self.wrelease(par, sPE)
        fin_done = sPE.n

        def v(v):
            sPE.wait_at(v, fin_done)
            self.sX.wait_at(v, self.x_ready)
            self.sST.wait_at(v, self.prev_store)
            for tb in range(NB):
                i = v.tensor_scalar(self.xrb(tb), self.xrb(tb),
                                    self.SOUT[:, sl * 8 + tb: sl * 8 + tb + 1],
                                    self.SOUT[:, sl * 8 + 4 + tb: sl * 8 + 5 + tb],
                                    op0=ALU.mult, op1=ALU.add)
                sDV.inc(i); v.wait_ge(sDV.h, sDV.n)
                i = v.tensor_tensor(self.outb(tb)[:, 0:512], self.pb[2 * tb][:, 0:512],
                                    self.xrb(tb)[:, 0:512], op=ALU.add)
                sDV.inc(i)
                i = v.tensor_tensor(self.outb(tb)[:, 512:1024], self.pb[2 * tb + 1][:, 0:512],
                                    self.xrb(tb)[:, 512:1024], op=ALU.add)
                sDV.inc(i)
        self.V(v)
        for i_ in range(8):
            self.bank_free(i_, sDV)

    def m_swiglu(self, sl):
        sPE, sDV, sAC = self.sPE, self.sDV, self.sAC
        ysig_done = self.ysig_done
        self.sw_scr_guard = [0, 0, 0, 0]
        self.sw_sg_ac = [0] * 8
        quarters = [(8, "swg1"), (12, "swg2"), (0, "swu1"), (4, "swu2")]
        for qi, (f0, name) in enumerate(quarters):
            par, wr = self.wload_cached(name + str(sl), 32 + qi)
            for fj in range(4):
                fb = f0 + fj
                bk = (0, 1, 4, 5)[fb % 4]
                scrX = (self.SCRH, self.SCR2, self.SCR3, self.SCR4)[fb % 4]
                is_g = fb >= 8
                def t(tt, fj=fj, bk=bk, par=par, wr=wr):
                    self.bank_wait(tt, bk)
                    self.sWLs[par].wait_at(tt, wr)
                    sDV.wait_at(tt, ysig_done)
                    for cb in range(CB):
                        i = tt.matmul(self.pb[bk][:, 0:512],
                                      self.slot(par)[:, cb * 512 + fj * 128: cb * 512 + (fj + 1) * 128],
                                      self.ys(cb),
                                      start=(cb == 0), stop=(cb == CB - 1))
                    sPE.inc(i)
                self.T(t)
                peval = self.sPE.n
                if is_g:
                    scr_guard = self.sw_scr_guard[fb % 4]
                    def ev(sc, fb=fb, bk=bk, peval=peval, scrX=scrX, scr_guard=scr_guard):
                        sPE.wait_at(sc, peval)
                        sDV.wait_at(sc, scr_guard)
                        i = sc.activation(scrX[:, :], self.pb[bk][:, 0:512], AF.Exp, scale=self.cst(5))
                        sAC.inc(i); sc.wait_ge(sAC.h, sAC.n)
                        i = sc.activation(self.sg(fb - 8), scrX[:, :], AF.Ln, bias=self.cst(4))
                        sAC.inc(i); sc.wait_ge(sAC.h, sAC.n)
                        i = sc.activation(scrX[:, :], self.sg(fb - 8), AF.Exp, scale=self.cst(5))
                        sAC.inc(i)
                    self.A(ev)
                    acv = sAC.n
                    self.sw_sg_ac[fb - 8] = acv
                    def ev_m(v, fb=fb, bk=bk, acv=acv, scrX=scrX):
                        sAC.wait_at(v, acv)
                        i = v.tensor_tensor(self.sg(fb - 8), self.pb[bk][:, 0:512], scrX[:, :], op=ALU.mult)
                        sDV.inc(i)
                    self.V(ev_m)
                    self.bank_free(bk, sAC, sDV)
                    self.sw_scr_guard[fb % 4] = sDV.n
                else:
                    sg_ac = self.sw_sg_ac[fb]
                    def ev(v, fb=fb, bk=bk, peval=peval, sg_ac=sg_ac):
                        sPE.wait_at(v, peval)
                        sAC.wait_at(v, sg_ac)
                        i = v.tensor_tensor(self.ug(fb), self.pb[bk][:, 0:512],
                                            self.sg(fb), op=ALU.mult)
                        sDV.inc(i)
                    self.V(ev)
                    self.bank_free(bk, sDV)
            self.wrelease(par, sPE)

    def m_proj(self, sl):
        sPE, sDV = self.sPE, self.sDV
        ug_done = sDV.n
        parA, wrA = self.wload_cached(f"wo0_{sl}", 36)
        parB, wr = self.wload_cached(f"wo1_{sl}", 37)

        for tb in range(NB):
            b0 = 2 + 4 * (tb % 2)
            def t(tt, tb=tb, b0=b0):
                self.bank_wait(tt, b0)
                self.bank_wait(tt, b0 + 1)
                self.sWLs[parA].wait_at(tt, wrA)
                self.sWLs[parB].wait_at(tt, wr)
                sDV.wait_at(tt, ug_done)
                for dcb in range(CB):
                    lhsT = self.BSC[:, 4096 + dcb * 512 + tb * 128: 4096 + dcb * 512 + (tb + 1) * 128]
                    i = tt.matmul(self.pb[b0][:, 0:512],
                                  lhsT, self.slot(parA)[:, dcb * 512:(dcb + 1) * 512],
                                  start=(dcb == 0), stop=(dcb == CB - 1))
                    i = tt.matmul(self.pb[b0 + 1][:, 0:512],
                                  lhsT, self.slot(parB)[:, dcb * 512:(dcb + 1) * 512],
                                  start=(dcb == 0), stop=(dcb == CB - 1))
                sPE.inc(i)
            self.T(t)
            peval = sPE.n
            def ev(v, tb=tb, b0=b0, peval=peval):
                sPE.wait_at(v, peval)
                i = v.tensor_tensor(self.outb(tb)[:, 0:512], self.outb(tb)[:, 0:512],
                                    self.pb[b0][:, 0:512], op=ALU.add)
                sDV.inc(i)
                i = v.tensor_tensor(self.outb(tb)[:, 512:1024], self.outb(tb)[:, 512:1024],
                                    self.pb[b0 + 1][:, 0:512], op=ALU.add)
                sDV.inc(i)
            self.V(ev)
            self.bank_free(b0, sDV)
            self.bank_free(b0 + 1, sDV)
        self.wrelease(parA, sPE)
        self.wrelease(parB, sPE)
        out_done = sDV.n

        def y(sync):
            sDV.wait_at(sync, out_done)
            self.sST.inc(sync.dma_start(
                out=self.d_y[sl * TS:(sl + 1) * TS, :].rearrange("(b p) d -> p b d", p=128),
                in_=self.AR_XN[:].rearrange("p (b d) -> p b d", d=1024)), 16)
        self.Y(y)

    def program(self):
        self.startup()
        self.build_convert_plan()
        self.convert_all()
        self.cv_total = self.sCV.n
        self.pp_xn_done = 0
        self.pp_var_done = 0
        self.pp_xpose_done = 0
        self.pp_vevac = 0
        self.pp_wte_done = 0
        self.pp_skb = 0
        self.pp_srow_v2 = 0
        self.pp_rbc_pe = 0
        self.pp_gate_pe = 0
        self.pp_gt_pe = 0
        self.pp_wt_pe = 0
        for sl in range(NSL):
            self.pre_x(sl)
            self.slice_barrier()
            self.pre_ln(sl)
            self.pre_gate(sl)
            self.pre_mid(sl)
        self.wr_written_all = self.sDW.n
        for s_ in range(4):
            self.wrelease(s_, self.sPE)
        for name, idx in (("k0_0", 0), ("k1_0", 1), ("v0_0", 2), ("v1_0", 3), ("fc1_0", 4)):
            self.preloaded[name] = self.wload_cached(name, idx)
        self.make_offsets()
        for sl in range(NSL):
            self.prev_store = self.sST.n
            self.slice_barrier(with_store=False)
            self.m_begin(sl)
            self.m_k(sl)
            self.m_v(sl)
            self.m_fc1(sl)
            self._s_row_math()
            self.m_q_jobs(sl)
            self.m_smath_k()
            self.m_experts(sl, [0], first=True)
            self.m_q_tail(sl)
            self.m_experts(sl, [1], first=False)
            self.m_scans(0)
            self.m_experts(sl, [2], first=False)
            self.m_scans(1)
            self.m_experts(sl, [3], first=False)
            self.m_scans(2)
            self.m_experts(sl, [4], first=False)
            self.m_scans(3)
            self.m_q_sig(sl)
            self.m_experts(sl, [5, 6, 7], first=False)
            self.m_comb_evac()
            self.m_y_tail(sl)
            self.m_up(sl)
            self.m_fc2_down(sl)
            self.m_swiglu(sl)
            self.m_proj(sl)

        def y(sync):
            sync.wait_ge(self.sST.h, self.sST.n)
        self.Y(y)


def build_nc():
    nc = bass.Bass("TRN2", target_bir_lowering=False)
    with contextlib.ExitStack() as stack:
        b = Builder(nc, None, stack)
        b.block = stack.enter_context(nc.Block())
        b.program()
    return nc


def kernel(**inputs):
    x = np.asarray(inputs["L0_attn_x"], np.float32)
    wqkvT = np.ascontiguousarray(np.asarray(inputs["L0_attn_w_qkv"], np.float32).T)
    wswT = np.ascontiguousarray(np.asarray(inputs["L0_attn_w_swiglu"], np.float32).T)
    woutT = np.ascontiguousarray(np.asarray(inputs["L0_attn_w_out"], np.float32).T)
    gateT = np.ascontiguousarray(np.asarray(inputs["L0_ffn_gate"], np.float32).T)
    cb = np.asarray(inputs["L0_ffn_corr_bias"], np.float32)
    fc1T = np.ascontiguousarray(np.asarray(inputs["L0_ffn_fc1_w"], np.float32).T)
    fc2T = np.ascontiguousarray(np.asarray(inputs["L0_ffn_fc2_w"], np.float32).T)
    w1T = np.ascontiguousarray(np.transpose(np.asarray(inputs["L0_ffn_w1"], np.float32), (0, 2, 1)))
    w2T = np.ascontiguousarray(np.transpose(np.asarray(inputs["L0_ffn_w2"], np.float32), (0, 2, 1)))
    upT = np.ascontiguousarray(np.asarray(inputs["L0_ffn_up_w"], np.float32).T)
    downT = np.ascontiguousarray(np.asarray(inputs["L0_ffn_down_w"], np.float32).T)

    ident = np.eye(128, dtype=np.float32)
    ones1 = np.ones((1, 128), BF)
    ones128 = np.ones((128, 1), BF)
    consts = np.zeros((128, 8), np.float32)
    consts[:, 0] = EPS_LN
    consts[:, 1] = EPS_F
    consts[:, 2] = EPS_Y
    consts[:, 3] = -0.5
    consts[:, 4] = 1.0
    consts[:, 5] = -1.0
    consts[:, 7] = 1.0 / 1024.0
    cbb = np.tile(cb[None, :], (128, 1)).astype(np.float32)

    in_maps = []
    for cid in range(8):
        bidx, q = cid // 4, cid % 4
        x_own = np.ascontiguousarray(x[bidx, q * TOK:(q + 1) * TOK, :])
        mask = np.zeros((128, NR), np.float32)
        mask[:, bidx * 4: bidx * 4 + q] = 1.0
        in_maps.append({
            "x_own": x_own, "mask": mask,
            "wqkvT": wqkvT, "wswT": wswT, "woutT": woutT,
            "gateT": gateT, "cbb": cbb, "fc1T": fc1T, "fc2T": fc2T,
            "w1T": w1T, "w2T": w2T, "upT": upT, "downT": downT,
            "ident": ident, "ones1": ones1, "ones128": ones128,
            "consts": consts,
        })

    nc = build_nc()
    res = run_bass_kernel_spmd(nc, in_maps, core_ids=list(range(8)))
    outs = [res.results[c]["y"] for c in range(8)]
    full = np.stack([np.concatenate(outs[0:4], axis=0),
                     np.concatenate(outs[4:8], axis=0)], axis=0)
    return full.astype(np.float32)


# version 43
# speedup vs baseline: 1.1359x; 1.0050x over previous
import numpy as np
import ml_dtypes
import contextlib
import concourse.bass as bass
import concourse.mybir as mybir
from concourse.bass_utils import run_bass_kernel_spmd

F32 = mybir.dt.float32
BF16 = mybir.dt.bfloat16
AF = mybir.ActivationFunctionType
ALU = mybir.AluOpType
AX = mybir.AxisListType

D = 1024
TOK = 2048
TS = 512
NB = TS // 128
NSL = TOK // TS
NR = 8
CB = D // 128

EPS_LN = 1e-5
EPS_F = 1.1920929e-07
EPS_Y = 1e-6
DEFER_W2 = True

BF = ml_dtypes.bfloat16


class Sem:
    def __init__(self, stack, nc, name):
        self.h = stack.enter_context(nc.semaphore(name))
        self.n = 0

    def inc(self, inst, k=1):
        inst.then_inc(self.h, k)
        self.n += k
        return inst

    def wait_all(self, eng):
        if self.n:
            eng.wait_ge(self.h, self.n)

    def wait_at(self, eng, val):
        if val:
            eng.wait_ge(self.h, val)


class Builder:
    def __init__(self, nc, block, stack):
        self.nc = nc
        self.block = block
        s = stack.enter_context

        self.d_x = nc.dram_tensor("x_own", [TOK, D], F32, kind="ExternalInput").ap()
        self.d_mask = nc.dram_tensor("mask", [128, NR], F32, kind="ExternalInput").ap()
        self.d_tin = nc.dram_tensor("tot_in", [128, 16], F32, kind="Internal").ap()
        self.d_tout = nc.dram_tensor("tot_out", [NR * 128, 16], F32, kind="Internal").ap()
        self.d_wqkvT = nc.dram_tensor("wqkvT", [D, 3 * D], F32, kind="ExternalInput").ap()
        self.d_wswT = nc.dram_tensor("wswT", [D, 2 * D], F32, kind="ExternalInput").ap()
        self.d_woutT = nc.dram_tensor("woutT", [D, D], F32, kind="ExternalInput").ap()
        self.d_gateT = nc.dram_tensor("gateT", [D, 8], F32, kind="ExternalInput").ap()
        self.d_cb = nc.dram_tensor("cbb", [128, 8], F32, kind="ExternalInput").ap()
        self.d_fc1T = nc.dram_tensor("fc1T", [D, 512], F32, kind="ExternalInput").ap()
        self.d_fc2T = nc.dram_tensor("fc2T", [512, D], F32, kind="ExternalInput").ap()
        self.d_w1T = nc.dram_tensor("w1T", [8, 512, 512], F32, kind="ExternalInput").ap()
        self.d_w2T = nc.dram_tensor("w2T", [8, 512, 512], F32, kind="ExternalInput").ap()
        self.d_upT = nc.dram_tensor("upT", [D, 4 * D], F32, kind="ExternalInput").ap()
        self.d_downT = nc.dram_tensor("downT", [4 * D, D], F32, kind="ExternalInput").ap()
        self.d_ident = nc.dram_tensor("ident", [128, 128], F32, kind="ExternalInput").ap()
        self.d_ones1 = nc.dram_tensor("ones1", [1, 128], BF16, kind="ExternalInput").ap()
        self.d_ones128 = nc.dram_tensor("ones128", [128, 1], BF16, kind="ExternalInput").ap()
        self.d_consts = nc.dram_tensor("consts", [128, 8], F32, kind="ExternalInput").ap()
        self.d_y = nc.dram_tensor("y", [TOK, D], F32, kind="ExternalOutput").ap()
        self.d_wr = nc.dram_tensor("wr_scratch", [8, TOK], BF16, kind="Internal").ap()
        self.d_wbf = nc.dram_tensor("wbf", [76, 128, 2048], BF16, kind="Internal").ap()

        self.HI = s(nc.sbuf_tensor("HI", [128, NSL * 4096], BF16))
        self.AR_A = s(nc.sbuf_tensor("AR_A", [128, 4096], F32))
        self.AR_B = s(nc.sbuf_tensor("AR_B", [128, 4096], F32))
        self.AR_C = s(nc.sbuf_tensor("AR_C", [128, 4096], F32))
        self.AR_XN = s(nc.sbuf_tensor("AR_XN", [128, 4096], F32))
        self.BSC = s(nc.sbuf_tensor("BSC", [128, 8192], BF16))
        self.B2 = s(nc.sbuf_tensor("B2", [128, 8192], BF16))
        self.WBuf = s(nc.sbuf_tensor("WBuf", [128, 20480], BF16))
        self.SCR2 = s(nc.sbuf_tensor("SCR2", [128, 512], F32))
        self.SCRH = s(nc.sbuf_tensor("SCRH", [128, 512], F32))
        self.SCR3 = s(nc.sbuf_tensor("SCR3", [128, 512], F32))
        self.SCR4 = s(nc.sbuf_tensor("SCR4", [128, 512], F32))
        self.W8H = s(nc.sbuf_tensor("W8H", [8, 512], BF16))
        self.G8 = self.SCR3[0:8, :]
        self.ST1 = self.SCR4[:, 64:96]
        self.MASK = self.SCR4[:, 96:104]
        self.WBC2 = s(nc.sbuf_tensor("WBC2", [128, 1024], F32))
        self.ROWS = s(nc.sbuf_tensor("ROWS", [1, 2048], F32))
        self.ROWSB = s(nc.sbuf_tensor("ROWSB", [1, 2048], BF16))
        self.R_ALL = s(nc.sbuf_tensor("R_ALL", [1, NSL * 512], F32))
        self.SOUT = s(nc.sbuf_tensor("SOUT", [128, NSL * 8], F32))
        self.WR16 = s(nc.sbuf_tensor("WR16", [1, 1024], BF16))
        self.GT = s(nc.sbuf_tensor("GT", [128, 64], F32))
        self.GH = s(nc.sbuf_tensor("GH", [128, 64], BF16))
        self.GL = s(nc.sbuf_tensor("GL", [128, 64], BF16))
        self.CBB = s(nc.sbuf_tensor("CBB", [128, 8], F32))
        self.TOTS = s(nc.sbuf_tensor("TOTS", [128, NR * 16], F32))
        self.IDENT = s(nc.sbuf_tensor("IDENT", [128, 128], F32))
        self.ONES1 = s(nc.sbuf_tensor("ONES1", [1, 128], BF16))
        self.ONES128 = s(nc.sbuf_tensor("ONES128", [128, 1], BF16))
        self.CONSTS = s(nc.sbuf_tensor("CONSTS", [128, 8], F32))
        self.CARRY = s(nc.sbuf_tensor("CARRY", [128, 16], F32))
        self.SLOT = s(nc.sbuf_tensor("SLOT", [128, 16], F32))
        self.OFFS = s(nc.sbuf_tensor("OFFS", [128, 16], F32))
        self.COLT = s(nc.sbuf_tensor("COLT", [128, 16], F32))
        self.R_s = s(nc.sbuf_tensor("R_s", [128, 32], F32))
        self.R_wte = s(nc.sbuf_tensor("R_wte", [128, 32], F32))
        self.R_t = [s(nc.sbuf_tensor(f"R_t{i}", [128, 32], F32)) for i in range(5)]
        self.R_m = s(nc.sbuf_tensor("R_m", [128, 32], F32))

        self.pb = [s(nc.psum_tensor(f"pb{i}", [128, 512], F32)) for i in range(8)]
        self.brel = [[] for _ in range(8)]

        self.sLD = Sem(stack, nc, "sLD")
        self.sX = Sem(stack, nc, "sX")
        self.sDW = Sem(stack, nc, "sDW")
        self.sR = [Sem(stack, nc, "sR0"), Sem(stack, nc, "sR1")]
        self.sCC = Sem(stack, nc, "sCC")
        self.sCL = Sem(stack, nc, "sCL")
        self.sPL = Sem(stack, nc, "sPL")
        self.sCV = Sem(stack, nc, "sCV")
        self.sWLs = [Sem(stack, nc, f"sWL{i}") for i in range(5)]
        self.sPE = Sem(stack, nc, "sPE")
        self.sDV = Sem(stack, nc, "sDV")
        self.sAC = Sem(stack, nc, "sAC")
        self.sST = Sem(stack, nc, "sST")

        self.wready = {}
        self.wslotrel = {}
        self.wparity = 0
        self.wr_written = [0] * NSL
        self.preloaded = {}
        self.ex_pend = []
        self.ex_g2_prev = 0

    def T(self, f): self.block.tensor(f)
    def V(self, f): self.block.vector(f)
    def A(self, f): self.block.scalar(f)
    def G(self, f): self.block.gpsimd(f)
    def Y(self, f): self.block.sync(f)

    def hi(self, sl, cb): return self.HI[:, sl * 4096 + cb * 512: sl * 4096 + (cb + 1) * 512]
    def lo(self, cb):   return self.BSC[:, cb * 512:(cb + 1) * 512]
    def ys(self, cb):   return self.BSC[:, cb * 512:(cb + 1) * 512]
    def ug(self, cb):   return self.BSC[:, 4096 + cb * 512: 4096 + (cb + 1) * 512]
    def k2(self, j):    return self.B2[:, j * 512:(j + 1) * 512]
    def combv(self, lb): return self.B2[:, lb * 512:(lb + 1) * 512]
    def lat(self, lb):  return self.B2[:, 4096 + lb * 512: 4096 + (lb + 1) * 512]
    def q2v(self, j):   return self.B2[:, 4096 + j * 512: 4096 + (j + 1) * 512]
    def g2(self, ib):   return self.B2[:, 6144 + ib * 512: 6144 + (ib + 1) * 512]
    def A_(self, j):    return self.AR_A[:, j * 512:(j + 1) * 512]
    def B_(self, j):    return self.AR_B[:, j * 512:(j + 1) * 512]
    def C_(self, j):    return self.AR_C[:, j * 512:(j + 1) * 512]
    def XN_(self, j):   return self.AR_XN[:, j * 512:(j + 1) * 512]
    def sg(self, j):    return self.AR_A[:, j * 512:(j + 1) * 512]
    def xb(self, b):    return self.AR_C[:, b * 1024:(b + 1) * 1024]
    def xnb(self, b):   return self.AR_XN[:, b * 1024:(b + 1) * 1024]
    def outb(self, b):  return self.AR_XN[:, b * 1024:(b + 1) * 1024]
    def xrb(self, b):   return self.AR_C[:, b * 1024:(b + 1) * 1024]
    def hsq(self, sb):
        ar = self.AR_A if sb < 16 else self.AR_B
        j = sb % 16
        return ar[:].bitcast(BF16)[:, j * 512:(j + 1) * 512]
    def kv_blk(self, cb, j):
        return self.WBuf[:, cb * 2048 + j * 128: cb * 2048 + (j + 1) * 128]
    def slot_blk(self, par, cb, jj):
        return self.slot(par)[:, cb * 512 + jj * 128: cb * 512 + (jj + 1) * 128]
    def wbc2(self, e):  return self.WBC2[:, (e % 2) * 512:((e % 2) + 1) * 512]
    def cst(self, i):   return self.CONSTS[:, i:i + 1]
    def slot(self, par): return self.WBuf[:, par * 4096:(par + 1) * 4096]

    def bank_wait(self, eng, i):
        for tr, val in self.brel[i]:
            tr.wait_at(eng, val)
        self.brel[i] = []

    def bank_free(self, i, *trackers):
        self.brel[i] = [(tr, tr.n) for tr in trackers]

    def wload(self, name, dst, src_ap, slot_key=None):
        keys = slot_key if isinstance(slot_key, (list, tuple)) else (
            [] if slot_key is None else [slot_key])
        def g(g):
            for k in keys:
                rel = self.wslotrel.pop(k, None)
                if rel is not None:
                    tr, val = rel
                    tr.wait_at(g, val)
            self.sWLs[0].inc(g.dma_start(out=dst, in_=src_ap), 16)
        self.G(g)
        self.wready[name] = self.sWLs[0].n

    def wload_slot(self, name, src_cols):
        par = self.wparity; self.wparity = (self.wparity + 1) % 5
        self.wload(name, self.slot(par).rearrange("p (cb e) -> p cb e", e=512),
                   src_cols.rearrange("(cb p) e -> p cb e", p=128), slot_key=par)
        return par, self.wready[name]

    def wrelease(self, slot_key, tracker):
        self.wslotrel[slot_key] = (tracker, tracker.n)

    def build_convert_plan(self):
        def halves(src2d):
            r = src2d.shape[0]
            h = r // 2
            return [src2d[0:h].rearrange("(o p) i -> p o i", p=128),
                    src2d[h:r].rearrange("(o p) i -> p o i", p=128)]
        loads = []
        loads.append(halves(self.d_wqkvT[:, 1024:1536]))
        loads.append(halves(self.d_wqkvT[:, 1536:2048]))
        loads.append(halves(self.d_wqkvT[:, 2048:2560]))
        loads.append(halves(self.d_wqkvT[:, 2560:3072]))
        loads.append(halves(self.d_fc1T[:, 0:512]))
        for e in range(8):
            loads.append([self.d_w1T[e].rearrange("(o p) i -> p o i", p=128),
                          self.d_w2T[e].rearrange("(o p) i -> p o i", p=128)])
        loads.append(halves(self.d_wqkvT[:, 0:512]))
        loads.append(halves(self.d_wqkvT[:, 512:1024]))
        for p in range(8):
            loads.append(halves(self.d_upT[:, p * 512:(p + 1) * 512]))
        loads.append(halves(self.d_fc2T))
        for p in range(8):
            loads.append(halves(self.d_downT[p * 512:(p + 1) * 512, :]))
        for f0 in (8, 12, 0, 4):
            loads.append(halves(self.d_wswT[:, f0 * 128:(f0 + 4) * 128]))
        loads.append(halves(self.d_woutT[:, 0:512]))
        loads.append(halves(self.d_woutT[:, 512:1024]))
        self.cv_half = [h for pair in loads for h in pair]
        self.cv_load_val = [0] * 76
        self.cv_store_val = [0] * 76
        self.cv_next = 0

    def convert_all(self):
        def g(g):
            for i in range(76):
                src = self.cv_half[i]
                isz = src.shape[2]
                dst = self.d_wbf[i].rearrange("p (o i) -> p o i", i=isz)
                self.sCV.inc(g.dma_start(out=dst, in_=src), 16)
                self.cv_store_val[i] = self.sCV.n
        self.G(g)

    def wload_cached(self, name, idx):
        if name in self.preloaded:
            return self.preloaded.pop(name)
        par = self.wparity; self.wparity = (self.wparity + 1) % 5
        def g(g):
            rel = self.wslotrel.pop(par, None)
            if rel is not None:
                tr, val = rel
                tr.wait_at(g, val)
            g.wait_ge(self.sCV.h, self.cv_total)
            self.sWLs[par].inc(g.dma_start(
                out=self.slot(par).rearrange("p (t c) -> p t c", c=2048),
                in_=self.d_wbf[2 * idx:2 * idx + 2].rearrange("t p c -> p t c")), 16)
        self.G(g)
        self.wready[name] = self.sWLs[par].n
        return par, self.wready[name]

    def slice_barrier(self, with_store=True):
        trs = (self.sPE, self.sDV, self.sAC, self.sST, self.sLD) if with_store else (
            self.sPE, self.sDV, self.sAC, self.sLD)
        snaps = [(tr, tr.n) for tr in trs]
        def mk(engf):
            def f(e):
                for tr, val in snaps:
                    tr.wait_at(e, val)
            engf(f)
        mk(self.T); mk(self.V); mk(self.A); mk(self.Y)

    def startup(self):
        sLD, sDV = self.sLD, self.sDV

        def y(sync):
            sLD.inc(sync.dma_start(out=self.IDENT[:], in_=self.d_ident), 16)
            sLD.inc(sync.dma_start(out=self.ONES1[:], in_=self.d_ones1), 16)
            sLD.inc(sync.dma_start(out=self.ONES128[:], in_=self.d_ones128), 16)
            sLD.inc(sync.dma_start(out=self.CONSTS[:], in_=self.d_consts), 16)
            sLD.inc(sync.dma_start(out=self.CBB[:], in_=self.d_cb), 16)
            sLD.inc(sync.dma_start(out=self.MASK[:], in_=self.d_mask), 16)
            sLD.inc(sync.dma_start(
                out=self.GT[:].rearrange("p (cb e) -> p cb e", e=8),
                in_=self.d_gateT.rearrange("(cb p) e -> p cb e", p=128)), 16)
        self.Y(y)

        def g(g):
            self.sWLs[0].inc(g.dma_start(
                out=self.WBuf[:, 0:16384].rearrange("p (cb e) -> p cb e", e=2048),
                in_=self.d_wqkvT[:, 1024:3072].rearrange("(cb p) e -> p cb e", p=128)), 16)
        self.G(g)
        self.wready["KV"] = self.sWLs[0].n

        def v(v):
            sLD.wait_all(v)
            v.memset(self.SLOT[:], 0.0)
            v.memset(self.OFFS[:], 0.0)
            i = v.tensor_copy(self.GH[:], self.GT[:])
            sDV.inc(i); v.wait_ge(sDV.h, sDV.n)
            i = v.tensor_tensor(self.GL[:], self.GT[:], self.GH[:], op=ALU.subtract)
            sDV.inc(i)
        self.V(v)

    def pre_x(self, sl):
        sDV = self.sDV

        def y(sync):
            sDV.wait_at(sync, self.pp_xn_done)
            self.sX.inc(sync.dma_start(
                out=self.AR_C[:].rearrange("p (b d) -> p b d", d=1024),
                in_=self.d_x[sl * TS:(sl + 1) * TS, :].rearrange("(b p) d -> p b d", p=128)), 16)
        self.Y(y)
        self.pp_x_ready = self.sX.n

    def pre_ln(self, sl):
        sLD, sPE, sDV, sAC = self.sLD, self.sPE, self.sDV, self.sAC
        x_ready = self.pp_x_ready

        def v(v):
            self.sX.wait_at(v, x_ready)
            for b in range(NB):
                i = v.tensor_reduce(self.ST1[:, b:b + 1], self.xb(b), axis=AX.X, op=ALU.add)
            sDV.inc(i)
        self.V(v)

        def a(sc):
            self.sX.wait_at(sc, x_ready)
            sDV.wait_at(sc, self.pp_var_done)
            dmp = self.B2[:].bitcast(F32)
            for b in range(NB):
                i = sc.activation(dmp[:, b * 1024:(b + 1) * 1024], self.xb(b), AF.Square,
                                  accum_out=self.ST1[:, 4 + b:5 + b])
            sAC.inc(i)
        self.A(a)

        def v2(v):
            sDV.wait_all(v)
            sAC.wait_all(v)
            i = v.tensor_scalar(self.ST1[:, 8:12], self.ST1[:, 0:4], 1.0 / 1024.0, None, op0=ALU.mult)
            sDV.inc(i); v.wait_ge(sDV.h, sDV.n)
            i = v.tensor_scalar(self.ST1[:, 20:24], self.ST1[:, 4:8], 1.0 / 1024.0, None, op0=ALU.mult)
            sDV.inc(i); v.wait_ge(sDV.h, sDV.n)
            i = v.tensor_tensor(self.ST1[:, 24:28], self.ST1[:, 8:12], self.ST1[:, 8:12], op=ALU.mult)
            sDV.inc(i); v.wait_ge(sDV.h, sDV.n)
            i = v.tensor_tensor(self.ST1[:, 4:8], self.ST1[:, 20:24], self.ST1[:, 24:28], op=ALU.subtract)
            sDV.inc(i)
        self.V(v2)
        self.pp_var_done = sDV.n

        def a2(sc):
            sDV.wait_all(sc)
            i = sc.activation(self.ST1[:, 20:24], self.ST1[:, 4:8], AF.Ln, bias=self.cst(0))
            sAC.inc(i); sc.wait_ge(sAC.h, sAC.n)
            i = sc.activation(self.ST1[:, 12:16], self.ST1[:, 20:24], AF.Exp, scale=self.cst(3))
            sAC.inc(i)
        self.A(a2)

        def v3(v):
            sAC.wait_all(v)
            i = v.tensor_tensor(self.ST1[:, 24:28], self.ST1[:, 4:8], self.ST1[:, 12:16], op=ALU.mult)
            sDV.inc(i); v.wait_ge(sDV.h, sDV.n)
            i = v.tensor_tensor(self.ST1[:, 24:28], self.ST1[:, 24:28], self.ST1[:, 12:16], op=ALU.mult)
            sDV.inc(i)
        self.V(v3)

        def a3(sc):
            sDV.wait_all(sc)
            sPE.wait_at(sc, self.pp_xpose_done)
            i = sc.activation(self.ST1[:, 20:24], self.ST1[:, 24:28], AF.Ln, bias=self.cst(1))
            sAC.inc(i); sc.wait_ge(sAC.h, sAC.n)
            i = sc.activation(self.ST1[:, 16:20], self.ST1[:, 20:24], AF.Exp, scale=self.cst(3))
            sAC.inc(i)
        self.A(a3)

        def v4(v):
            sAC.wait_all(v)
            i = v.tensor_scalar(self.SOUT[:, sl * 8: sl * 8 + 4], self.ST1[:, 12:16],
                                1.0, None, op0=ALU.add)
            sDV.inc(i); v.wait_ge(sDV.h, sDV.n)
            i = v.tensor_tensor(self.ST1[:, 28:32], self.ST1[:, 8:12], self.ST1[:, 12:16], op=ALU.mult)
            sDV.inc(i); v.wait_ge(sDV.h, sDV.n)
            i = v.tensor_scalar(self.SOUT[:, sl * 8 + 4: sl * 8 + 8], self.ST1[:, 28:32],
                                -1.0, None, op0=ALU.mult)
            sDV.inc(i)
        self.V(v4)

        def v4b(v):
            sPE.wait_at(v, self.pp_xpose_done)
            for b in range(NB):
                i = v.tensor_scalar(self.xnb(b), self.xb(b),
                                    self.ST1[:, 8 + b:9 + b], self.ST1[:, 12 + b:13 + b],
                                    op0=ALU.subtract, op1=ALU.mult)
            sDV.inc(i)
        self.V(v4b)
        xn_done = sDV.n
        self.pp_xn_done = xn_done

        def t(tt):
            self.bank_wait(tt, 6)
            sDV.wait_at(tt, xn_done)
            for b in range(NB):
                i = tt.transpose(self.pb[6][0:1, b * 128:(b + 1) * 128],
                                 self.ST1[:, 16 + b:17 + b], self.IDENT[:])
            sPE.inc(i)
        self.T(t)
        tr_done = sPE.n

        def a4(sc):
            sPE.wait_at(sc, tr_done)
            sDV.wait_at(sc, self.pp_srow_v2)
            i = sc.activation(self.ROWS[0:1, 0:512], self.pb[6][0:1, 0:512], AF.Copy)
            sAC.inc(i)
        self.A(a4)
        self.bank_free(6, sAC)

        def v5(v):
            sPE.wait_at(v, self.pp_rbc_pe)
            sAC.wait_all(v)
            i = v.tensor_copy(self.R_ALL[0:1, sl * 512:(sl + 1) * 512], self.ROWS[0:1, 0:512])
            sDV.inc(i); v.wait_ge(sDV.h, sDV.n)
            i = v.tensor_copy(self.ROWSB[0:1, 0:512], self.ROWS[0:1, 0:512])
            sDV.inc(i); v.wait_ge(sDV.h, sDV.n)
            i = v.tensor_tensor(self.ROWSB[0:1, 512:1024], self.ROWS[0:1, 0:512],
                                self.ROWSB[0:1, 0:512], op=ALU.subtract)
            sDV.inc(i)
        self.V(v5)

        self.pp_hilo = []
        for cb in range(CB):
            bk = 6 + (cb % 2)
            def t2(tt, cb=cb, bk=bk):
                self.bank_wait(tt, bk)
                sDV.wait_at(tt, xn_done)
                for b in range(NB):
                    i = tt.transpose(self.pb[bk][:, b * 128:(b + 1) * 128],
                                     self.AR_XN[:, b * 1024 + cb * 128: b * 1024 + (cb + 1) * 128],
                                     self.IDENT[:])
                sPE.inc(i)
            self.T(t2)
            pe_done = sPE.n

            def e2(v, sl=sl, cb=cb, bk=bk, pe_done=pe_done):
                sPE.wait_at(v, pe_done)
                sPE.wait_at(v, self.pp_gate_pe)
                i = v.tensor_copy(self.hi(sl, cb), self.pb[bk][:, 0:512])
                sDV.inc(i); v.wait_ge(sDV.h, sDV.n)
                i = v.tensor_tensor(self.lo(cb), self.pb[bk][:, 0:512], self.hi(sl, cb),
                                    op=ALU.subtract)
                sDV.inc(i)
            self.V(e2)
            self.bank_free(bk, sDV)
            self.pp_hilo.append(sDV.n)
        self.pp_xpose_done = self.sPE.n

    def pre_gate(self, sl):
        sPE, sDV, sAC, sLD = self.sPE, self.sDV, self.sAC, self.sLD
        hi_done = sDV.n

        def t(tt):
            self.bank_wait(tt, 4)
            n = 0
            prods = [(self.GH, lambda cb: self.hi(sl, cb)), (self.GH, self.lo),
                     (self.GL, lambda cb: self.hi(sl, cb)), (self.GL, self.lo)]
            total = len(prods) * CB
            for cb in range(CB):
                sDV.wait_at(tt, self.pp_hilo[cb])
                for gw, xf in prods:
                    i = tt.matmul(self.pb[4][0:8, 0:512],
                                  gw[:, cb * 8:(cb + 1) * 8], xf(cb),
                                  start=(n == 0), stop=(n == total - 1))
                    n += 1
            sPE.inc(i)
        self.T(t)
        g_done = sPE.n
        self.pp_gate_pe = g_done

        def a(sc):
            sPE.wait_at(sc, g_done)
            sPE.wait_at(sc, self.pp_gt_pe)
            i = sc.activation(self.G8[:, :], self.pb[4][0:8, 0:512], AF.Copy)
            sAC.inc(i)
        self.A(a)
        self.bank_free(4, sAC)
        g8_done = sAC.n

        def t2(tt):
            self.bank_wait(tt, 5)
            sAC.wait_at(tt, g8_done)
            for b in range(NB):
                i = tt.transpose(self.pb[5][:, b * 8:(b + 1) * 8],
                                 self.G8[:, b * 128:(b + 1) * 128],
                                 self.IDENT[0:8, 0:8])
            sPE.inc(i)
        self.T(t2)
        gt_done = sPE.n
        self.pp_gt_pe = gt_done

        def a2(sc):
            sPE.wait_at(sc, gt_done)
            sDV.wait_at(sc, self.pp_wte_done)
            sPE.wait_at(sc, self.pp_wt_pe)
            i = sc.activation(self.R_s[:, 0:32], self.pb[5][:, 0:32], AF.Exp, scale=self.cst(5))
            sAC.inc(i); sc.wait_ge(sAC.h, sAC.n)
            i = sc.activation(self.R_wte[:, 0:32], self.R_s[:, 0:32], AF.Ln, bias=self.cst(4))
            sAC.inc(i); sc.wait_ge(sAC.h, sAC.n)
            i = sc.activation(self.R_s[:, 0:32], self.R_wte[:, 0:32], AF.Exp, scale=self.cst(5))
            sAC.inc(i)
        self.A(a2)
        self.bank_free(5, sAC)

        def v(v):
            sAC.wait_all(v)
            B4 = range(NB)
            def step(fn):
                for b in B4:
                    i = fn(b)
                self.sDV.inc(i)
                v.wait_ge(self.sDV.h, self.sDV.n)
            s_ = lambda b: self.R_s[:, b * 8:(b + 1) * 8]
            t0 = lambda b: self.R_t[0][:, b * 8:(b + 1) * 8]
            t1 = lambda b: self.R_t[1][:, b * 8:(b + 1) * 8]
            t2_ = lambda b: self.R_t[2][:, b * 8:(b + 1) * 8]
            t3 = lambda b: self.R_t[3][:, b * 8:(b + 1) * 8]
            t4 = lambda b: self.R_t[4][:, b * 8:(b + 1) * 8]
            wte = lambda b: self.R_wte[:, b * 8:(b + 1) * 8]
            m = lambda b, i_: self.R_m[:, b * 8 + i_: b * 8 + i_ + 1]
            step(lambda b: v.tensor_tensor(t0(b), s_(b), self.CBB[:, 0:8], op=ALU.add))
            step(lambda b: v.tensor_reduce(m(b, 0), t0(b)[:, 0:4], axis=AX.X, op=ALU.max))
            step(lambda b: v.tensor_scalar(t1(b)[:, 0:4], t0(b)[:, 0:4], m(b, 0), None, op0=ALU.is_equal))
            step(lambda b: v.tensor_scalar(t1(b)[:, 0:4], t1(b)[:, 0:4], -1e30, None, op0=ALU.mult))
            step(lambda b: v.tensor_tensor(t2_(b)[:, 0:4], t0(b)[:, 0:4], t1(b)[:, 0:4], op=ALU.add))
            step(lambda b: v.tensor_reduce(m(b, 1), t2_(b)[:, 0:4], axis=AX.X, op=ALU.max))
            step(lambda b: v.tensor_tensor(m(b, 2), m(b, 0), m(b, 1), op=ALU.add))
            step(lambda b: v.tensor_reduce(m(b, 3), t0(b)[:, 4:8], axis=AX.X, op=ALU.max))
            step(lambda b: v.tensor_scalar(t1(b)[:, 4:8], t0(b)[:, 4:8], m(b, 3), None, op0=ALU.is_equal))
            step(lambda b: v.tensor_scalar(t1(b)[:, 4:8], t1(b)[:, 4:8], -1e30, None, op0=ALU.mult))
            step(lambda b: v.tensor_tensor(t2_(b)[:, 4:8], t0(b)[:, 4:8], t1(b)[:, 4:8], op=ALU.add))
            step(lambda b: v.tensor_reduce(m(b, 4), t2_(b)[:, 4:8], axis=AX.X, op=ALU.max))
            step(lambda b: v.tensor_tensor(m(b, 5), m(b, 3), m(b, 4), op=ALU.add))
            step(lambda b: v.tensor_tensor(m(b, 6), m(b, 2), m(b, 5), op=ALU.is_ge))
            step(lambda b: v.tensor_scalar(m(b, 7), m(b, 6), -1.0, 1.0, op0=ALU.mult, op1=ALU.add))
            step(lambda b: v.tensor_scalar(t3(b)[:, 0:4], t0(b)[:, 0:4], m(b, 6), None, op0=ALU.mult))
            step(lambda b: v.tensor_scalar(t3(b)[:, 4:8], t0(b)[:, 4:8], m(b, 7), None, op0=ALU.mult))
            step(lambda b: v.tensor_reduce(m(b, 0), t3(b), axis=AX.X, op=ALU.max))
            step(lambda b: v.tensor_scalar(t1(b), t3(b), m(b, 0), None, op0=ALU.is_equal))
            step(lambda b: v.tensor_scalar(t4(b), t1(b), -1e30, None, op0=ALU.mult))
            step(lambda b: v.tensor_tensor(t4(b), t3(b), t4(b), op=ALU.add))
            step(lambda b: v.tensor_reduce(m(b, 1), t4(b), axis=AX.X, op=ALU.max))
            step(lambda b: v.tensor_scalar(t2_(b), t4(b), m(b, 1), None, op0=ALU.is_equal))
            step(lambda b: v.tensor_tensor(t1(b), t1(b), t2_(b), op=ALU.add))
            step(lambda b: v.tensor_tensor(wte(b), s_(b), t1(b), op=ALU.mult))
            step(lambda b: v.tensor_reduce(m(b, 2), wte(b), axis=AX.X, op=ALU.add))
            step(lambda b: v.tensor_scalar(m(b, 2), m(b, 2), 1e-20, None, op0=ALU.add))
            step(lambda b: v.reciprocal(m(b, 3), m(b, 2)))
            step(lambda b: v.tensor_scalar(wte(b), wte(b), m(b, 3), None, op0=ALU.mult))
        self.V(v)
        wte_done = self.sDV.n
        self.pp_wte_done = wte_done

        def t3_(tt):
            self.bank_wait(tt, 6)
            sDV.wait_at(tt, wte_done)
            for b in range(NB):
                i = tt.transpose(self.pb[6][0:8, b * 128:(b + 1) * 128],
                                 self.R_wte[:, b * 8:(b + 1) * 8],
                                 self.IDENT[:])
            sPE.inc(i)
        self.T(t3_)
        wt_done = sPE.n
        self.pp_wt_pe = wt_done

        def a3(sc):
            sPE.wait_at(sc, wt_done)
            if sl > 0:
                self.sDW.wait_at(sc, self.wr_written[sl - 1])
            i = sc.activation(self.W8H[:, :], self.pb[6][0:8, 0:512], AF.Copy)
            sAC.inc(i)
        self.A(a3)
        self.bank_free(6, sAC)
        w8_done = sAC.n

        def y(sync):
            sAC.wait_at(sync, w8_done)
            self.sDW.inc(sync.dma_start(out=self.d_wr[0:8, sl * 512:(sl + 1) * 512], in_=self.W8H[:, :]), 16)
        self.Y(y)
        self.wr_written[sl] = self.sDW.n

    def pre_mid(self, sl):
        sPE, sDV, sAC = self.sPE, self.sDV, self.sAC

        for j in range(CB):
            bk = j % 4
            def t(tt, sl=sl, j=j, bk=bk):
                self.bank_wait(tt, bk)
                self.sWLs[0].wait_at(tt, self.wready["KV"])
                for cb in range(CB):
                    i = tt.matmul(self.pb[bk][:, 0:512],
                                  self.kv_blk(cb, j), self.hi(sl, cb),
                                  start=(cb == 0), stop=(cb == CB - 1))
                sPE.inc(i)
            self.T(t)
            peval = sPE.n
            def ev(sc, j=j, bk=bk, peval=peval):
                sPE.wait_at(sc, peval)
                i = sc.activation(self.k2(j), self.pb[bk][:, 0:512], AF.Square)
                sAC.inc(i)
            self.A(ev)
            def ev2(v, j=j, bk=bk, peval=peval):
                sPE.wait_at(v, peval)
                i = v.tensor_copy(self.A_(j), self.pb[bk][:, 0:512])
                sDV.inc(i)
            self.V(ev2)
            self.bank_free(bk, sAC, sDV)
        k2_done = sAC.n

        def t(tt):
            self.bank_wait(tt, 4)
            sAC.wait_at(tt, k2_done)
            for j in range(CB):
                i = tt.matmul(self.pb[4][0:1, 0:512], self.ONES128[:, :], self.k2(j),
                              start=(j == 0), stop=(j == CB - 1))
            sPE.inc(i)
        self.T(t)
        ss_done = sPE.n

        def a(sc):
            sPE.wait_at(sc, ss_done)
            sDV.wait_at(sc, self.pp_srow_v2)
            i = sc.activation(self.ROWS[0:1, 512:1024], self.pb[4][0:1, 0:512], AF.Copy)
            sAC.inc(i)
        self.A(a)
        self.bank_free(4, sAC)

        self._s_row_math(pe_guard=self.pp_skb)
        self.pp_srow_v2 = self.sDV.n

        def t2(tt):
            self.bank_wait(tt, 5)
            sDV.wait_at(tt, self.sDV.n)
            i = tt.matmul(self.pb[5][:, 0:512], self.ONES1[:, :],
                          self.ROWSB[0:1, 1024:1536], start=True, stop=False)
            i = tt.matmul(self.pb[5][:, 0:512], self.ONES1[:, :],
                          self.ROWSB[0:1, 1536:2048], start=False, stop=True)
            sPE.inc(i)
        self.T(t2)
        skb_done = sPE.n
        self.pp_skb = skb_done

        def v(v):
            sPE.wait_at(v, skb_done)
            for j in range(CB):
                i = v.tensor_tensor(self.A_(j), self.A_(j), self.pb[5][:, 0:512], op=ALU.mult)
            sDV.inc(i)
        self.V(v)
        ks_done = sDV.n
        self.bank_free(5, sDV)

        def a2(sc):
            sDV.wait_at(sc, ks_done)
            for j in range(CB):
                i = sc.activation(self.B_(j), self.A_(j), AF.Exp,
                                  accum_out=self.COLT[:, j:j + 1])
            sAC.inc(i)
        self.A(a2)
        w_done = sAC.n

        def t3(tt):
            self.bank_wait(tt, 5)
            i = tt.matmul(self.pb[5][:, 0:512], self.ONES1[:, :],
                          self.ROWSB[0:1, 0:512], start=True, stop=False)
            i = tt.matmul(self.pb[5][:, 0:512], self.ONES1[:, :],
                          self.ROWSB[0:1, 512:1024], start=False, stop=True)
            sPE.inc(i)
        self.T(t3)
        rb_done = sPE.n
        self.pp_rbc_pe = rb_done

        def a3(sc):
            sPE.wait_at(sc, rb_done)
            sDV.wait_at(sc, self.pp_vevac)
            i = sc.activation(self.SCR2[:, :], self.pb[5][:, 0:512], AF.Copy)
            sAC.inc(i)
        self.A(a3)
        self.bank_free(5, sAC)
        rbc_done = sAC.n

        for j in range(CB):
            bk = j % 4
            def t4(tt, sl=sl, j=j, bk=bk):
                self.bank_wait(tt, bk)
                for cb in range(CB):
                    i = tt.matmul(self.pb[bk][:, 0:512],
                                  self.kv_blk(cb, 8 + j), self.hi(sl, cb),
                                  start=(cb == 0), stop=(cb == CB - 1))
                sPE.inc(i)
            self.T(t4)
            peval = sPE.n
            def ev(v, j=j, bk=bk, peval=peval):
                sPE.wait_at(v, peval)
                sAC.wait_at(v, max(rbc_done, w_done))
                i = v.tensor_tensor(self.A_(j), self.pb[bk][:, 0:512], self.SCR2[:, :], op=ALU.mult)
                sDV.inc(i)
            self.V(ev)
            self.bank_free(bk, sDV)
        self.pp_vevac = sDV.n

        def v2(v):
            sAC.wait_at(v, w_done)
            v.wait_ge(sDV.h, sDV.n)
            dmp = self.B2[:].bitcast(F32)
            for j in range(CB):
                i = v.scalar_tensor_tensor(dmp[:, j * 512:(j + 1) * 512],
                                           self.A_(j), 1.0, self.B_(j),
                                           op0=ALU.mult, op1=ALU.mult,
                                           accum_out=self.COLT[:, 8 + j:9 + j])
            sDV.inc(i); v.wait_ge(sDV.h, sDV.n)
            i = v.tensor_tensor(self.SLOT[:, 0:16], self.SLOT[:, 0:16],
                                self.COLT[:, 0:16], op=ALU.add)
            sDV.inc(i)
        self.V(v2)

    def _s_row_math(self, s2_off=512, pe_guard=0):
        sDV, sAC, sPE = self.sDV, self.sAC, self.sPE
        R = self.ROWS
        RB = self.ROWSB
        scr1 = R[0:1, 1536:2048]
        scr2 = R[0:1, 1024:1536]

        def v(v):
            sAC.wait_all(v)
            sPE.wait_at(v, pe_guard)
            i = v.tensor_scalar(scr1, R[0:1, s2_off:s2_off + 512], 1.0 / 1024.0, None, op0=ALU.mult)
            sDV.inc(i); v.wait_ge(sDV.h, sDV.n)
            i = v.tensor_tensor(scr2, R[0:1, 0:512], R[0:1, 0:512], op=ALU.mult)
            sDV.inc(i); v.wait_ge(sDV.h, sDV.n)
            i = v.tensor_tensor(scr1, scr1, scr2, op=ALU.mult)
            sDV.inc(i)
        self.V(v)

        def a(sc):
            sDV.wait_all(sc)
            i = sc.activation(scr2, scr1, AF.Ln, bias=self.CONSTS[0:1, 1:2])
            sAC.inc(i); sc.wait_ge(sAC.h, sAC.n)
            i = sc.activation(scr1, scr2, AF.Exp, scale=self.CONSTS[0:1, 3:4])
            sAC.inc(i)
        self.A(a)

        def v2(v):
            sAC.wait_all(v)
            i = v.tensor_tensor(R[0:1, 1024:1536], R[0:1, 0:512], scr1, op=ALU.mult)
            sDV.inc(i); v.wait_ge(sDV.h, sDV.n)
            i = v.tensor_copy(RB[0:1, 1024:1536], R[0:1, 1024:1536])
            sDV.inc(i); v.wait_ge(sDV.h, sDV.n)
            i = v.tensor_tensor(RB[0:1, 1536:2048], R[0:1, 1024:1536],
                                RB[0:1, 1024:1536], op=ALU.subtract)
            sDV.inc(i)
        self.V(v2)

    def make_offsets(self):
        sDV, sLD, sCC = self.sDV, self.sLD, self.sCC
        tot_ready = sDV.n

        sCL, sPL = self.sCL, self.sPL

        def g(g):
            sDV.wait_at(g, tot_ready)
            sCC.inc(g.dma_start(out=self.d_tin, in_=self.SLOT[:, 0:16]), 16)
            g.wait_ge(sCC.h, sCC.n)
            sCL.inc(g.collective_compute(
                "AllGather",
                mybir.AluOpType.bypass,
                replica_groups=[list(range(NR))],
                ins=[self.d_tin.opt()],
                outs=[self.d_tout.opt()],
            ), 1)
            g.wait_ge(sCL.h, sCL.n)
            sCC.inc(g.dma_start(
                out=self.TOTS[:].rearrange("p (r c) -> p r c", c=16),
                in_=self.d_tout.rearrange("(r p) c -> p r c", p=128)), 16)
        self.G(g)
        gath_done = sCC.n

        def g2(g):
            g.wait_ge(sCC.h, gath_done)
            for r in range(NR):
                i = g.tensor_scalar(self.COLT[:, 0:16], self.TOTS[:, r * 16:(r + 1) * 16],
                                    self.MASK[:, r:r + 1], None, op0=ALU.mult)
                sPL.inc(i, 1)
                g.wait_ge(sPL.h, sPL.n)
                i = g.tensor_tensor(self.OFFS[:, :], self.OFFS[:, :], self.COLT[:, 0:16], op=ALU.add)
                sPL.inc(i, 1)
                g.wait_ge(sPL.h, sPL.n)
            i = g.tensor_copy(self.CARRY[:, :], self.OFFS[:, :])
            sPL.inc(i, 1)
        self.G(g2)
        self.offs_done = sPL.n

    def m_begin(self, sl):
        sLD, sPE, sDV, sAC = self.sLD, self.sPE, self.sDV, self.sAC
        self.wr16_guard = [0, 0]

        def v(v):
            i = v.tensor_copy(self.ROWS[0:1, 0:512], self.R_ALL[0:1, sl * 512:(sl + 1) * 512])
            sDV.inc(i); v.wait_ge(sDV.h, sDV.n)
            i = v.tensor_copy(self.ROWSB[0:1, 0:512], self.ROWS[0:1, 0:512])
            sDV.inc(i); v.wait_ge(sDV.h, sDV.n)
            i = v.tensor_tensor(self.ROWSB[0:1, 512:1024], self.ROWS[0:1, 0:512],
                                self.ROWSB[0:1, 0:512], op=ALU.subtract)
            sDV.inc(i)
        self.V(v)
        rb_ready = sDV.n

        def t(tt):
            self.bank_wait(tt, 5)
            sDV.wait_at(tt, rb_ready)
            i = tt.matmul(self.pb[5][:, 0:512], self.ONES1[:, :],
                          self.ROWSB[0:1, 0:512], start=True, stop=False)
            i = tt.matmul(self.pb[5][:, 0:512], self.ONES1[:, :],
                          self.ROWSB[0:1, 512:1024], start=False, stop=True)
            sPE.inc(i)
        self.T(t)
        rb_done = sPE.n

        def a(sc):
            sPE.wait_at(sc, rb_done)
            i = sc.activation(self.SCR2[:, :], self.pb[5][:, 0:512], AF.Copy)
            sAC.inc(i)
        self.A(a)
        self.bank_free(5, sAC)
        self.rbc_done = sAC.n

    def m_k(self, sl):
        sPE, sDV, sAC = self.sPE, self.sDV, self.sAC
        p0, w0 = self.wload_cached(f"k0_{sl}", 0)
        p1, w1 = self.wload_cached(f"k1_{sl}", 1)
        pars = [(p0, w0)] * 4 + [(p1, w1)] * 4
        for j in range(CB):
            par, wr = pars[j]
            bk = j % 4
            def t(tt, sl=sl, j=j, bk=bk, par=par, wr=wr):
                self.bank_wait(tt, bk)
                self.sWLs[par].wait_at(tt, wr)
                for cb in range(CB):
                    i = tt.matmul(self.pb[bk][:, 0:512],
                                  self.slot_blk(par, cb, j % 4), self.hi(sl, cb),
                                  start=(cb == 0), stop=(cb == CB - 1))
                sPE.inc(i)
            self.T(t)
            peval = sPE.n
            def ev(sc, j=j, bk=bk, peval=peval):
                sPE.wait_at(sc, peval)
                i = sc.activation(self.k2(j), self.pb[bk][:, 0:512], AF.Square)
                sAC.inc(i)
            self.A(ev)
            def ev2(v, j=j, bk=bk, peval=peval):
                sPE.wait_at(v, peval)
                i = v.tensor_copy(self.A_(j), self.pb[bk][:, 0:512])
                sDV.inc(i)
            self.V(ev2)
            self.bank_free(bk, sAC, sDV)
        self.wrelease(p0, sPE)
        self.wrelease(p1, sPE)
        self.k2_done = sAC.n

    def m_ksum(self):
        sPE, sAC = self.sPE, self.sAC
        k2_done = self.k2_done

        def t(tt):
            self.bank_wait(tt, 4)
            sAC.wait_at(tt, k2_done)
            for j in range(CB):
                i = tt.matmul(self.pb[4][0:1, 0:512], self.ONES128[:, :], self.k2(j),
                              start=(j == 0), stop=(j == CB - 1))
            sPE.inc(i)
        self.T(t)
        ss_done = sPE.n

        def a(sc):
            sPE.wait_at(sc, ss_done)
            i = sc.activation(self.ROWS[0:1, 512:1024], self.pb[4][0:1, 0:512], AF.Copy)
            sAC.inc(i)
        self.A(a)
        self.bank_free(4, sAC)

    def m_v(self, sl):
        sPE, sDV, sAC = self.sPE, self.sDV, self.sAC
        p0, w0 = self.wload_cached(f"v0_{sl}", 2)
        p1, w1 = self.wload_cached(f"v1_{sl}", 3)
        pars = [(p0, w0)] * 4 + [(p1, w1)] * 4
        rbc_done = self.rbc_done
        for j in range(CB):
            if j == 2:
                self.m_ksum()
            par, wr = pars[j]
            bk = j % 4
            def t(tt, sl=sl, j=j, bk=bk, par=par, wr=wr):
                self.bank_wait(tt, bk)
                self.sWLs[par].wait_at(tt, wr)
                for cb in range(CB):
                    i = tt.matmul(self.pb[bk][:, 0:512],
                                  self.slot_blk(par, cb, j % 4), self.hi(sl, cb),
                                  start=(cb == 0), stop=(cb == CB - 1))
                sPE.inc(i)
            self.T(t)
            peval = sPE.n
            def ev(v, j=j, bk=bk, peval=peval):
                sPE.wait_at(v, peval)
                sAC.wait_at(v, rbc_done)
                i = v.tensor_tensor(self.C_(j), self.pb[bk][:, 0:512], self.SCR2[:, :], op=ALU.mult)
                sDV.inc(i)
            self.V(ev)
            self.bank_free(bk, sDV)
        self.wrelease(p0, sPE)
        self.wrelease(p1, sPE)

    def m_fc1(self, sl):
        sPE, sDV = self.sPE, self.sDV
        par, wr = self.wload_cached(f"fc1_{sl}", 4)
        for lb in range(4):
            bk = 6 + (lb % 2)
            def t(tt, sl=sl, lb=lb, bk=bk, par=par, wr=wr):
                self.bank_wait(tt, bk)
                self.sWLs[par].wait_at(tt, wr)
                for cb in range(CB):
                    i = tt.matmul(self.pb[bk][:, 0:512],
                                  self.slot_blk(par, cb, lb), self.hi(sl, cb),
                                  start=(cb == 0), stop=(cb == CB - 1))
                sPE.inc(i)
            self.T(t)
            peval = sPE.n
            def ev(v, lb=lb, bk=bk, peval=peval):
                sPE.wait_at(v, peval)
                i = v.tensor_copy(self.lat(lb), self.pb[bk][:, 0:512])
                sDV.inc(i)
            self.V(ev)
            self.bank_free(bk, sDV)
        self.wrelease(par, sPE)
        self.lat_done = sDV.n

    def m_smath_k(self):
        sPE, sDV, sAC = self.sPE, self.sDV, self.sAC

        def t(tt):
            self.bank_wait(tt, 5)
            sDV.wait_at(tt, self.sDV.n)
            i = tt.matmul(self.pb[5][:, 0:512], self.ONES1[:, :],
                          self.ROWSB[0:1, 1024:1536], start=True, stop=False)
            i = tt.matmul(self.pb[5][:, 0:512], self.ONES1[:, :],
                          self.ROWSB[0:1, 1536:2048], start=False, stop=True)
            sPE.inc(i)
        self.T(t)
        skb_done = sPE.n
        self.skb_done = skb_done

        def v(v):
            sPE.wait_at(v, skb_done)
            for j in range(CB):
                i = v.tensor_tensor(self.A_(j), self.A_(j), self.pb[5][:, 0:512], op=ALU.mult)
            sDV.inc(i)
        self.V(v)
        ks_done = sDV.n
        self.bank_free(5, sDV)

        def a(sc):
            sDV.wait_at(sc, ks_done)
            for j in range(CB):
                i = sc.activation(self.B_(j), self.A_(j), AF.Exp)
            sAC.inc(i)
        self.A(a)
        w_done = sAC.n

        def v2(v):
            sAC.wait_at(v, w_done)
            for j in range(CB):
                i = v.tensor_tensor(self.C_(j), self.C_(j), self.B_(j), op=ALU.mult)
            sDV.inc(i)
        self.V(v2)
        kv_done = sDV.n

        self.kv_done = kv_done
        self.w_done_k = w_done

    def m_scans(self, part):
        sDV, sAC, sCC = self.sDV, self.sAC, self.sCC
        kv_done, w_done = self.kv_done, self.w_done_k

        def v2(v):
            if part == 0:
                sAC.wait_at(v, w_done)
                sDV.wait_at(v, kv_done)
                self.sPL.wait_at(v, self.offs_done)
            js = range((part % 2) * 4, (part % 2) * 4 + 4)
            if part < 2:
                for j in js:
                    i = v.tensor_tensor_scan(self.A_(j), self.B_(j), self.B_(j),
                                             self.CARRY[:, j:j + 1], op0=ALU.add, op1=ALU.bypass)
                    sDV.inc(i)
            else:
                for j in js:
                    i = v.tensor_tensor_scan(self.B_(j), self.C_(j), self.C_(j),
                                             self.CARRY[:, 8 + j:9 + j], op0=ALU.add, op1=ALU.bypass)
                    sDV.inc(i)
            if part == 3:
                v.wait_ge(sDV.h, sDV.n)
                for j in range(CB):
                    i = v.tensor_copy(self.CARRY[:, j:j + 1], self.A_(j)[:, 511:512])
                    i = v.tensor_copy(self.CARRY[:, 8 + j:9 + j], self.B_(j)[:, 511:512])
                sDV.inc(i)
        self.V(v2)
        if part == 3:
            self.scans_done = sDV.n

    def m_experts(self, sl, es, first):
        sPE, sDV, sAC, sLD = self.sPE, self.sDV, self.sAC, self.sLD
        lat_done = self.lat_done
        if first:
            self.ex_rot = 0
        for e in es:
            while self.ex_pend and self.ex_pend[0][1] == 3 and self.ex_pend[0][0] <= e - 2:
                self._emit_w2(*self.ex_pend.pop(0))
            par, wr = self.wload_cached(f"ex_{e}_{sl}", 5 + e)

            half = e % 2
            guard = self.wr16_guard[half]
            def y(sync, sl=sl, e=e, half=half, guard=guard):
                self.sDW.wait_at(sync, self.wr_written_all)
                sPE.wait_at(sync, guard)
                self.sR[half].inc(sync.dma_start(out=self.WR16[0:1, half * 512:(half + 1) * 512],
                                                 in_=self.d_wr[e:e + 1, sl * 512:(sl + 1) * 512]), 16)
            self.Y(y)
            row_ready = self.sR[half].n

            wbk = (4, 5, 6, 7)[self.ex_rot % 4]; self.ex_rot += 1
            def tb(tt, e=e, wbk=wbk, half=half, row_ready=row_ready):
                self.bank_wait(tt, wbk)
                self.sR[half].wait_at(tt, row_ready)
                i = tt.matmul(self.pb[wbk][:, 0:512], self.ONES1[:, :],
                              self.WR16[0:1, half * 512:(half + 1) * 512], start=True, stop=True)
                sPE.inc(i)
            self.T(tb)
            wb_peval = sPE.n
            self.wr16_guard[half] = wb_peval
            def ab(v, e=e, wbk=wbk, wb_peval=wb_peval):
                sPE.wait_at(v, wb_peval)
                i = v.tensor_copy(self.wbc2(e), self.pb[wbk][:, 0:512])
                sDV.inc(i)
            self.V(ab)
            self.bank_free(wbk, sDV)
            wbc_ready = sDV.n

            for ib in range(4):
                bk = (4, 5, 6, 7)[self.ex_rot % 4]; self.ex_rot += 1
                def t(tt, e=e, ib=ib, bk=bk, par=par, wr=wr):
                    self.bank_wait(tt, bk)
                    self.sWLs[par].wait_at(tt, wr)
                    sDV.wait_at(tt, lat_done)
                    for lb in range(4):
                        i = tt.matmul(self.pb[bk][:, 0:512],
                                      self.slot(par)[:, lb * 512 + ib * 128: lb * 512 + (ib + 1) * 128],
                                      self.lat(lb),
                                      start=(lb == 0), stop=(lb == 3))
                    sPE.inc(i)
                self.T(t)
                peval = sPE.n
                guard = max(wbc_ready, self.ex_g2_prev)
                def ev(v, e=e, ib=ib, bk=bk, peval=peval, guard=guard):
                    sPE.wait_at(v, peval)
                    sDV.wait_at(v, guard)
                    i = v.tensor_tensor(self.SCRH[:, :], self.pb[bk][:, 0:512],
                                        self.wbc2(e), op=ALU.mult)
                    sDV.inc(i); v.wait_ge(sDV.h, sDV.n)
                    i = v.scalar_tensor_tensor(self.g2(ib), self.pb[bk][:, 0:512], 0.0,
                                               self.SCRH[:, :], op0=ALU.max, op1=ALU.mult)
                    sDV.inc(i)
                self.V(ev)
                self.ex_g2_prev = sDV.n
                self.bank_free(bk, sDV)
                if DEFER_W2:
                    self.ex_pend.append((e, ib, par, sDV.n, first and e == es[0] and ib == 0))
                    if len(self.ex_pend) > 2:
                        self._emit_w2(*self.ex_pend.pop(0))
                else:
                    self._emit_w2(e, ib, par, sDV.n, first and e == es[0] and ib == 0)

    def _emit_w2(self, e, ib, par, g2_done, first_w2):
        sPE, sDV = self.sPE, self.sDV
        def t2(tt, e=e, ib=ib, par=par, g2_done=g2_done, first_w2=first_w2):
            if first_w2:
                for lb in range(4):
                    self.bank_wait(tt, lb)
            sDV.wait_at(tt, g2_done)
            for lb in range(4):
                i = tt.matmul(self.pb[lb][:, 0:512],
                              self.slot(par)[:, 2048 + ib * 512 + lb * 128: 2048 + ib * 512 + (lb + 1) * 128],
                              self.g2(ib),
                              start=(e == 0 and ib == 0), stop=(e == 7 and ib == 3))
            sPE.inc(i)
        self.T(t2)
        if ib == 3:
            self.wrelease(par, sPE)

    def m_comb_evac(self):
        sPE, sDV = self.sPE, self.sDV
        while self.ex_pend:
            self._emit_w2(*self.ex_pend.pop(0))
        comb_done = sPE.n

        def v(v):
            sPE.wait_at(v, comb_done)
            for lb in range(4):
                i = v.tensor_copy(self.combv(lb), self.pb[lb][:, 0:512])
                sDV.inc(i)
        self.V(v)
        for lb in range(4):
            self.bank_free(lb, sDV)

    def m_q_jobs(self, sl):
        sPE, sDV, sAC = self.sPE, self.sDV, self.sAC
        p0, w0 = self.wload_cached(f"q0_{sl}", 13)
        p1, w1 = self.wload_cached(f"q1_{sl}", 14)
        pars = [(p0, w0)] * 4 + [(p1, w1)] * 4
        for j in range(CB):
            par, wr = pars[j]
            bk = j % 4
            def t(tt, sl=sl, j=j, bk=bk, par=par, wr=wr):
                self.bank_wait(tt, bk)
                self.sWLs[par].wait_at(tt, wr)
                for cb in range(CB):
                    i = tt.matmul(self.pb[bk][:, 0:512],
                                  self.slot_blk(par, cb, j % 4), self.hi(sl, cb),
                                  start=(cb == 0), stop=(cb == CB - 1))
                sPE.inc(i)
            self.T(t)
            peval = sPE.n
            def ev(sc, j=j, bk=bk, peval=peval):
                sPE.wait_at(sc, peval)
                if j == 0:
                    self.sST.wait_at(sc, self.prev_store)
                i = sc.activation(self.k2(j), self.pb[bk][:, 0:512], AF.Square)
                sAC.inc(i)
                i = sc.activation(self.XN_(j), self.pb[bk][:, 0:512], AF.Copy)
                sAC.inc(i)
            self.A(ev)
            self.bank_free(bk, sAC)
            if j == 3:
                q2_half = sAC.n
        self.wrelease(p0, sPE)
        self.wrelease(p1, sPE)
        q2_done = sAC.n
        self.q_evac_ac = q2_done

        def t(tt):
            self.bank_wait(tt, 4)
            sAC.wait_at(tt, q2_done)
            for j in range(CB):
                i = tt.matmul(self.pb[4][0:1, 0:512], self.ONES128[:, :], self.k2(j),
                              start=(j == 0), stop=(j == CB - 1))
            sPE.inc(i)
        self.T(t)
        ss_done = sPE.n

        def a(sc):
            sPE.wait_at(sc, ss_done)
            i = sc.activation(self.ROWS[0:1, 512:1024], self.pb[4][0:1, 0:512], AF.Copy)
            sAC.inc(i)
        self.A(a)
        self.bank_free(4, sAC)

    def m_q_tail(self, sl):
        sPE, sDV, sAC, sLD = self.sPE, self.sDV, self.sAC, self.sLD
        self._s_row_math(s2_off=512, pe_guard=self.skb_done)

        def t2(tt):
            self.bank_wait(tt, 5)
            sDV.wait_at(tt, self.sDV.n)
            i = tt.matmul(self.pb[5][:, 0:512], self.ONES1[:, :],
                          self.ROWSB[0:1, 1024:1536], start=True, stop=False)
            i = tt.matmul(self.pb[5][:, 0:512], self.ONES1[:, :],
                          self.ROWSB[0:1, 1536:2048], start=False, stop=True)
            sPE.inc(i)
        self.T(t2)
        sqb_done = sPE.n

        def v(v):
            sPE.wait_at(v, sqb_done)
            sAC.wait_at(v, self.q_evac_ac)
            for j in range(CB):
                i = v.tensor_tensor(self.XN_(j), self.XN_(j), self.pb[5][:, 0:512], op=ALU.mult)
            sDV.inc(i)
        self.V(v)
        self.bank_free(5, sDV)
        self.qs_done = sDV.n

    def m_q_sig(self, sl):
        sDV, sAC = self.sDV, self.sAC
        qs_done = self.qs_done

        def a2(sc):
            sDV.wait_at(sc, qs_done)
            sDV.wait_at(sc, self.scans_done)
            for j in range(CB):
                i = sc.activation(self.C_(j), self.XN_(j), AF.Exp, scale=self.cst(5))
            sAC.inc(i); sc.wait_ge(sAC.h, sAC.n)
            for j in range(CB):
                i = sc.activation(self.XN_(j), self.C_(j), AF.Ln, bias=self.cst(4))
            sAC.inc(i); sc.wait_ge(sAC.h, sAC.n)
            for j in range(CB):
                i = sc.activation(self.C_(j), self.XN_(j), AF.Exp, scale=self.cst(5))
            sAC.inc(i); sc.wait_ge(sAC.h, sAC.n)
            for j in range(CB):
                i = sc.activation(self.XN_(j), self.A_(j), AF.Ln, bias=self.cst(2))
            sAC.inc(i); sc.wait_ge(sAC.h, sAC.n)
            for j in range(CB):
                i = sc.activation(self.A_(j), self.XN_(j), AF.Exp, scale=self.cst(5))
            sAC.inc(i)
        self.A(a2)
        self.rw_done = sAC.n

    def m_y_tail(self, sl):
        sDV, sAC, sLD = self.sDV, self.sAC, self.sLD

        def v2(v):
            sAC.wait_at(v, self.rw_done)
            for j in range(CB):
                i = v.tensor_tensor(self.B_(j), self.B_(j), self.A_(j), op=ALU.mult)
            sDV.inc(i); v.wait_ge(sDV.h, sDV.n)
            for j in range(CB):
                i = v.tensor_tensor(self.ys(j), self.B_(j), self.C_(j), op=ALU.mult)
            sDV.inc(i)
        self.V(v2)
        self.ysig_done = sDV.n

        def y(sync):
            sDV.wait_at(sync, self.ysig_done)
            self.sX.inc(sync.dma_start(
                out=self.AR_C[:].rearrange("p (b d) -> p b d", d=1024),
                in_=self.d_x[sl * TS:(sl + 1) * TS, :].rearrange("(b p) d -> p b d", p=128)), 16)
        self.Y(y)
        self.x_ready = self.sX.n

    def m_up(self, sl):
        sPE, sDV, sAC = self.sPE, self.sDV, self.sAC
        ysig_done = self.ysig_done
        for p in range(8):
            par, wr = self.wload_cached(f"up_{p}_{sl}", 15 + p)
            for sbw in range(4):
                sb = p * 4 + sbw
                bk = (6, 7, 0, 1)[sbw]
                def t(tt, sl=sl, sbw=sbw, bk=bk, par=par, wr=wr):
                    self.bank_wait(tt, bk)
                    self.sWLs[par].wait_at(tt, wr)
                    for cb in range(CB):
                        i = tt.matmul(self.pb[bk][:, 0:512],
                                      self.slot_blk(par, cb, sbw), self.hi(sl, cb),
                                      start=(cb == 0), stop=(cb == CB - 1))
                    sPE.inc(i)
                self.T(t)
                peval = sPE.n
                scr = (self.SCRH, self.SCR2, self.SCR3, self.SCR4)[sb % 4]
                def ev_a(sc, bk=bk, peval=peval, scr=scr):
                    sPE.wait_at(sc, peval)
                    i = sc.activation(scr[:, :], self.pb[bk][:, 0:512], AF.Relu)
                    sAC.inc(i)
                self.A(ev_a)
                acval = sAC.n
                def ev(v, sb=sb, bk=bk, acval=acval, scr=scr):
                    sAC.wait_at(v, acval)
                    sDV.wait_at(v, ysig_done)
                    i = v.tensor_tensor(self.hsq(sb), self.pb[bk][:, 0:512], scr[:, :], op=ALU.mult)
                    sDV.inc(i)
                self.V(ev)
                self.bank_free(bk, sDV, sAC)
            self.wrelease(par, sPE)

    def m_fc2_down(self, sl):
        sPE, sDV, sLD = self.sPE, self.sDV, self.sLD
        parC, wr2 = self.wload_cached(f"fc2_{sl}", 23)
        comb_evac = sDV.n

        def bw(tt):
            for i_ in range(8):
                self.bank_wait(tt, i_)
        self.T(bw)

        def t2(tt):
            self.sWLs[parC].wait_at(tt, wr2)
            sDV.wait_at(tt, comb_evac)
            for tb in range(NB):
                for lb in range(4):
                    lhsT = self.B2[:, lb * 512 + tb * 128: lb * 512 + (tb + 1) * 128]
                    i = tt.matmul(self.pb[2 * tb][:, 0:512],
                                  lhsT, self.slot(parC)[:, lb * 1024: lb * 1024 + 512],
                                  start=(lb == 0), stop=False)
                    i = tt.matmul(self.pb[2 * tb + 1][:, 0:512],
                                  lhsT, self.slot(parC)[:, lb * 1024 + 512: (lb + 1) * 1024],
                                  start=(lb == 0), stop=False)
            sPE.inc(i)
        self.T(t2)
        self.wrelease(parC, sPE)

        hsq_done = sDV.n
        for p in range(8):
            par, wr3 = self.wload_cached(f"dn_{p}_{sl}", 24 + p)

            def t3(tt, p=p, par=par, wr3=wr3):
                self.sWLs[par].wait_at(tt, wr3)
                sDV.wait_at(tt, hsq_done)
                last_piece = (p == 7)
                for tb in range(NB):
                    for sbw in range(4):
                        sb = p * 4 + sbw
                        ar = self.AR_A if sb < 16 else self.AR_B
                        jj = sb % 16
                        lhsT = ar[:].bitcast(BF16)[:, jj * 512 + tb * 128: jj * 512 + (tb + 1) * 128]
                        stop = last_piece and (sbw == 3)
                        i = tt.matmul(self.pb[2 * tb][:, 0:512],
                                      lhsT, self.slot(par)[:, sbw * 1024: sbw * 1024 + 512],
                                      start=False, stop=stop)
                        i = tt.matmul(self.pb[2 * tb + 1][:, 0:512],
                                      lhsT, self.slot(par)[:, sbw * 1024 + 512: (sbw + 1) * 1024],
                                      start=False, stop=stop)
                sPE.inc(i)
            self.T(t3)
            self.wrelease(par, sPE)
        fin_done = sPE.n

        def v(v):
            sPE.wait_at(v, fin_done)
            self.sX.wait_at(v, self.x_ready)
            self.sST.wait_at(v, self.prev_store)
            for tb in range(NB):
                i = v.tensor_scalar(self.xrb(tb), self.xrb(tb),
                                    self.SOUT[:, sl * 8 + tb: sl * 8 + tb + 1],
                                    self.SOUT[:, sl * 8 + 4 + tb: sl * 8 + 5 + tb],
                                    op0=ALU.mult, op1=ALU.add)
                sDV.inc(i); v.wait_ge(sDV.h, sDV.n)
                i = v.tensor_tensor(self.outb(tb)[:, 0:512], self.pb[2 * tb][:, 0:512],
                                    self.xrb(tb)[:, 0:512], op=ALU.add)
                sDV.inc(i)
                i = v.tensor_tensor(self.outb(tb)[:, 512:1024], self.pb[2 * tb + 1][:, 0:512],
                                    self.xrb(tb)[:, 512:1024], op=ALU.add)
                sDV.inc(i)
        self.V(v)
        for i_ in range(8):
            self.bank_free(i_, sDV)

    def m_swiglu(self, sl):
        sPE, sDV, sAC = self.sPE, self.sDV, self.sAC
        ysig_done = self.ysig_done
        self.sw_scr_guard = [0, 0, 0, 0]
        self.sw_sg_ac = [0] * 8
        quarters = [(8, "swg1"), (12, "swg2"), (0, "swu1"), (4, "swu2")]
        for qi, (f0, name) in enumerate(quarters):
            par, wr = self.wload_cached(name + str(sl), 32 + qi)
            for fj in range(4):
                fb = f0 + fj
                bk = (0, 1, 4, 5)[fb % 4]
                scrX = (self.SCRH, self.SCR2, self.SCR3, self.SCR4)[fb % 4]
                is_g = fb >= 8
                def t(tt, fj=fj, bk=bk, par=par, wr=wr):
                    self.bank_wait(tt, bk)
                    self.sWLs[par].wait_at(tt, wr)
                    sDV.wait_at(tt, ysig_done)
                    for cb in range(CB):
                        i = tt.matmul(self.pb[bk][:, 0:512],
                                      self.slot(par)[:, cb * 512 + fj * 128: cb * 512 + (fj + 1) * 128],
                                      self.ys(cb),
                                      start=(cb == 0), stop=(cb == CB - 1))
                    sPE.inc(i)
                self.T(t)
                peval = self.sPE.n
                if is_g:
                    scr_guard = self.sw_scr_guard[fb % 4]
                    def ev(sc, fb=fb, bk=bk, peval=peval, scrX=scrX, scr_guard=scr_guard):
                        sPE.wait_at(sc, peval)
                        sDV.wait_at(sc, scr_guard)
                        i = sc.activation(scrX[:, :], self.pb[bk][:, 0:512], AF.Exp, scale=self.cst(5))
                        sAC.inc(i); sc.wait_ge(sAC.h, sAC.n)
                        i = sc.activation(self.sg(fb - 8), scrX[:, :], AF.Ln, bias=self.cst(4))
                        sAC.inc(i); sc.wait_ge(sAC.h, sAC.n)
                        i = sc.activation(scrX[:, :], self.sg(fb - 8), AF.Exp, scale=self.cst(5))
                        sAC.inc(i)
                    self.A(ev)
                    acv = sAC.n
                    self.sw_sg_ac[fb - 8] = acv
                    def ev_m(v, fb=fb, bk=bk, acv=acv, scrX=scrX):
                        sAC.wait_at(v, acv)
                        i = v.tensor_tensor(self.sg(fb - 8), self.pb[bk][:, 0:512], scrX[:, :], op=ALU.mult)
                        sDV.inc(i)
                    self.V(ev_m)
                    self.bank_free(bk, sAC, sDV)
                    self.sw_scr_guard[fb % 4] = sDV.n
                else:
                    sg_ac = self.sw_sg_ac[fb]
                    def ev(v, fb=fb, bk=bk, peval=peval, sg_ac=sg_ac):
                        sPE.wait_at(v, peval)
                        sAC.wait_at(v, sg_ac)
                        i = v.tensor_tensor(self.ug(fb), self.pb[bk][:, 0:512],
                                            self.sg(fb), op=ALU.mult)
                        sDV.inc(i)
                    self.V(ev)
                    self.bank_free(bk, sDV)
            self.wrelease(par, sPE)

    def m_proj(self, sl):
        sPE, sDV = self.sPE, self.sDV
        ug_done = sDV.n
        parA, wrA = self.wload_cached(f"wo0_{sl}", 36)
        parB, wr = self.wload_cached(f"wo1_{sl}", 37)

        for tb in range(NB):
            b0 = 2 + 4 * (tb % 2)
            def t(tt, tb=tb, b0=b0):
                self.bank_wait(tt, b0)
                self.bank_wait(tt, b0 + 1)
                self.sWLs[parA].wait_at(tt, wrA)
                self.sWLs[parB].wait_at(tt, wr)
                sDV.wait_at(tt, ug_done)
                for dcb in range(CB):
                    lhsT = self.BSC[:, 4096 + dcb * 512 + tb * 128: 4096 + dcb * 512 + (tb + 1) * 128]
                    i = tt.matmul(self.pb[b0][:, 0:512],
                                  lhsT, self.slot(parA)[:, dcb * 512:(dcb + 1) * 512],
                                  start=(dcb == 0), stop=(dcb == CB - 1))
                    i = tt.matmul(self.pb[b0 + 1][:, 0:512],
                                  lhsT, self.slot(parB)[:, dcb * 512:(dcb + 1) * 512],
                                  start=(dcb == 0), stop=(dcb == CB - 1))
                sPE.inc(i)
            self.T(t)
            peval = sPE.n
            def ev(v, tb=tb, b0=b0, peval=peval):
                sPE.wait_at(v, peval)
                i = v.tensor_tensor(self.outb(tb)[:, 0:512], self.outb(tb)[:, 0:512],
                                    self.pb[b0][:, 0:512], op=ALU.add)
                sDV.inc(i)
                i = v.tensor_tensor(self.outb(tb)[:, 512:1024], self.outb(tb)[:, 512:1024],
                                    self.pb[b0 + 1][:, 0:512], op=ALU.add)
                sDV.inc(i)
            self.V(ev)
            self.bank_free(b0, sDV)
            self.bank_free(b0 + 1, sDV)
        self.wrelease(parA, sPE)
        self.wrelease(parB, sPE)
        out_done = sDV.n

        def y(sync):
            sDV.wait_at(sync, out_done)
            self.sST.inc(sync.dma_start(
                out=self.d_y[sl * TS:(sl + 1) * TS, :].rearrange("(b p) d -> p b d", p=128),
                in_=self.AR_XN[:].rearrange("p (b d) -> p b d", d=1024)), 16)
        self.Y(y)

    def program(self):
        self.startup()
        self.build_convert_plan()
        self.convert_all()
        self.cv_total = self.sCV.n
        self.pp_xn_done = 0
        self.pp_var_done = 0
        self.pp_xpose_done = 0
        self.pp_vevac = 0
        self.pp_wte_done = 0
        self.pp_skb = 0
        self.pp_srow_v2 = 0
        self.pp_rbc_pe = 0
        self.pp_gate_pe = 0
        self.pp_gt_pe = 0
        self.pp_wt_pe = 0
        for sl in range(NSL):
            self.pre_x(sl)
            self.pre_ln(sl)
            self.pre_gate(sl)
            self.pre_mid(sl)
        self.wr_written_all = self.sDW.n
        for s_ in range(4):
            self.wrelease(s_, self.sPE)
        for name, idx in (("k0_0", 0), ("k1_0", 1), ("v0_0", 2), ("v1_0", 3), ("fc1_0", 4)):
            self.preloaded[name] = self.wload_cached(name, idx)
        self.make_offsets()
        for sl in range(NSL):
            self.prev_store = self.sST.n
            self.slice_barrier(with_store=False)
            self.m_begin(sl)
            self.m_k(sl)
            self.m_v(sl)
            self.m_fc1(sl)
            self._s_row_math()
            self.m_q_jobs(sl)
            self.m_smath_k()
            self.m_experts(sl, [0], first=True)
            self.m_q_tail(sl)
            self.m_experts(sl, [1], first=False)
            self.m_scans(0)
            self.m_experts(sl, [2], first=False)
            self.m_scans(1)
            self.m_experts(sl, [3], first=False)
            self.m_scans(2)
            self.m_experts(sl, [4], first=False)
            self.m_scans(3)
            self.m_q_sig(sl)
            self.m_experts(sl, [5, 6, 7], first=False)
            self.m_comb_evac()
            self.m_y_tail(sl)
            self.m_up(sl)
            self.m_fc2_down(sl)
            self.m_swiglu(sl)
            self.m_proj(sl)

        def y(sync):
            sync.wait_ge(self.sST.h, self.sST.n)
        self.Y(y)


def build_nc():
    nc = bass.Bass("TRN2", target_bir_lowering=False)
    with contextlib.ExitStack() as stack:
        b = Builder(nc, None, stack)
        b.block = stack.enter_context(nc.Block())
        b.program()
    return nc


def kernel(**inputs):
    x = np.asarray(inputs["L0_attn_x"], np.float32)
    wqkvT = np.ascontiguousarray(np.asarray(inputs["L0_attn_w_qkv"], np.float32).T)
    wswT = np.ascontiguousarray(np.asarray(inputs["L0_attn_w_swiglu"], np.float32).T)
    woutT = np.ascontiguousarray(np.asarray(inputs["L0_attn_w_out"], np.float32).T)
    gateT = np.ascontiguousarray(np.asarray(inputs["L0_ffn_gate"], np.float32).T)
    cb = np.asarray(inputs["L0_ffn_corr_bias"], np.float32)
    fc1T = np.ascontiguousarray(np.asarray(inputs["L0_ffn_fc1_w"], np.float32).T)
    fc2T = np.ascontiguousarray(np.asarray(inputs["L0_ffn_fc2_w"], np.float32).T)
    w1T = np.ascontiguousarray(np.transpose(np.asarray(inputs["L0_ffn_w1"], np.float32), (0, 2, 1)))
    w2T = np.ascontiguousarray(np.transpose(np.asarray(inputs["L0_ffn_w2"], np.float32), (0, 2, 1)))
    upT = np.ascontiguousarray(np.asarray(inputs["L0_ffn_up_w"], np.float32).T)
    downT = np.ascontiguousarray(np.asarray(inputs["L0_ffn_down_w"], np.float32).T)

    ident = np.eye(128, dtype=np.float32)
    ones1 = np.ones((1, 128), BF)
    ones128 = np.ones((128, 1), BF)
    consts = np.zeros((128, 8), np.float32)
    consts[:, 0] = EPS_LN
    consts[:, 1] = EPS_F
    consts[:, 2] = EPS_Y
    consts[:, 3] = -0.5
    consts[:, 4] = 1.0
    consts[:, 5] = -1.0
    consts[:, 7] = 1.0 / 1024.0
    cbb = np.tile(cb[None, :], (128, 1)).astype(np.float32)

    in_maps = []
    for cid in range(8):
        bidx, q = cid // 4, cid % 4
        x_own = np.ascontiguousarray(x[bidx, q * TOK:(q + 1) * TOK, :])
        mask = np.zeros((128, NR), np.float32)
        mask[:, bidx * 4: bidx * 4 + q] = 1.0
        in_maps.append({
            "x_own": x_own, "mask": mask,
            "wqkvT": wqkvT, "wswT": wswT, "woutT": woutT,
            "gateT": gateT, "cbb": cbb, "fc1T": fc1T, "fc2T": fc2T,
            "w1T": w1T, "w2T": w2T, "upT": upT, "downT": downT,
            "ident": ident, "ones1": ones1, "ones128": ones128,
            "consts": consts,
        })

    nc = build_nc()
    res = run_bass_kernel_spmd(nc, in_maps, core_ids=list(range(8)))
    outs = [res.results[c]["y"] for c in range(8)]
    full = np.stack([np.concatenate(outs[0:4], axis=0),
                     np.concatenate(outs[4:8], axis=0)], axis=0)
    return full.astype(np.float32)
